# version 24
# baseline (speedup 1.0000x reference)
import numpy as np
from contextlib import ExitStack

import concourse.bass as bass
import concourse.mybir as mybir
from concourse.bass_utils import run_bass_kernel_spmd

import concourse.bass_utils as _bu

if not getattr(_bu, "_ldw_patch", False):
    _orig_run_command = _bu.run_command

    def _run_command_ldw(argv, **kwargs):
        argv = ["--enable-ldw-opt=true" if a == "--enable-ldw-opt=false" else a
                for a in argv]
        return _orig_run_command(argv, **kwargs)

    _bu.run_command = _run_command_ldw
    _bu._ldw_patch = True


F32 = mybir.dt.float32
BF16 = mybir.dt.bfloat16
AF = mybir.ActivationFunctionType
ALU = mybir.AluOpType
AX = mybir.AxisListType

B = 2
T = 2048
NT = B * T
D = 1024
H = 4096
DCH = D // 128
HCH = H // 128
NCHUNK = NT // 128
TILE = 512
NTILE = NT // TILE
SUB = TILE // 128


class Sem:
    def __init__(self, nc, name):
        self.h = nc.semaphore(name)
        self.n = 0

    def __enter__(self):
        self.h = self.h.__enter__()
        return self

    def __exit__(self, *a):
        return False


def build_program():
    nc = bass.Bass("TRN2", target_bir_lowering=False)

    x_d = nc.dram_tensor("x", [B, T, D], F32, kind="ExternalInput")
    out_d = nc.dram_tensor("out", [B, T, D], F32, kind="ExternalOutput")
    wnames = ["Wg", "Wi", "Wfg", "Wfu", "Wfd", "Wo"]
    w_d = {n: nc.dram_tensor(n, [D, D], F32, kind="ExternalInput") for n in wnames}
    vnames = ["bg", "bi", "A", "Bp", "Cp", "Dp", "gamma", "beta",
              "bfg", "bfu", "bfd", "bo"]
    v_d = {n: nc.dram_tensor(n, [D], F32, kind="ExternalInput") for n in vnames}
    wgT_d = nc.dram_tensor("wgT", [D, H], F32, kind="ExternalInput")
    wuT_d = nc.dram_tensor("wuT", [D, H], F32, kind="ExternalInput")
    wdT_d = nc.dram_tensor("wdT", [H, D], F32, kind="ExternalInput")

    xn_sp = nc.dram_tensor("xn_sp", [NT, D], BF16)
    mean_sp = nc.dram_tensor("mean_sp", [3, 128], F32)
    yn_sp = nc.dram_tensor("yn_sp", [D, NT], BF16)
    x2_sp = nc.dram_tensor("x2_sp", [NT, D], F32)
    xq_sp = nc.dram_tensor("xq_sp", [NT, D], BF16)
    hq_sp = nc.dram_tensor("hq_sp", [NT, H], BF16)
    wq_sp = {"g": nc.dram_tensor("wqg_sp", [D, H], BF16),
             "u": nc.dram_tensor("wqu_sp", [D, H], BF16),
             "d": nc.dram_tensor("wqd_sp", [H, D], BF16)}

    es = ExitStack()
    Sb = lambda name, shape, dt=F32: es.enter_context(nc.sbuf_tensor(name, shape, dt))
    Ps = lambda name, shape: es.enter_context(nc.psum_tensor(name, shape, F32))
    SemN = lambda name: es.enter_context(Sem(nc, name))

    cols = {n: Sb("c_" + n, [128, DCH]) for n in vnames}
    cpbp = Sb("c_cpbp", [128, DCH])
    bfgp = Sb("c_bfgp", [128, DCH])
    bfup = Sb("c_bfup", [128, DCH])
    onescol_f = Sb("onescol_f", [128, 1])
    onescol_b = Sb("onescol_b", [128, 1], BF16)
    ones_k1 = Sb("ones_k1", [1, 128])
    ident = Sb("ident", [128, 128])
    iota_f = Sb("iota_f", [128, 128])
    pidx = Sb("pidx", [128, 1])
    bo_bc = Sb("bo_bc", [128, D])
    cs127d = Sb("cs127d", [128, D])
    m_cols = Sb("m_cols", [128, 6])
    c_eps5 = Sb("c_eps5", [128, 1])
    c_eps3 = Sb("c_eps3", [128, 1])
    c_1275 = Sb("c_1275", [128, 1])
    c_magic = Sb("c_magic", [128, 1])
    e_gs = Sb("e_gs", [128, NCHUNK])
    e_us = Sb("e_us", [128, NCHUNK])
    e_ds = Sb("e_ds", [128, NCHUNK])
    s3_cols = Sb("s3_cols", [128, NCHUNK])

    PB = SemN("PB")
    DS = SemN("DS")
    SD = SemN("SD")
    GD = SemN("GD")
    VS = SemN("VS")
    SS = SemN("SS")
    GS = SemN("GS")
    TS_ = SemN("TS")

    block = es.enter_context(nc.Block())

    def fence(eng, sem):
        eng.drain().then_inc(sem.h, 1)
        sem.n += 1

    def dma(eng, sem, dst, src, transpose=False, slow=False):
        if transpose:
            eng.dma_start(dst, src, transpose=True).then_inc(sem.h, 16)
        elif slow:
            eng.dma_start(dst, src,
                          allow_slow_non_contiguous=True).then_inc(sem.h, 16)
        else:
            eng.dma_start(dst, src).then_inc(sem.h, 16)
        sem.n += 16

    def wait_all(eng, *sems):
        for s in sems:
            eng.wait_ge(s.h, s.n)

    ROUND_MODE = "magic"
    MAGIC = 12582912.0

    def emit_round_ladder(v, t_ap, scr_ap, out_ap, out_shift):
        v.tensor_copy(scr_ap, t_ap)
        for k in range(7, -1, -1):
            c = float(1 << k)
            v.drain()
            v.tensor_scalar(out_ap, scr_ap, c, None, ALU.is_ge)
            v.drain()
            v.scalar_tensor_tensor(scr_ap, out_ap, -c, scr_ap, ALU.mult, ALU.add)
        v.drain()
        v.tensor_tensor(scr_ap, t_ap, scr_ap, ALU.subtract)
        v.drain()
        v.tensor_scalar(out_ap, scr_ap, float(out_shift), None, ALU.subtract)

    xr = x_d[:, :, :].rearrange("b t d -> (b t) d")
    outr = out_d[:, :, :].rearrange("b t d -> (b t) d")

    @block.sync
    def _(sync):
        for n in vnames:
            dma(sync, DS, cols[n][:, :],
                v_d[n][:].rearrange("(c p) -> p c", p=128), slow=True)
        sync.wait_ge(DS.h, DS.n)

    @block.gpsimd
    def _(g):
        g.iota(iota_f[:], [[1, 128]], channel_multiplier=0,
               allow_small_or_imprecise_dtypes=True)
        g.iota(pidx[:], [[0, 1]], channel_multiplier=1,
               allow_small_or_imprecise_dtypes=True)
        g.memset(onescol_f[:], 1.0)
        g.memset(onescol_b[:], 1.0)
        g.memset(ones_k1[:], 1.0)
        g.memset(c_eps5[:], 1e-5)
        g.memset(c_eps3[:], 1e-3)
        g.memset(c_1275[:], 127.5)
        g.memset(c_magic[:], 12582912.0)
        g.drain()
        g.tensor_scalar(ident[:], iota_f[:], pidx[:], None, ALU.is_equal)
        fence(g, GS)

    @block.vector
    def _(v):
        wait_all(v, DS)
        v.tensor_tensor(cpbp[:], cols["Cp"][:], cols["Bp"][:], ALU.mult)
        fence(v, VS)

    bo_row = Sb("bo_row", [1, D])

    @block.sync
    def _(sync):
        dma(sync, DS, bo_row[:, :], v_d["bo"][:].rearrange("(o d) -> o d", o=1))
        sync.wait_ge(DS.h, DS.n)

    pC = ExitStack()
    pbc = pC.enter_context(nc.psum_tensor("pbc", [128, 512], F32))

    @block.tensor
    def _(t):
        wait_all(t, DS, GS)
        t.matmul(pbc[:, :], ones_k1[:, :], bo_row[:, 0:512],
                 start=True, stop=True).then_inc(TS_.h, 1)
        TS_.n += 1

    @block.vector
    def _(v):
        wait_all(v, TS_)
        v.tensor_copy(bo_bc[:, 0:512], pbc[:, :]).then_inc(VS.h, 1)
        VS.n += 1

    @block.tensor
    def _(t):
        wait_all(t, VS)
        t.matmul(pbc[:, :], ones_k1[:, :], bo_row[:, 512:1024],
                 start=True, stop=True).then_inc(TS_.h, 1)
        TS_.n += 1

    @block.vector
    def _(v):
        wait_all(v, TS_)
        v.tensor_copy(bo_bc[:, 512:1024], pbc[:, :])
        fence(v, VS)

    pC.close()

    @block.sync
    def _(sync):
        wait_all(sync, VS, SS, GS, TS_, DS, SD, GD)
        sync.nop().then_inc(PB.h, 1)
    PB.n += 1

    p1 = ExitStack()
    wg_big = p1.enter_context(nc.sbuf_tensor("wg_big", [128, DCH * D], BF16))
    wi_big = p1.enter_context(nc.sbuf_tensor("wi_big", [128, DCH * D], BF16))
    wg_sb = [wg_big[:, k * D:(k + 1) * D] for k in range(DCH)]
    wi_sb = [wi_big[:, k * D:(k + 1) * D] for k in range(DCH)]
    a_bc = [bass.AP(cols["A"].tensor if hasattr(cols["A"], "tensor") else cols["A"],
                    c, [[DCH, 128], [0, TILE]]) for c in range(DCH)]
    x_t = [p1.enter_context(nc.sbuf_tensor(f"x_t{p}", [128, SUB * D], F32))
           for p in range(2)]
    xn_tm = p1.enter_context(nc.sbuf_tensor("xn_tm", [128, SUB * D], BF16))
    xn_fm = [[p1.enter_context(nc.sbuf_tensor(f"xnfm{p}_{c}", [128, TILE], BF16))
              for c in range(DCH)] for p in range(2)]
    gate_t = [p1.enter_context(nc.sbuf_tensor(f"gate{c}", [128, TILE], F32)) for c in range(DCH)]
    ui_t = [p1.enter_context(nc.sbuf_tensor(f"ui{c}", [128, TILE], F32)) for c in range(DCH)]
    yc_t = [p1.enter_context(nc.sbuf_tensor(f"yc{c}", [128, TILE], F32)) for c in range(DCH)]
    y_t = [p1.enter_context(nc.sbuf_tensor(f"y{c}", [128, TILE], F32)) for c in range(DCH)]
    ysq_t = [p1.enter_context(nc.sbuf_tensor(f"ysq{c}", [128, TILE], BF16)) for c in range(DCH)]
    yn_t = [p1.enter_context(nc.sbuf_tensor(f"yn{c}", [128, TILE], BF16)) for c in range(DCH)]
    sq_scr = p1.enter_context(nc.sbuf_tensor("sq_scr", [128, D], F32))
    st_s1 = p1.enter_context(nc.sbuf_tensor("st_s1", [128, SUB], F32))
    st_s2 = p1.enter_context(nc.sbuf_tensor("st_s2", [128, SUB], F32))
    st_mu = p1.enter_context(nc.sbuf_tensor("st_mu", [128, SUB], F32))
    st_t1 = p1.enter_context(nc.sbuf_tensor("st_t1", [128, SUB], F32))
    st_var = p1.enter_context(nc.sbuf_tensor("st_var", [128, SUB], F32))
    st_r = p1.enter_context(nc.sbuf_tensor("st_r", [128, SUB], F32))
    st_nmr = p1.enter_context(nc.sbuf_tensor("st_nmr", [128, SUB], F32))
    carry = p1.enter_context(nc.sbuf_tensor("carry", [128, DCH], F32))
    r2_sc = p1.enter_context(nc.sbuf_tensor("r2_sc", [1, 4 * TILE], F32))
    p0w = [p1.enter_context(nc.sbuf_tensor(f"p0w{p}", [128, 512], F32)) for p in range(3)]
    p0q = [p1.enter_context(nc.sbuf_tensor(f"p0q{p}", [128, 512], BF16)) for p in range(2)]
    p0c1 = p1.enter_context(nc.sbuf_tensor("p0c1", [128, 512], F32))
    p0c2 = p1.enter_context(nc.sbuf_tensor("p0c2", [128, 512], F32))
    p0stat = p1.enter_context(nc.sbuf_tensor("p0stat", [128, 64], F32))
    p0row = p1.enter_context(nc.sbuf_tensor("p0row", [1, 128], F32))
    p0thr = p1.enter_context(nc.sbuf_tensor("p0thr", [128, 2], F32))
    P0L = [SemN(f"P0L{i}") for i in range(3)]
    P0V = SemN("P0V")
    P0G = SemN("P0G")
    P0SP = [SemN("P0SPa"), SemN("P0SPb")]
    P0D = SemN("P0D")
    P0M = SemN("P0M")
    pg = [p1.enter_context(nc.psum_tensor(f"pg{p}", [128, TILE], F32)) for p in range(2)]
    pi_ = [p1.enter_context(nc.psum_tensor(f"pi{p}", [128, TILE], F32)) for p in range(2)]
    WL = [SemN("WL0"), SemN("WL1")]
    XL = [SemN("XL0"), SemN("XL1")]
    XNT = [SemN("XNT0"), SemN("XNT1")]
    XNS = SemN("XNS")
    YNS = SemN("YNS")
    TSC = SemN("TSC")
    VSC = SemN("VSC")
    SSC = SemN("SSC")
    GSC = SemN("GSC")
    ps1r = p1.enter_context(nc.psum_tensor("ps1r", [1, TILE], F32))
    ps2r = p1.enter_context(nc.psum_tensor("ps2r", [1, TILE], F32))
    pR = p1.enter_context(nc.psum_tensor("pR", [128, TILE], F32))
    pNM = p1.enter_context(nc.psum_tensor("pNM", [128, TILE], F32))

    x_after = {}

    def emit_x_loads(ti):
        par = ti % 2
        t0 = ti * TILE

        @block.scalar
        def _(s, ti=ti, par=par, t0=t0):
            if ti < 2:
                s.wait_ge(PB.h, PB.n)
            else:
                s.wait_ge(SSC.h, 12 * (ti - 2) + 3)
            for ss in range(SUB):
                dma(s, XL[par], x_t[par][:, ss * D:(ss + 1) * D],
                    xr[t0 + ss * 128:t0 + (ss + 1) * 128, :])
            x_after[ti] = XL[par].n

    emit_x_loads(0)
    emit_x_loads(1)

    @block.gpsimd
    def _(g):
        g.wait_ge(PB.h, PB.n)
        g.memset(carry[:, :], 0.0)
        for k in range(DCH):
            dma(g, WL[0], wg_sb[k][:, :], w_d["Wg"][k * 128:(k + 1) * 128, :])
        for k in range(DCH):
            dma(g, WL[0], wi_sb[k][:, :], w_d["Wi"][k * 128:(k + 1) * 128, :])

    p0_mats = [("g", wgT_d, D // 128, 8, 0), ("u", wuT_d, D // 128, 8, 1),
               ("d", wdT_d, H // 128, 2, 2)]
    p0_tasks = []
    for mname, wdram, nrt, ncc, midx in p0_mats:
        inv_size = 1.0 / float(nrt * 128 * ncc * 512)
        for c in range(nrt * ncc):
            p0_tasks.append(("l1", mname, wdram, nrt, ncc, midx, c, inv_size))
        p0_tasks.append(("mean", mname, wdram, nrt, ncc, midx, None, inv_size))
        for c in range(nrt * ncc):
            p0_tasks.append(("l2", mname, wdram, nrt, ncc, midx, c, inv_size))
        p0_tasks.append(("fin", mname, wdram, nrt, ncc, midx, None, inv_size))
    p0_state = {"i": 0, "nl": 0, "nsp": 0,
                "slot_free": [("none", 0)] * 3}

    def _p0_wait(eng, tag_val):
        tag, val = tag_val
        if tag == "P0V":
            eng.wait_ge(P0V.h, val)

    def emit_p0(n_tasks):
        i0 = p0_state["i"]
        for task in p0_tasks[i0:i0 + n_tasks]:
            kind, mname, wdram, nrt, ncc, midx, c, inv_size = task
            if kind in ("l1", "l2"):
                rt, cc = divmod(c, ncc)
                nl = p0_state["nl"]
                lp = nl % 3
                freev = p0_state["slot_free"][lp]

                @block.scalar
                def _(s, nl=nl, lp=lp, wdram=wdram, rt=rt, cc=cc, freev=freev):
                    if nl == 0:
                        s.wait_ge(PB.h, 1)
                    _p0_wait(s, freev)
                    dma(s, P0L[lp], p0w[lp][:, :],
                        wdram[rt * 128:(rt + 1) * 128, cc * 512:(cc + 1) * 512])
                p0_state["nl"] += 1

                if kind == "l1":
                    @block.vector
                    def _(v, c=c, lp=lp, lv=P0L[lp].n):
                        v.wait_ge(P0L[lp].h, lv)
                        v.reduce_sum(p0stat[:, c:c + 1], p0w[lp][:, :], AX.X,
                                     apply_absolute_value=True)
                        v.engine_nop().then_inc(P0V.h, 1)
                        P0V.n += 1
                    p0_state["slot_free"][lp] = ("P0V", P0V.n)
                else:
                    nsp = p0_state["nsp"]

                    @block.vector
                    def _(v, lp=lp, midx=midx, lv=P0L[lp].n, nsp=nsp):
                        v.wait_ge(P0L[lp].h, lv)
                        if True:
                            v.wait_ge(P0M.h, midx + 1)
                        if nsp >= 2:
                            v.wait_ge(P0SP[nsp % 2].h, 16 * (nsp // 2))
                        c2b = [p0c2, p0c1][nsp % 2]
                        v.tensor_scalar(c2b[:, :], p0w[lp][:, :],
                                        p0thr[:, 1:2], None, ALU.is_le)
                        v.drain()
                        v.scalar_tensor_tensor(p0q[nsp % 2][:, :], p0w[lp][:, :],
                                               p0thr[:, 0:1], c2b[:, :],
                                               ALU.is_ge, ALU.subtract)
                        v.engine_nop().then_inc(P0V.h, 1)
                        P0V.n += 1
                    p0_state["slot_free"][lp] = ("P0V", P0V.n)

                    @block.gpsimd
                    def _(g, mname=mname, rt=rt, cc=cc, nsp=nsp, nv=P0V.n):
                        g.wait_ge(P0V.h, nv)
                        dma(g, P0SP[nsp % 2],
                            wq_sp[mname][rt * 128:(rt + 1) * 128,
                                         cc * 512:(cc + 1) * 512],
                            p0q[nsp % 2][:, :])
                    p0_state["nsp"] += 1
            elif kind == "mean":
                nch = nrt * ncc

                @block.vector
                def _(v, nch=nch, nv=P0V.n):
                    v.wait_ge(P0V.h, nv)
                    v.reduce_sum(p0stat[:, 63:64], p0stat[:, 0:nch], AX.X)
                    v.engine_nop().then_inc(P0V.h, 1)
                    P0V.n += 1

                @block.gpsimd
                def _(g, midx=midx, nv=P0V.n):
                    g.wait_ge(P0V.h, nv)
                    dma(g, P0D, mean_sp[midx:midx + 1, :],
                        p0stat[:, 63:64], slow=True)
                    g.wait_ge(P0D.h, P0D.n)
                    dma(g, P0D, p0row[:, :], mean_sp[midx:midx + 1, :])
                    g.wait_ge(P0D.h, P0D.n)
                    g.engine_nop().then_inc(P0G.h, 1)
                    P0G.n += 1

                @block.vector
                def _(v, midx=midx, inv_size=inv_size, gv=P0G.n):
                    v.wait_ge(P0G.h, gv)
                    v.reduce_sum(p0stat[0:1, 0:8],
                                 p0row[:, :].rearrange("o (a b) -> o a b", b=16),
                                 AX.X)
                    v.drain()
                    v.reduce_sum(p0stat[0:1, 8:9], p0stat[0:1, 0:8], AX.X)
                    v.drain()
                    v.tensor_scalar(p0stat[0:1, 9:10], p0stat[0:1, 8:9],
                                    inv_size, 1e-5, ALU.mult, ALU.max)
                    v.engine_nop().then_inc(P0V.h, 1)
                    P0V.n += 1

                @block.gpsimd
                def _(g, midx=midx, nv=P0V.n):
                    g.wait_ge(P0V.h, nv)
                    dma(g, P0D, mean_sp[midx:midx + 1, 0:1], p0stat[0:1, 9:10])
                    g.wait_ge(P0D.h, P0D.n)
                    dma(g, P0D, m_cols[:, midx:midx + 1],
                        bass.AP(mean_sp, midx * 128, [[0, 128], [1, 1]]))
                    g.wait_ge(P0D.h, P0D.n)
                    g.engine_nop().then_inc(P0G.h, 1)
                    P0G.n += 1

                @block.vector
                def _(v, midx=midx, gv=P0G.n):
                    v.wait_ge(P0G.h, gv)
                    v.tensor_scalar(p0thr[:, 0:1], m_cols[:, midx:midx + 1],
                                    0.5, None, ALU.mult)
                    v.tensor_scalar(p0thr[:, 1:2], m_cols[:, midx:midx + 1],
                                    -0.5, None, ALU.mult)
                    v.engine_nop().then_inc(P0M.h, 1)
                    P0M.n += 1
            else:
                @block.vector
                def _(v, a=P0SP[0].n, b=P0SP[1].n):
                    v.wait_ge(P0SP[0].h, a)
                    v.wait_ge(P0SP[1].h, b)
                    v.engine_nop().then_inc(P0V.h, 1)
                    P0V.n += 1
        p0_state["i"] = i0 + n_tasks

    P0_PER_TILE = (len(p0_tasks) + NTILE - 1) // NTILE

    for ti in range(NTILE):
        t0 = ti * TILE
        par = ti % 2
        batch_start = (t0 % T) == 0

        @block.vector
        def _(v, ti=ti, par=par):
            v.wait_ge(XL[par].h, x_after[ti])
            for ss in range(SUB):
                v.reduce_sum(st_s1[:, ss:ss + 1],
                             x_t[par][:, ss * D:(ss + 1) * D], AX.X)
            fence(v, VSC)

        @block.scalar
        def _(s, ti=ti, par=par):
            s.wait_ge(XL[par].h, x_after[ti])
            for ss in range(SUB):
                if ss:
                    s.drain()
                s.activation(sq_scr[:, :], x_t[par][:, ss * D:(ss + 1) * D],
                             AF.Square, bias=0.0, scale=1.0,
                             accum_out=st_s2[:, ss:ss + 1])
            fence(s, SSC)

        @block.vector
        def _(v, ti=ti):
            v.wait_ge(SSC.h, 12 * ti + 1)
            v.tensor_scalar(st_mu[:, :], st_s1[:, :], 1.0 / D, None, ALU.mult)
            v.drain()
            v.scalar_tensor_tensor(st_t1[:, :], st_mu[:, :], -1.0, st_mu[:, :],
                                   ALU.mult, ALU.mult)
            v.drain()
            v.scalar_tensor_tensor(st_var[:, :], st_s2[:, :], 1.0 / D, st_t1[:, :],
                                   ALU.mult, ALU.add)
            fence(v, VSC)

        @block.scalar
        def _(s, ti=ti):
            s.wait_ge(VSC.h, 15 * ti + 2)
            s.activation(st_t1[:, :], st_var[:, :], AF.Sqrt, bias=c_eps5[:, :],
                         scale=1.0)
            fence(s, SSC)

        @block.vector
        def _(v, ti=ti):
            v.wait_ge(SSC.h, 12 * ti + 2)
            v.reciprocal(st_r[:, :], st_t1[:, :])
            v.drain()
            v.scalar_tensor_tensor(st_nmr[:, :], st_mu[:, :], -1.0, st_r[:, :],
                                   ALU.mult, ALU.mult)
            fence(v, VSC)

        @block.scalar
        def _(s, ti=ti, par=par):
            s.wait_ge(VSC.h, 15 * ti + 3)
            s.wait_ge(XNS.h, 64 * ti)
            for ss in range(SUB):
                s.activation(xn_tm[:, ss * D:(ss + 1) * D],
                             x_t[par][:, ss * D:(ss + 1) * D], AF.Identity,
                             bias=st_nmr[:, ss:ss + 1], scale=st_r[:, ss:ss + 1])
            fence(s, SSC)

        @block.sync
        def _(sync, ti=ti, par=par, t0=t0):
            sync.wait_ge(SSC.h, 12 * ti + 3)
            for ss in range(SUB):
                dma(sync, XNS, xn_sp[t0 + ss * 128:t0 + (ss + 1) * 128, :],
                    xn_tm[:, ss * D:(ss + 1) * D])
            sync.wait_ge(XNS.h, XNS.n)
            if ti >= 2:
                sync.wait_ge(TSC.h, 20 * (ti - 1))
            for c in range(DCH):
                dma(sync, XNT[par], xn_fm[par][c][:, :],
                    xn_sp[t0:t0 + TILE, c * 128:(c + 1) * 128], transpose=True)

        for m in range(DCH):
            @block.tensor
            def _(t, ti=ti, par=par, m=m):
                if m == 0:
                    t.wait_ge(XNT[par].h, XNT[par].n)
                    if ti == 0:
                        t.wait_ge(WL[0].h, 256)
                if m >= 2:
                    t.wait_ge(SSC.h, 12 * ti + (m - 2) + 4)
                    t.wait_ge(VSC.h, 15 * ti + (m - 2) + 4)
                elif ti >= 1:
                    t.wait_ge(SSC.h, 12 * (ti - 1) + (m + 6) + 4)
                    t.wait_ge(VSC.h, 15 * (ti - 1) + (m + 6) + 4)
                for k in range(DCH):
                    i1 = t.matmul(pg[m % 2][:, :],
                                  wg_sb[k][:, m * 128:(m + 1) * 128],
                                  xn_fm[par][k][:, :],
                                  start=(k == 0), stop=(k == DCH - 1))
                i1.then_inc(TSC.h, 1)
                TSC.n += 1
                for k in range(DCH):
                    i2 = t.matmul(pi_[m % 2][:, :],
                                  wi_sb[k][:, m * 128:(m + 1) * 128],
                                  xn_fm[par][k][:, :],
                                  start=(k == 0), stop=(k == DCH - 1))
                i2.then_inc(TSC.h, 1)
                TSC.n += 1

            @block.scalar
            def _(s, ti=ti, m=m):
                s.wait_ge(TSC.h, 20 * ti + 2 * m + 1)
                if m == 0 and ti >= 1:
                    s.wait_ge(VSC.h, 15 * (ti - 1) + 12)
                s.activation(gate_t[m][:, :], pg[m % 2][:, :], AF.Silu,
                             bias=cols["bg"][:, m:m + 1],
                             scale=1.0).then_inc(SSC.h, 1)
                SSC.n += 1

            @block.vector
            def _(v, ti=ti, m=m):
                v.wait_ge(TSC.h, 20 * ti + 2 * m + 2)
                v.tensor_scalar(ui_t[m][:, :], pi_[m % 2][:, :],
                                cols["bi"][:, m:m + 1], None,
                                ALU.add).then_inc(VSC.h, 1)
                VSC.n += 1

        @block.vector
        def _(v, ti=ti, batch_start=batch_start):
            v.drain()
            v.wait_ge(SSC.h, 12 * ti + 11)
            for m in range(DCH):
                v.tensor_tensor(gate_t[m][:, :], gate_t[m][:, :], ui_t[m][:, :],
                                ALU.mult)
            v.drain()
            for m in range(DCH):
                v.tensor_scalar(ui_t[m][:, :], gate_t[m][:, :],
                                cpbp[:, m:m + 1], None, ALU.mult)
            v.drain()
            if batch_start:
                v.memset(carry[:, :], 0.0)
                v.drain()
            if ti == 0:
                v.wait_ge(GSC.h, 1) if False else None
            for m in range(DCH):
                v.tensor_tensor_scan(yc_t[m][:, :], a_bc[m], ui_t[m][:, :],
                                     carry[:, m:m + 1], ALU.mult, ALU.add)
            v.drain()
            for m in range(DCH):
                v.tensor_copy(carry[:, m:m + 1], yc_t[m][:, TILE - 1:TILE])
            v.drain()
            for m in range(DCH):
                v.scalar_tensor_tensor(y_t[m][:, :], gate_t[m][:, :],
                                       cols["Dp"][:, m:m + 1], yc_t[m][:, :],
                                       ALU.mult, ALU.add)
            fence(v, VSC)

        emit_p0(P0_PER_TILE // 2)

        @block.gpsimd
        def _(g, ti=ti):
            g.wait_ge(VSC.h, 15 * ti + 12)
            if ti >= 1:
                g.wait_ge(TSC.h, 20 * (ti - 1) + 18)
            for m in range(DCH):
                g.tensor_tensor(ysq_t[m][:, :], y_t[m][:, :], y_t[m][:, :],
                                ALU.mult)
            fence(g, GSC)

        @block.tensor
        def _(t, ti=ti):
            t.wait_ge(GSC.h, ti + 1)
            if ti >= 1:
                t.wait_ge(VSC.h, 15 * (ti - 1) + 13)
            for m in range(DCH):
                i1 = t.matmul(ps1r[:, :], onescol_f[:, :], y_t[m][:, :],
                              start=(m == 0), stop=(m == DCH - 1))
            i1.then_inc(TSC.h, 1)
            TSC.n += 1
            for m in range(DCH):
                i2 = t.matmul(ps2r[:, :], onescol_b[:, :], ysq_t[m][:, :],
                              start=(m == 0), stop=(m == DCH - 1))
            i2.then_inc(TSC.h, 1)
            TSC.n += 1

        @block.vector
        def _(v, ti=ti):
            v.wait_ge(TSC.h, 20 * ti + 18)
            v.tensor_scalar(r2_sc[:, 0:TILE], ps1r[:, :], 1.0 / D, None, ALU.mult)
            v.tensor_scalar(r2_sc[:, TILE:2 * TILE], ps2r[:, :], 1.0 / D, None,
                            ALU.mult)
            v.drain()
            v.tensor_tensor(r2_sc[:, 2 * TILE:3 * TILE], r2_sc[:, 0:TILE],
                            r2_sc[:, 0:TILE], ALU.mult)
            v.drain()
            v.tensor_tensor(r2_sc[:, 3 * TILE:4 * TILE], r2_sc[:, TILE:2 * TILE],
                            r2_sc[:, 2 * TILE:3 * TILE], ALU.subtract)
            fence(v, VSC)

        @block.scalar
        def _(s, ti=ti):
            s.wait_ge(VSC.h, 15 * ti + 13)
            s.activation(r2_sc[:, 2 * TILE:3 * TILE], r2_sc[:, 3 * TILE:4 * TILE],
                         AF.Sqrt, bias=c_eps3[0:1, :], scale=1.0)
            fence(s, SSC)

        @block.vector
        def _(v, ti=ti):
            v.wait_ge(SSC.h, 12 * ti + 12)
            v.reciprocal(r2_sc[:, TILE:2 * TILE], r2_sc[:, 2 * TILE:3 * TILE])
            v.drain()
            v.scalar_tensor_tensor(r2_sc[:, 2 * TILE:3 * TILE], r2_sc[:, 0:TILE],
                                   -1.0, r2_sc[:, TILE:2 * TILE],
                                   ALU.mult, ALU.mult)
            fence(v, VSC)

        @block.tensor
        def _(t, ti=ti):
            t.wait_ge(VSC.h, 15 * ti + 14)
            if ti >= 1:
                t.wait_ge(VSC.h, 15 * (ti - 1) + 15)
            i1 = t.matmul(pR[:, :], ones_k1[:, :], r2_sc[:, TILE:2 * TILE],
                          start=True, stop=True)
            i1.then_inc(TSC.h, 1)
            TSC.n += 1
            i2 = t.matmul(pNM[:, :], ones_k1[:, :], r2_sc[:, 2 * TILE:3 * TILE],
                          start=True, stop=True)
            i2.then_inc(TSC.h, 1)
            TSC.n += 1

        @block.vector
        def _(v, ti=ti):
            v.wait_ge(TSC.h, 20 * ti + 20)
            v.wait_ge(YNS.h, 128 * ti)
            for m in range(DCH):
                v.scalar_tensor_tensor(yc_t[m][:, :], y_t[m][:, :], 1.0,
                                       pR[:, :], ALU.mult, ALU.mult)
            v.drain()
            for m in range(DCH):
                v.tensor_tensor(yn_t[m][:, :], yc_t[m][:, :], pNM[:, :], ALU.add)
            fence(v, VSC)

        @block.sync
        def _(sync, ti=ti, t0=t0):
            sync.wait_ge(VSC.h, 15 * ti + 15)
            for m in range(DCH):
                dma(sync, YNS, yn_sp[m * 128:(m + 1) * 128, t0:t0 + TILE],
                    yn_t[m][:, :])

        if ti + 2 < NTILE:
            emit_x_loads(ti + 2)
        emit_p0(P0_PER_TILE - P0_PER_TILE // 2)

    @block.sync
    def _(sync):
        wait_all(sync, VS, SS, GS, TS_, DS, SD, GD)
        wait_all(sync, WL[0], WL[1], XL[0], XL[1], XNT[0], XNT[1], XNS, YNS,
                 TSC, VSC, SSC, GSC)
        wait_all(sync, P0L[0], P0L[1], P0L[2], P0V, P0G, P0SP[0], P0SP[1], P0D, P0M)
        sync.nop().then_inc(PB.h, 1)
    PB.n += 1
    p1.close()

    p2 = ExitStack()
    wfg_sb = [p2.enter_context(nc.sbuf_tensor(f"wfg{k}", [128, D], BF16)) for k in range(DCH)]
    wfu_sb = [p2.enter_context(nc.sbuf_tensor(f"wfu{k}", [128, D], BF16)) for k in range(DCH)]
    wfd_sb = [p2.enter_context(nc.sbuf_tensor(f"wfd{k}", [128, D], BF16)) for k in range(DCH)]
    wo_sb = [p2.enter_context(nc.sbuf_tensor(f"wo{k}", [128, D], BF16)) for k in range(DCH)]
    wf32b = [p2.enter_context(nc.sbuf_tensor(f"wf32b{p}", [128, D], F32))
             for p in range(2)]
    WLB = [SemN("WLB0"), SemN("WLB1")]
    WPV = SemN("WPV")
    WPT = SemN("WPT")
    yn_in = [[p2.enter_context(nc.sbuf_tensor(f"ynin{p}_{c}", [128, TILE], BF16))
              for c in range(DCH)] for p in range(2)]
    big_a = p2.enter_context(nc.sbuf_tensor("big_a", [128, SUB * D], F32))
    big_b = p2.enter_context(nc.sbuf_tensor("big_b", [128, SUB * D], F32))
    hg_t = [big_a[:, c * TILE:(c + 1) * TILE] for c in range(DCH)]
    hu_t = [big_b[:, c * TILE:(c + 1) * TILE] for c in range(DCH)]
    h_t = [p2.enter_context(nc.sbuf_tensor(f"h{c}", [128, TILE], BF16)) for c in range(DCH)]
    fd_t = [p2.enter_context(nc.sbuf_tensor(f"fd{c}", [128, TILE], BF16)) for c in range(DCH)]
    xre_t = p2.enter_context(nc.sbuf_tensor("xre", [128, SUB * D], F32))
    x2_t = [p2.enter_context(nc.sbuf_tensor(f"x2t{p}", [128, SUB * D], F32))
            for p in range(2)]
    xn3_t = big_a
    tq_t = big_b
    mq_t = big_a
    xq_t = p2.enter_context(nc.sbuf_tensor("xq", [128, SUB * D], BF16))
    sq_b = big_b[:, 0:D]
    s1_b = p2.enter_context(nc.sbuf_tensor("s1_b", [128, SUB], F32))
    s2_b = p2.enter_context(nc.sbuf_tensor("s2_b", [128, SUB], F32))
    mu_b = p2.enter_context(nc.sbuf_tensor("mu_b", [128, SUB], F32))
    t1_b = p2.enter_context(nc.sbuf_tensor("t1_b", [128, SUB], F32))
    var_b = p2.enter_context(nc.sbuf_tensor("var_b", [128, SUB], F32))
    r_b = p2.enter_context(nc.sbuf_tensor("r_b", [128, SUB], F32))
    nmr_b = p2.enter_context(nc.sbuf_tensor("nmr_b", [128, SUB], F32))
    mx_b = p2.enter_context(nc.sbuf_tensor("mx_b", [128, SUB], F32))
    pfg = [p2.enter_context(nc.psum_tensor(f"pfg{p}", [128, TILE], F32)) for p in range(3)]
    pfu = [p2.enter_context(nc.psum_tensor(f"pfu{p}", [128, TILE], F32)) for p in range(3)]
    pot = [p2.enter_context(nc.psum_tensor(f"pot{p}", [128, 512], F32)) for p in range(2)]
    pbias = pot[0][:, 0:2 * DCH]
    YNL = [SemN("YNL0"), SemN("YNL1")]
    X2P = [SemN("X2P0"), SemN("X2P1")]
    XQS = SemN("XQS")
    XRL = SemN("XRL")
    TSB = SemN("TSB")
    VSB = SemN("VSB")
    SSB = SemN("SSB")
    GSB = SemN("GSB")

    WCB = SemN("WCB")
    WCO = SemN("WCO")
    @block.gpsimd
    def _(g):
        g.wait_ge(PB.h, PB.n)
        for k in range(DCH):
            dma(g, WCB, wfd_sb[k][:, :], w_d["Wfd"][k * 128:(k + 1) * 128, :])
        for k in range(DCH):
            dma(g, WCO, wo_sb[k][:, :], w_d["Wo"][k * 128:(k + 1) * 128, :])

    wlist_b = ([("Wfg", wfg_sb, True, k) for k in range(DCH)]
               + [("Wfu", wfu_sb, True, k) for k in range(DCH)])
    wlb_after = {}
    for j, (wname, sb, scale_gamma, k) in enumerate(wlist_b):
        @block.sync
        def _(sync, j=j, wname=wname, k=k):
            if j < 2:
                sync.wait_ge(PB.h, PB.n)
            else:
                sync.wait_ge(WPV.h, j - 1)
                sync.wait_ge(WPT.h, j - 1)
            dma(sync, WLB[j % 2], wf32b[j % 2][:, :],
                w_d[wname][k * 128:(k + 1) * 128, :])
            wlb_after[j] = WLB[j % 2].n

        if scale_gamma:
            @block.tensor
            def _(t, j=j, k=k):
                t.wait_ge(WLB[j % 2].h, wlb_after[j])
                base = (j // DCH) * DCH
                for m in range(DCH):
                    t.matmul(pbias[:, base + m:base + m + 1],
                             wf32b[j % 2][:, m * 128:(m + 1) * 128],
                             cols["beta"][:, k:k + 1],
                             start=(j == 0 and m == 0),
                             stop=(j == 15 and m == DCH - 1),
                             skip_group_check=True)
                fence(t, WPT)

        @block.vector
        def _(v, j=j, sb=sb, k=k, scale_gamma=scale_gamma):
            v.wait_ge(WLB[j % 2].h, wlb_after[j])
            if scale_gamma:
                v.tensor_scalar(sb[k][:, :], wf32b[j % 2][:, :],
                                cols["gamma"][:, k:k + 1], None, ALU.mult)
            else:
                v.tensor_copy(sb[k][:, :], wf32b[j % 2][:, :])
            v.engine_nop().then_inc(WPV.h, 1)
            WPV.n += 1

    @block.vector
    def _(v):
        v.wait_ge(WPT.h, 16)
        v.tensor_tensor(bfgp[:, :], pbias[:, 0:DCH], cols["bfg"][:, :], ALU.add)
        v.tensor_tensor(bfup[:, :], pbias[:, DCH:2 * DCH], cols["bfu"][:, :],
                        ALU.add)
        fence(v, VS)
    bias_ready = VS.n

    yn_after = {}
    yn_after = {}

    def emit_yn_loads(ti):
        par = ti % 2
        t0 = ti * TILE

        @block.sync
        def _(sync, ti=ti, par=par, t0=t0):
            if ti >= 2:
                sync.wait_ge(TSB.h, 32 * (ti - 2) + 16)
            for c in range(DCH):
                dma(sync, YNL[par], yn_in[par][c][:, :],
                    yn_sp[c * 128:(c + 1) * 128, t0:t0 + TILE])
            yn_after[ti] = YNL[par].n

    emit_yn_loads(0)
    emit_yn_loads(1)

    @block.sync
    def _(sync):
        for ss in range(SUB):
            dma(sync, XRL, xre_t[:, ss * D:(ss + 1) * D],
                xr[ss * 128:(ss + 1) * 128, :])

    for ti in range(NTILE):
        t0 = ti * TILE
        par = ti % 2

        for m in range(DCH):
            @block.tensor
            def _(t, ti=ti, par=par, m=m):
                if m == 0:
                    t.wait_ge(YNL[par].h, yn_after[ti])
                    if ti == 0:
                        t.wait_ge(WPV.h, len(wlist_b))
                if m >= 3:
                    t.wait_ge(SSB.h, 20 * ti + (m - 3) + 1)
                    t.wait_ge(VSB.h, 22 * ti + (m - 3) + 1)
                elif ti >= 1:
                    t.wait_ge(SSB.h, 20 * (ti - 1) + 8 + {0: 6, 1: 7, 2: 5}[m] + 1)
                for k in range(DCH):
                    i1 = t.matmul(pfg[m % 3][:, :],
                                  wfg_sb[k][:, m * 128:(m + 1) * 128],
                                  yn_in[par][k][:, :],
                                  start=(k == 0), stop=(k == DCH - 1))
                i1.then_inc(TSB.h, 1)
                TSB.n += 1
                for k in range(DCH):
                    i2 = t.matmul(pfu[m % 3][:, :],
                                  wfu_sb[k][:, m * 128:(m + 1) * 128],
                                  yn_in[par][k][:, :],
                                  start=(k == 0), stop=(k == DCH - 1))
                i2.then_inc(TSB.h, 1)
                TSB.n += 1

            @block.scalar
            def _(s, ti=ti, m=m):
                s.wait_ge(TSB.h, 32 * ti + 2 * m + 1)
                if ti == 0 and m == 0:
                    s.wait_ge(VS.h, bias_ready)
                s.activation(hg_t[m], pfg[m % 3][:, :], AF.Silu,
                             bias=bfgp[:, m:m + 1], scale=1.0).then_inc(SSB.h, 1)
                SSB.n += 1

            @block.vector
            def _(v, ti=ti, m=m):
                v.wait_ge(TSB.h, 32 * ti + 2 * m + 2)
                if ti == 0 and m == 0:
                    v.wait_ge(VS.h, bias_ready)
                v.tensor_scalar(hu_t[m], pfu[m % 3][:, :], bfup[:, m:m + 1],
                                None, ALU.add).then_inc(VSB.h, 1)
                VSB.n += 1

        @block.vector
        def _(v, ti=ti):
            v.drain()
            v.wait_ge(SSB.h, 20 * ti + 8)
            v.wait_ge(TSB.h, 32 * ti)
            for m in range(DCH):
                v.tensor_tensor(h_t[m][:, :], hg_t[m], hu_t[m], ALU.mult)
            fence(v, VSB)

        for m in range(DCH):
            @block.tensor
            def _(t, ti=ti, m=m):
                if m == 0:
                    t.wait_ge(VSB.h, 22 * ti + 9)
                    if ti == 0:
                        t.wait_ge(WCB.h, 128)
                if m >= 3:
                    t.wait_ge(SSB.h, 20 * ti + 8 + (m - 3) + 1)
                else:
                    t.wait_ge(SSB.h, 20 * ti + {0: 6, 1: 7, 2: 5}[m] + 1)
                for k in range(DCH):
                    i1 = t.matmul(pfg[m % 3][:, :],
                                  wfd_sb[k][:, m * 128:(m + 1) * 128],
                                  h_t[k][:, :],
                                  start=(k == 0), stop=(k == DCH - 1))
                i1.then_inc(TSB.h, 1)
                TSB.n += 1

            @block.scalar
            def _(s, ti=ti, m=m):
                s.wait_ge(TSB.h, 32 * ti + 16 + m + 1)
                s.activation(fd_t[m][:, :], pfg[m % 3][:, :], AF.Identity,
                             bias=cols["bfd"][:, m:m + 1],
                             scale=1.0).then_inc(SSB.h, 1)
                SSB.n += 1

        for g in range(8):
            ss_, nh = divmod(g, 2)

            @block.tensor
            def _(t, ti=ti, g=g, ss_=ss_, nh=nh):
                if g == 0:
                    t.wait_ge(SSB.h, 20 * ti + 16)
                    if ti == 0:
                        t.wait_ge(WCO.h, 128)
                if g >= 2:
                    t.wait_ge(VSB.h, 22 * ti + 10 + (g - 2))
                elif ti >= 1:
                    t.wait_ge(VSB.h, 22 * (ti - 1) + 10 + (g + 6))
                for k in range(DCH):
                    i1 = t.matmul(pot[g % 2][:, :],
                                  fd_t[k][:, ss_ * 128:(ss_ + 1) * 128],
                                  wo_sb[k][:, nh * 512:(nh + 1) * 512],
                                  start=(k == 0), stop=(k == DCH - 1))
                i1.then_inc(TSB.h, 1)
                TSB.n += 1

            @block.vector
            def _(v, ti=ti, par=par, g=g, ss_=ss_, nh=nh):
                v.wait_ge(TSB.h, 32 * ti + 24 + g + 1)
                if g == 0:
                    if ti >= 2:
                        v.wait_ge(X2P[par].h, 64 * (ti // 2))
                        v.wait_ge(SSB.h, 20 * (ti - 1) + 19)
                v.scalar_tensor_tensor(
                    x2_t[par][:, ss_ * D + nh * 512:ss_ * D + (nh + 1) * 512],
                    pot[g % 2][:, :], 1.0, bo_bc[:, nh * 512:(nh + 1) * 512],
                    ALU.mult, ALU.add).then_inc(VSB.h, 1)
                VSB.n += 1

        @block.gpsimd
        def _(g_, ti=ti, par=par):
            g_.wait_ge(VSB.h, 22 * ti + 17)
            g_.wait_ge(XRL.h, 64 * (ti + 1))
            for ss in range(SUB):
                g_.tensor_tensor(x2_t[par][:, ss * D:(ss + 1) * D],
                                 x2_t[par][:, ss * D:(ss + 1) * D],
                                 xre_t[:, ss * D:(ss + 1) * D], ALU.add)
            fence(g_, GSB)

        @block.sync
        def _(sync, ti=ti, par=par, t0=t0):
            if ti + 1 < NTILE:
                sync.wait_ge(GSB.h, ti + 1)
                for ss in range(SUB):
                    dma(sync, XRL, xre_t[:, ss * D:(ss + 1) * D],
                        xr[t0 + TILE + ss * 128:t0 + TILE + (ss + 1) * 128, :])
            sync.wait_ge(GSB.h, ti + 1)
            for ss in range(SUB):
                dma(sync, X2P[par], x2_sp[t0 + ss * 128:t0 + (ss + 1) * 128, :],
                    x2_t[par][:, ss * D:(ss + 1) * D])

        @block.vector
        def _(v, ti=ti, par=par):
            v.wait_ge(GSB.h, ti + 1)
            for ss in range(SUB):
                v.reduce_sum(s1_b[:, ss:ss + 1], x2_t[par][:, ss * D:(ss + 1) * D],
                             AX.X)
            fence(v, VSB)

        @block.scalar
        def _(s, ti=ti, par=par):
            s.wait_ge(GSB.h, ti + 1)
            for ss in range(SUB):
                if ss:
                    s.drain()
                s.activation(sq_b, x2_t[par][:, ss * D:(ss + 1) * D],
                             AF.Square, bias=0.0, scale=1.0,
                             accum_out=s2_b[:, ss:ss + 1])
            fence(s, SSB)

        @block.vector
        def _(v, ti=ti):
            v.wait_ge(SSB.h, 20 * ti + 17)
            v.tensor_scalar(mu_b[:, :], s1_b[:, :], 1.0 / D, None, ALU.mult)
            v.drain()
            v.scalar_tensor_tensor(t1_b[:, :], mu_b[:, :], -1.0, mu_b[:, :],
                                   ALU.mult, ALU.mult)
            v.drain()
            v.scalar_tensor_tensor(var_b[:, :], s2_b[:, :], 1.0 / D, t1_b[:, :],
                                   ALU.mult, ALU.add)
            fence(v, VSB)

        @block.scalar
        def _(s, ti=ti):
            s.wait_ge(VSB.h, 22 * ti + 19)
            s.activation(t1_b[:, :], var_b[:, :], AF.Sqrt, bias=c_eps5[:, :],
                         scale=1.0)
            fence(s, SSB)

        @block.vector
        def _(v, ti=ti):
            v.wait_ge(SSB.h, 20 * ti + 18)
            v.reciprocal(r_b[:, :], t1_b[:, :])
            v.drain()
            v.scalar_tensor_tensor(nmr_b[:, :], mu_b[:, :], -1.0, r_b[:, :],
                                   ALU.mult, ALU.mult)
            fence(v, VSB)

        @block.scalar
        def _(s, ti=ti, par=par):
            s.wait_ge(VSB.h, 22 * ti + 20)
            for ss in range(SUB):
                s.activation(xn3_t[:, ss * D:(ss + 1) * D],
                             x2_t[par][:, ss * D:(ss + 1) * D], AF.Identity,
                             bias=nmr_b[:, ss:ss + 1], scale=r_b[:, ss:ss + 1])
            fence(s, SSB)

        @block.vector
        def _(v, ti=ti):
            v.wait_ge(SSB.h, 20 * ti + 19)
            for ss in range(SUB):
                v.reduce_max(mx_b[:, ss:ss + 1], xn3_t[:, ss * D:(ss + 1) * D],
                             AX.X, apply_absolute_value=True)
            v.drain()
            v.tensor_scalar(mx_b[:, :], mx_b[:, :], 1e-5, None, ALU.max)
            v.drain()
            v.reciprocal(t1_b[:, :], mx_b[:, :])
            v.drain()
            v.tensor_scalar(s3_cols[:, ti * SUB:(ti + 1) * SUB], t1_b[:, :],
                            127.0, None, ALU.mult)
            v.drain()
            v.tensor_scalar(t1_b[:, :], mx_b[:, :], 1.0 / 127.0, None, ALU.mult)
            v.drain()
            v.tensor_scalar(e_gs[:, ti * SUB:(ti + 1) * SUB], t1_b[:, :],
                            m_cols[:, 0:1], None, ALU.mult)
            v.tensor_scalar(e_us[:, ti * SUB:(ti + 1) * SUB], t1_b[:, :],
                            m_cols[:, 1:2], None, ALU.mult)
            fence(v, VSB)

        @block.scalar
        def _(s, ti=ti):
            s.wait_ge(VSB.h, 22 * ti + 21)
            for ss in range(SUB):
                s.activation(tq_t[:, ss * D:(ss + 1) * D],
                             xn3_t[:, ss * D:(ss + 1) * D], AF.Identity,
                             bias=(c_1275[:, :] if ROUND_MODE == "ladder"
                                   else c_magic[:, :]),
                             scale=s3_cols[:, ti * SUB + ss:ti * SUB + ss + 1])
            fence(s, SSB)

        @block.vector
        def _(v, ti=ti):
            v.wait_ge(SSB.h, 20 * ti + 20)
            v.wait_ge(XQS.h, 64 * ti)
            if ROUND_MODE == "ladder":
                emit_round_ladder(v, tq_t[:, :], mq_t[:, :], xq_t[:, :], 127.0)
            else:
                v.tensor_scalar(xq_t[:, :], tq_t[:, :], MAGIC, None,
                                ALU.subtract)
            fence(v, VSB)

        @block.sync
        def _(sync, ti=ti, t0=t0):
            sync.wait_ge(VSB.h, 22 * ti + 22)
            for ss in range(SUB):
                dma(sync, XQS, xq_sp[t0 + ss * 128:t0 + (ss + 1) * 128, :],
                    xq_t[:, ss * D:(ss + 1) * D])

        if ti + 2 < NTILE:
            emit_yn_loads(ti + 2)

    @block.sync
    def _(sync):
        wait_all(sync, VS, SS, GS, TS_, DS, SD, GD)
        wait_all(sync, YNL[0], YNL[1], X2P[0], X2P[1], XQS, XRL,
                 TSB, VSB, SSB, GSB, WCB, WCO)
        sync.nop().then_inc(PB.h, 1)
    PB.n += 1
    p2.close()

    p3 = ExitStack()
    wqg_sb = [p3.enter_context(nc.sbuf_tensor(f"wqg{k}", [128, H], BF16)) for k in range(DCH)]
    wqu_sb = [p3.enter_context(nc.sbuf_tensor(f"wqu{k}", [128, H], BF16)) for k in range(DCH)]
    xqf = [[p3.enter_context(nc.sbuf_tensor(f"xqf{p}_{k}", [128, TILE], BF16))
            for k in range(DCH)] for p in range(2)]
    sg_m = p3.enter_context(nc.sbuf_tensor("sg_m", [128, H], F32))
    h_m = p3.enter_context(nc.sbuf_tensor("h_m", [128, H], F32))
    hq_m = [p3.enter_context(nc.sbuf_tensor(f"hq_m{p}", [128, H], BF16))
            for p in range(2)]
    mx2_b = p3.enter_context(nc.sbuf_tensor("mx2_b", [128, 2], F32))
    s4_b = p3.enter_context(nc.sbuf_tensor("s4_b", [128, 2], F32))
    t2_b = p3.enter_context(nc.sbuf_tensor("t2_b", [128, 2], F32))
    pga = [p3.enter_context(nc.psum_tensor(f"pga{p}", [128, 512], F32)) for p in range(4)]
    pua = [p3.enter_context(nc.psum_tensor(f"pua{p}", [128, 512], F32)) for p in range(4)]
    XQ = [SemN("XQ0"), SemN("XQ1")]
    HQS = [SemN("HQS0"), SemN("HQS1")]
    TSA = SemN("TSA")
    VSA = SemN("VSA")
    SSA = SemN("SSA")

    @block.sync
    def _(sync):
        for k in range(DCH):
            dma(sync, DS, wqg_sb[k][:, :], wq_sp["g"][k * 128:(k + 1) * 128, :])
            dma(sync, DS, wqu_sb[k][:, :], wq_sp["u"][k * 128:(k + 1) * 128, :])
        sync.wait_ge(DS.h, DS.n)

    xq_after = {}

    def emit_xq_loads(gi):
        par = gi % 2
        t0 = gi * TILE

        @block.sync
        def _(sync, gi=gi, par=par, t0=t0):
            if gi >= 2:
                sync.wait_ge(TSA.h, 2 * 8 * SUB * (gi - 1))
            for k in range(DCH):
                dma(sync, XQ[par], xqf[par][k][:, :],
                    xq_sp[t0:t0 + TILE, k * 128:(k + 1) * 128], transpose=True)
            xq_after[gi] = XQ[par].n

    emit_xq_loads(0)
    emit_xq_loads(1)

    SPC = 9 if ROUND_MODE == "ladder" else 8
    for ci in range(NCHUNK):
        gi = ci // SUB
        par = gi % 2
        so = (ci % SUB) * 128
        cpar = ci % 2
        t0 = ci * 128

        for ob in range(8):
            G = ci * 16 + ob * 2

            @block.tensor
            def _(t, ci=ci, gi=gi, par=par, so=so, ob=ob):
                if ob == 0 and ci % SUB == 0:
                    t.wait_ge(XQ[par].h, xq_after[gi])
                slot = ob % 4
                if ob >= 4:
                    t.wait_ge(SSA.h, SPC * ci + (ob - 4) + 1)
                    t.wait_ge(VSA.h, 10 * ci + (ob - 4) + 1)
                elif ci >= 1:
                    t.wait_ge(SSA.h, SPC * (ci - 1) + (ob + 4) + 1)
                    t.wait_ge(VSA.h, 10 * (ci - 1) + (ob + 4) + 1)
                for k in range(DCH):
                    i1 = t.matmul(pga[slot][:, :], xqf[par][k][:, so:so + 128],
                                  wqg_sb[k][:, ob * 512:(ob + 1) * 512],
                                  start=(k == 0), stop=(k == DCH - 1))
                    i2 = t.matmul(pua[slot][:, :], xqf[par][k][:, so:so + 128],
                                  wqu_sb[k][:, ob * 512:(ob + 1) * 512],
                                  start=(k == 0), stop=(k == DCH - 1))
                i1.then_inc(TSA.h, 1)
                TSA.n += 1
                i2.then_inc(TSA.h, 1)
                TSA.n += 1

            @block.scalar
            def _(s, ci=ci, ob=ob):
                slot = ob % 4
                s.wait_ge(TSA.h, 2 * (ci * 8 + ob) + 2)
                if ob == 0 and ci >= 1:
                    s.wait_ge(VSA.h, 10 * ci)
                s.activation(sg_m[:, ob * 512:(ob + 1) * 512], pga[slot][:, :],
                             AF.Silu, bias=0.0,
                             scale=e_gs[:, ci:ci + 1]).then_inc(SSA.h, 1)
                SSA.n += 1

            @block.vector
            def _(v, ci=ci, ob=ob):
                slot = ob % 4
                v.wait_ge(SSA.h, SPC * ci + ob + 1)
                v.wait_ge(TSA.h, 2 * (ci * 8 + ob) + 2)
                v.scalar_tensor_tensor(h_m[:, ob * 512:(ob + 1) * 512],
                                       pua[slot][:, :], e_us[:, ci:ci + 1],
                                       sg_m[:, ob * 512:(ob + 1) * 512],
                                       ALU.mult, ALU.mult).then_inc(VSA.h, 1)
                VSA.n += 1

        @block.vector
        def _(v, ci=ci, cpar=cpar):
            v.wait_ge(VSA.h, 10 * ci + 8)
            v.drain()
            v.reduce_max(mx2_b[:, cpar:cpar + 1], h_m[:, :], AX.X,
                         apply_absolute_value=True)
            v.drain()
            v.tensor_scalar(mx2_b[:, cpar:cpar + 1], mx2_b[:, cpar:cpar + 1],
                            1e-5, None, ALU.max)
            v.drain()
            v.reciprocal(t2_b[:, cpar:cpar + 1], mx2_b[:, cpar:cpar + 1])
            v.drain()
            v.tensor_scalar(s4_b[:, cpar:cpar + 1], t2_b[:, cpar:cpar + 1],
                            127.0, None, ALU.mult)
            v.drain()
            v.tensor_scalar(t2_b[:, cpar:cpar + 1], mx2_b[:, cpar:cpar + 1],
                            1.0 / 127.0, None, ALU.mult)
            v.drain()
            v.tensor_scalar(e_ds[:, ci:ci + 1], t2_b[:, cpar:cpar + 1],
                            m_cols[:, 2:3], None, ALU.mult)
            v.engine_nop().then_inc(VSA.h, 1)
            VSA.n += 1

        if ROUND_MODE == "ladder":
            @block.scalar
            def _(s, ci=ci, cpar=cpar):
                s.wait_ge(VSA.h, 10 * ci + 9)
                s.activation(sg_m[:, :], h_m[:, :], AF.Identity,
                             bias=c_1275[:, :],
                             scale=s4_b[:, cpar:cpar + 1]).then_inc(SSA.h, 1)
                SSA.n += 1

            @block.vector
            def _(v, ci=ci, cpar=cpar):
                v.wait_ge(SSA.h, SPC * ci + 9)
                if ci >= 2:
                    v.wait_ge(HQS[cpar].h, 16 * (ci // 2))
                emit_round_ladder(v, sg_m[:, :], h_m[:, :], hq_m[cpar][:, :], 127.0)
                v.engine_nop().then_inc(VSA.h, 1)
                VSA.n += 1
        else:
            @block.vector
            def _(v, ci=ci, cpar=cpar):
                v.drain()
                if ci >= 2:
                    v.wait_ge(HQS[cpar].h, 16 * (ci // 2))
                v.tensor_scalar(sg_m[:, :], h_m[:, :],
                                s4_b[:, cpar:cpar + 1], MAGIC,
                                ALU.mult, ALU.add)
                v.drain()
                v.tensor_scalar(hq_m[cpar][:, :], sg_m[:, :], MAGIC, None,
                                ALU.subtract).then_inc(VSA.h, 1)
                VSA.n += 1

        @block.sync
        def _(sync, ci=ci, cpar=cpar, t0=t0):
            sync.wait_ge(VSA.h, 10 * ci + 10)
            dma(sync, HQS[cpar], hq_sp[t0:t0 + 128, :], hq_m[cpar][:, :])

        if ci % SUB == 0 and ci // SUB + 2 < NTILE:
            emit_xq_loads(ci // SUB + 2)

    @block.sync
    def _(sync):
        sync.wait_ge(HQS[0].h, HQS[0].n)
        sync.wait_ge(HQS[1].h, HQS[1].n)
        wait_all(sync, VS, SS, GS, TS_, DS, SD, GD)
        wait_all(sync, XQ[0], XQ[1], TSA, VSA, SSA)
        sync.nop().then_inc(PB.h, 1)
    PB.n += 1
    p3.close()

    p4 = ExitStack()
    wqd_sb = [p4.enter_context(nc.sbuf_tensor(f"wqd{k}", [128, D], BF16)) for k in range(HCH)]
    hqf = [[p4.enter_context(nc.sbuf_tensor(f"hqf{p}_{k}", [128, TILE], BF16))
            for k in range(HCH)] for p in range(2)]
    x2re = [p4.enter_context(nc.sbuf_tensor(f"x2re{p}", [128, SUB * D], F32))
            for p in range(2)]
    o_t = [p4.enter_context(nc.sbuf_tensor(f"o_t{p}", [128, SUB * D], F32))
           for p in range(2)]
    pdn = [p4.enter_context(nc.psum_tensor(f"pdn{p}", [128, 512], F32))
           for p in range(4)]
    HQ = [SemN("HQ0"), SemN("HQ1")]
    HS = [SemN("HS0"), SemN("HS1")]
    X2 = [SemN("X20"), SemN("X21")]
    OG = SemN("OG")

    vs0 = VS.n
    ts0 = TS_.n
    gs0 = GS.n

    @block.sync
    def _(sync):
        for k in range(HCH):
            dma(sync, DS, wqd_sb[k][:, :], wq_sp["d"][k * 128:(k + 1) * 128, :])
        sync.wait_ge(DS.h, DS.n)

    hq_after = {}
    hs_after = {}
    x2_after = {}

    def emit_loads(ti):
        par = ti % 2
        t0 = ti * TILE

        @block.sync
        def _(sync, ti=ti, par=par, t0=t0):
            if ti < 2:
                sync.wait_ge(PB.h, PB.n)
            else:
                sync.wait_ge(TS_.h, ts0 + 8 * (ti - 1))
                sync.wait_ge(VS.h, vs0 + 8 * (ti - 1))
            for k in range(0, HCH, 1):
                dma(sync, HQ[par], hqf[par][k][:, :],
                    hq_sp[t0:t0 + TILE, k * 128:(k + 1) * 128], transpose=True)
            hq_after[ti] = HQ[par].n

        @block.scalar
        def _(s, ti=ti, par=par, t0=t0):
            if ti < 2:
                s.wait_ge(PB.h, PB.n)
            else:
                s.wait_ge(VS.h, vs0 + 8 * (ti - 1))
            for ss in range(SUB):
                dma(s, X2[par], x2re[par][:, ss * D:(ss + 1) * D],
                    x2_sp[t0 + ss * 128:t0 + (ss + 1) * 128, :])
            x2_after[ti] = X2[par].n
            hs_after[ti] = HS[par].n

    emit_loads(0)
    emit_loads(1)

    for ti in range(NTILE):
        par = ti % 2
        t0 = ti * TILE

        for ss in range(SUB):
            G = ti * 8 + 2 * ss

            @block.tensor
            def _(t, ti=ti, par=par, ss=ss, G=G):
                if ss == 0:
                    t.wait_ge(HQ[par].h, hq_after[ti])
                    t.wait_ge(HS[par].h, hs_after[ti])
                if G >= 3:
                    t.wait_ge(VS.h, vs0 + G - 2)
                for k in range(HCH):
                    i1 = t.matmul(pdn[G % 4][:, :],
                                  hqf[par][k][:, ss * 128:(ss + 1) * 128],
                                  wqd_sb[k][:, 0:512],
                                  start=(k == 0), stop=(k == HCH - 1))
                    i2 = t.matmul(pdn[(G + 1) % 4][:, :],
                                  hqf[par][k][:, ss * 128:(ss + 1) * 128],
                                  wqd_sb[k][:, 512:1024],
                                  start=(k == 0), stop=(k == HCH - 1))
                i1.then_inc(TS_.h, 1)
                TS_.n += 1
                i2.then_inc(TS_.h, 1)
                TS_.n += 1

            for nh in range(2):
                ci = ti * SUB + ss

                @block.vector
                def _(v, ti=ti, par=par, ss=ss, nh=nh, G=G, ci=ci):
                    v.wait_ge(TS_.h, ts0 + G + nh + 1)
                    if ss == 0 and nh == 0:
                        v.wait_ge(X2[par].h, x2_after[ti])
                        if ti >= 2:
                            v.wait_ge(GS.h, gs0 + (ti - 1))
                    v.scalar_tensor_tensor(
                        o_t[par][:, ss * D + nh * 512:ss * D + (nh + 1) * 512],
                        pdn[(G + nh) % 4][:, :], e_ds[:, ci:ci + 1],
                        x2re[par][:, ss * D + nh * 512:ss * D + (nh + 1) * 512],
                        ALU.mult, ALU.add).then_inc(VS.h, 1)
                    VS.n += 1

        @block.gpsimd
        def _(g_, ti=ti, par=par, t0=t0):
            g_.wait_ge(VS.h, vs0 + 8 * (ti + 1))
            for ss in range(SUB):
                dma(g_, OG, outr[t0 + ss * 128:t0 + (ss + 1) * 128, :],
                    o_t[par][:, ss * D:(ss + 1) * D])
            g_.wait_ge(OG.h, OG.n)
            g_.engine_nop().then_inc(GS.h, 1)
            GS.n += 1

        if ti + 2 < NTILE:
            emit_loads(ti + 2)

    @block.sync
    def _(sync):
        sync.wait_ge(GS.h, gs0 + NTILE)

    @block.sync
    def _(sync):
        wait_all(sync, DS, SD, GD, VS, SS, GS, TS_)

    p4.close()
    es.close()
    return nc


_NC_CACHE = None


def _get_nc():
    global _NC_CACHE
    if _NC_CACHE is None:
        _NC_CACHE = build_program()
    return _NC_CACHE


def kernel(**inputs):
    n_cores = 8
    x_full = np.ascontiguousarray(np.asarray(inputs["L0_attn_x"]))
    wgT = np.ascontiguousarray(np.asarray(inputs["L0_ffn_w_gate"]).T)
    wuT = np.ascontiguousarray(np.asarray(inputs["L0_ffn_w_up"]).T)
    wdT = np.ascontiguousarray(np.asarray(inputs["L0_ffn_w_down"]).T)
    common = {
        "Wg": np.ascontiguousarray(inputs["L0_attn_Wg"]),
        "Wi": np.ascontiguousarray(inputs["L0_attn_Wi"]),
        "Wfg": np.ascontiguousarray(inputs["L0_attn_Wfg"]),
        "Wfu": np.ascontiguousarray(inputs["L0_attn_Wfu"]),
        "Wfd": np.ascontiguousarray(inputs["L0_attn_Wfd"]),
        "Wo": np.ascontiguousarray(inputs["L0_attn_Wo"]),
        "bg": np.ascontiguousarray(inputs["L0_attn_bg"]),
        "bi": np.ascontiguousarray(inputs["L0_attn_bi"]),
        "A": np.ascontiguousarray(inputs["L0_attn_A"]),
        "Bp": np.ascontiguousarray(inputs["L0_attn_Bp"]),
        "Cp": np.ascontiguousarray(inputs["L0_attn_Cp"]),
        "Dp": np.ascontiguousarray(inputs["L0_attn_Dp"]),
        "gamma": np.ascontiguousarray(inputs["L0_attn_gamma"]),
        "beta": np.ascontiguousarray(inputs["L0_attn_beta"]),
        "bfg": np.ascontiguousarray(inputs["L0_attn_bfg"]),
        "bfu": np.ascontiguousarray(inputs["L0_attn_bfu"]),
        "bfd": np.ascontiguousarray(inputs["L0_attn_bfd"]),
        "bo": np.ascontiguousarray(inputs["L0_attn_bo"]),
        "wgT": wgT, "wuT": wuT, "wdT": wdT,
    }
    in_maps = []
    for i in range(n_cores):
        m = dict(common)
        m["x"] = np.ascontiguousarray(x_full[2 * i:2 * i + 2])
        in_maps.append(m)
    nc = _get_nc()
    res = run_bass_kernel_spmd(nc, in_maps, core_ids=list(range(n_cores)))
    return np.concatenate([np.asarray(r["out"]) for r in res.results], axis=0)


# version 25
# speedup vs baseline: 1.0331x; 1.0331x over previous
import numpy as np
from contextlib import ExitStack

import concourse.bass as bass
import concourse.mybir as mybir
from concourse.bass_utils import run_bass_kernel_spmd

import concourse.bass_utils as _bu

if not getattr(_bu, "_ldw_patch", False):
    _orig_run_command = _bu.run_command

    def _run_command_ldw(argv, **kwargs):
        argv = ["--enable-ldw-opt=true" if a == "--enable-ldw-opt=false" else a
                for a in argv]
        return _orig_run_command(argv, **kwargs)

    _bu.run_command = _run_command_ldw
    _bu._ldw_patch = True


F32 = mybir.dt.float32
BF16 = mybir.dt.bfloat16
AF = mybir.ActivationFunctionType
ALU = mybir.AluOpType
AX = mybir.AxisListType

B = 2
T = 2048
NT = B * T
D = 1024
H = 4096
DCH = D // 128
HCH = H // 128
NCHUNK = NT // 128
TILE = 512
NTILE = NT // TILE
SUB = TILE // 128


class Sem:
    def __init__(self, nc, name):
        self.h = nc.semaphore(name)
        self.n = 0

    def __enter__(self):
        self.h = self.h.__enter__()
        return self

    def __exit__(self, *a):
        return False


def build_program():
    nc = bass.Bass("TRN2", target_bir_lowering=False)

    x_d = nc.dram_tensor("x", [B, T, D], F32, kind="ExternalInput")
    out_d = nc.dram_tensor("out", [B, T, D], F32, kind="ExternalOutput")
    wnames = ["Wg", "Wi", "Wfg", "Wfu", "Wfd", "Wo"]
    w_d = {n: nc.dram_tensor(n, [D, D], F32, kind="ExternalInput") for n in wnames}
    vnames = ["bg", "bi", "A", "Bp", "Cp", "Dp", "gamma", "beta",
              "bfg", "bfu", "bfd", "bo"]
    v_d = {n: nc.dram_tensor(n, [D], F32, kind="ExternalInput") for n in vnames}
    wgT_d = nc.dram_tensor("wgT", [D, H], F32, kind="ExternalInput")
    wuT_d = nc.dram_tensor("wuT", [D, H], F32, kind="ExternalInput")
    wdT_d = nc.dram_tensor("wdT", [H, D], F32, kind="ExternalInput")

    xn_sp = nc.dram_tensor("xn_sp", [NT, D], BF16)
    mean_sp = nc.dram_tensor("mean_sp", [3, 128], F32)
    yn_sp = nc.dram_tensor("yn_sp", [D, NT], BF16)
    x2_sp = nc.dram_tensor("x2_sp", [NT, D], F32)
    xq_sp = nc.dram_tensor("xq_sp", [NT, D], BF16)
    hq_sp = nc.dram_tensor("hq_sp", [NT, H], BF16)
    wq_sp = {"g": nc.dram_tensor("wqg_sp", [D, H], BF16),
             "u": nc.dram_tensor("wqu_sp", [D, H], BF16),
             "d": nc.dram_tensor("wqd_sp", [H, D], BF16)}

    es = ExitStack()
    Sb = lambda name, shape, dt=F32: es.enter_context(nc.sbuf_tensor(name, shape, dt))
    Ps = lambda name, shape: es.enter_context(nc.psum_tensor(name, shape, F32))
    SemN = lambda name: es.enter_context(Sem(nc, name))

    cols = {n: Sb("c_" + n, [128, DCH]) for n in vnames}
    cpbp = Sb("c_cpbp", [128, DCH])
    bfgp = Sb("c_bfgp", [128, DCH])
    bfup = Sb("c_bfup", [128, DCH])
    onescol_f = Sb("onescol_f", [128, 1])
    onescol_b = Sb("onescol_b", [128, 1], BF16)
    ones_k1 = Sb("ones_k1", [1, 128])
    ident = Sb("ident", [128, 128])
    iota_f = Sb("iota_f", [128, 128])
    pidx = Sb("pidx", [128, 1])
    bo_bc = Sb("bo_bc", [128, D])
    cs127d = Sb("cs127d", [128, D])
    m_cols = Sb("m_cols", [128, 6])
    c_eps5 = Sb("c_eps5", [128, 1])
    c_eps3 = Sb("c_eps3", [128, 1])
    c_1275 = Sb("c_1275", [128, 1])
    c_magic = Sb("c_magic", [128, 1])
    e_gs = Sb("e_gs", [128, NCHUNK])
    e_us = Sb("e_us", [128, NCHUNK])
    e_ds = Sb("e_ds", [128, NCHUNK])
    s3_cols = Sb("s3_cols", [128, NCHUNK])

    PB = SemN("PB")
    DS = SemN("DS")
    SD = SemN("SD")
    GD = SemN("GD")
    VS = SemN("VS")
    SS = SemN("SS")
    GS = SemN("GS")
    TS_ = SemN("TS")

    block = es.enter_context(nc.Block())

    def fence(eng, sem):
        eng.drain().then_inc(sem.h, 1)
        sem.n += 1

    def dma(eng, sem, dst, src, transpose=False, slow=False):
        if transpose:
            eng.dma_start(dst, src, transpose=True).then_inc(sem.h, 16)
        elif slow:
            eng.dma_start(dst, src,
                          allow_slow_non_contiguous=True).then_inc(sem.h, 16)
        else:
            eng.dma_start(dst, src).then_inc(sem.h, 16)
        sem.n += 16

    def wait_all(eng, *sems):
        for s in sems:
            eng.wait_ge(s.h, s.n)

    ROUND_MODE = "magic"
    MAGIC = 12582912.0

    def emit_round_ladder(v, t_ap, scr_ap, out_ap, out_shift):
        v.tensor_copy(scr_ap, t_ap)
        for k in range(7, -1, -1):
            c = float(1 << k)
            v.drain()
            v.tensor_scalar(out_ap, scr_ap, c, None, ALU.is_ge)
            v.drain()
            v.scalar_tensor_tensor(scr_ap, out_ap, -c, scr_ap, ALU.mult, ALU.add)
        v.drain()
        v.tensor_tensor(scr_ap, t_ap, scr_ap, ALU.subtract)
        v.drain()
        v.tensor_scalar(out_ap, scr_ap, float(out_shift), None, ALU.subtract)

    xr = x_d[:, :, :].rearrange("b t d -> (b t) d")
    outr = out_d[:, :, :].rearrange("b t d -> (b t) d")

    @block.sync
    def _(sync):
        for n in vnames:
            dma(sync, DS, cols[n][:, :],
                v_d[n][:].rearrange("(c p) -> p c", p=128), slow=True)
        sync.wait_ge(DS.h, DS.n)

    @block.gpsimd
    def _(g):
        g.iota(iota_f[:], [[1, 128]], channel_multiplier=0,
               allow_small_or_imprecise_dtypes=True)
        g.iota(pidx[:], [[0, 1]], channel_multiplier=1,
               allow_small_or_imprecise_dtypes=True)
        g.memset(onescol_f[:], 1.0)
        g.memset(onescol_b[:], 1.0)
        g.memset(ones_k1[:], 1.0)
        g.memset(c_eps5[:], 1e-5)
        g.memset(c_eps3[:], 1e-3)
        g.memset(c_1275[:], 127.5)
        g.memset(c_magic[:], 12582912.0)
        g.drain()
        g.tensor_scalar(ident[:], iota_f[:], pidx[:], None, ALU.is_equal)
        fence(g, GS)

    @block.vector
    def _(v):
        wait_all(v, DS)
        v.tensor_tensor(cpbp[:], cols["Cp"][:], cols["Bp"][:], ALU.mult)
        fence(v, VS)

    bo_row = Sb("bo_row", [1, D])

    @block.sync
    def _(sync):
        dma(sync, DS, bo_row[:, :], v_d["bo"][:].rearrange("(o d) -> o d", o=1))
        sync.wait_ge(DS.h, DS.n)

    pC = ExitStack()
    pbc = pC.enter_context(nc.psum_tensor("pbc", [128, 512], F32))

    @block.tensor
    def _(t):
        wait_all(t, DS, GS)
        t.matmul(pbc[:, :], ones_k1[:, :], bo_row[:, 0:512],
                 start=True, stop=True).then_inc(TS_.h, 1)
        TS_.n += 1

    @block.vector
    def _(v):
        wait_all(v, TS_)
        v.tensor_copy(bo_bc[:, 0:512], pbc[:, :]).then_inc(VS.h, 1)
        VS.n += 1

    @block.tensor
    def _(t):
        wait_all(t, VS)
        t.matmul(pbc[:, :], ones_k1[:, :], bo_row[:, 512:1024],
                 start=True, stop=True).then_inc(TS_.h, 1)
        TS_.n += 1

    @block.vector
    def _(v):
        wait_all(v, TS_)
        v.tensor_copy(bo_bc[:, 512:1024], pbc[:, :])
        fence(v, VS)

    pC.close()

    @block.sync
    def _(sync):
        wait_all(sync, VS, SS, GS, TS_, DS, SD, GD)
        sync.nop().then_inc(PB.h, 1)
    PB.n += 1

    p1 = ExitStack()
    wg_big = p1.enter_context(nc.sbuf_tensor("wg_big", [128, DCH * D], BF16))
    wi_big = p1.enter_context(nc.sbuf_tensor("wi_big", [128, DCH * D], BF16))
    wg_sb = [wg_big[:, k * D:(k + 1) * D] for k in range(DCH)]
    wi_sb = [wi_big[:, k * D:(k + 1) * D] for k in range(DCH)]
    a_bc = [bass.AP(cols["A"].tensor if hasattr(cols["A"], "tensor") else cols["A"],
                    c, [[DCH, 128], [0, TILE]]) for c in range(DCH)]
    x_t = [p1.enter_context(nc.sbuf_tensor(f"x_t{p}", [128, SUB * D], F32))
           for p in range(2)]
    xn_tm = p1.enter_context(nc.sbuf_tensor("xn_tm", [128, SUB * D], BF16))
    xn_fm = [[p1.enter_context(nc.sbuf_tensor(f"xnfm{p}_{c}", [128, TILE], BF16))
              for c in range(DCH)] for p in range(2)]
    gate_t = [p1.enter_context(nc.sbuf_tensor(f"gate{c}", [128, TILE], F32)) for c in range(DCH)]
    ui_t = [p1.enter_context(nc.sbuf_tensor(f"ui{c}", [128, TILE], F32)) for c in range(DCH)]
    yc_t = [p1.enter_context(nc.sbuf_tensor(f"yc{c}", [128, TILE], F32)) for c in range(DCH)]
    y_t = [p1.enter_context(nc.sbuf_tensor(f"y{c}", [128, TILE], F32)) for c in range(DCH)]
    ysq_t = [p1.enter_context(nc.sbuf_tensor(f"ysq{c}", [128, TILE], BF16)) for c in range(DCH)]
    yn_t = [p1.enter_context(nc.sbuf_tensor(f"yn{c}", [128, TILE], BF16)) for c in range(DCH)]
    sq_scr = p1.enter_context(nc.sbuf_tensor("sq_scr", [128, D], F32))
    st_s1 = p1.enter_context(nc.sbuf_tensor("st_s1", [128, SUB], F32))
    st_s2 = p1.enter_context(nc.sbuf_tensor("st_s2", [128, SUB], F32))
    st_mu = p1.enter_context(nc.sbuf_tensor("st_mu", [128, SUB], F32))
    st_t1 = p1.enter_context(nc.sbuf_tensor("st_t1", [128, SUB], F32))
    st_var = p1.enter_context(nc.sbuf_tensor("st_var", [128, SUB], F32))
    st_r = p1.enter_context(nc.sbuf_tensor("st_r", [128, SUB], F32))
    st_nmr = p1.enter_context(nc.sbuf_tensor("st_nmr", [128, SUB], F32))
    carry = p1.enter_context(nc.sbuf_tensor("carry", [128, DCH], F32))
    r2_sc = p1.enter_context(nc.sbuf_tensor("r2_sc", [1, 4 * TILE], F32))
    p0w = [p1.enter_context(nc.sbuf_tensor(f"p0w{p}", [128, 512], F32)) for p in range(3)]
    p0q = [p1.enter_context(nc.sbuf_tensor(f"p0q{p}", [128, 512], BF16)) for p in range(2)]
    p0c1 = p1.enter_context(nc.sbuf_tensor("p0c1", [128, 512], F32))
    p0c2 = p1.enter_context(nc.sbuf_tensor("p0c2", [128, 512], F32))
    p0stat = p1.enter_context(nc.sbuf_tensor("p0stat", [128, 64], F32))
    p0row = p1.enter_context(nc.sbuf_tensor("p0row", [1, 128], F32))
    p0thr = p1.enter_context(nc.sbuf_tensor("p0thr", [128, 2], F32))
    P0L = [SemN(f"P0L{i}") for i in range(3)]
    P0V = SemN("P0V")
    P0G = SemN("P0G")
    P0SP = [SemN("P0SPa"), SemN("P0SPb")]
    P0D = SemN("P0D")
    P0M = SemN("P0M")
    pg = [p1.enter_context(nc.psum_tensor(f"pg{p}", [128, TILE], F32)) for p in range(2)]
    pi_ = [p1.enter_context(nc.psum_tensor(f"pi{p}", [128, TILE], F32)) for p in range(2)]
    WL = [SemN("WL0"), SemN("WL1")]
    XL = [SemN("XL0"), SemN("XL1")]
    XNT = [SemN("XNT0"), SemN("XNT1")]
    XNS = SemN("XNS")
    YNS = SemN("YNS")
    TSC = SemN("TSC")
    VSC = SemN("VSC")
    SSC = SemN("SSC")
    GSC = SemN("GSC")
    ps1r = p1.enter_context(nc.psum_tensor("ps1r", [1, TILE], F32))
    ps2r = p1.enter_context(nc.psum_tensor("ps2r", [1, TILE], F32))
    pR = p1.enter_context(nc.psum_tensor("pR", [128, TILE], F32))
    pNM = p1.enter_context(nc.psum_tensor("pNM", [128, TILE], F32))

    x_after = {}

    def emit_x_loads(ti):
        par = ti % 2
        t0 = ti * TILE

        @block.scalar
        def _(s, ti=ti, par=par, t0=t0):
            if ti < 2:
                s.wait_ge(PB.h, PB.n)
            else:
                s.wait_ge(SSC.h, 13 * (ti - 2) + 3)
            for ss in range(SUB):
                dma(s, XL[par], x_t[par][:, ss * D:(ss + 1) * D],
                    xr[t0 + ss * 128:t0 + (ss + 1) * 128, :])
            x_after[ti] = XL[par].n

    emit_x_loads(0)
    emit_x_loads(1)

    @block.gpsimd
    def _(g):
        g.wait_ge(PB.h, PB.n)
        g.memset(carry[:, :], 0.0)
        for k in range(DCH):
            dma(g, WL[0], wg_sb[k][:, :], w_d["Wg"][k * 128:(k + 1) * 128, :])
        for k in range(DCH):
            dma(g, WL[0], wi_sb[k][:, :], w_d["Wi"][k * 128:(k + 1) * 128, :])

    p0_mats = [("g", wgT_d, D // 128, 8, 0), ("u", wuT_d, D // 128, 8, 1),
               ("d", wdT_d, H // 128, 2, 2)]
    p0_tasks = []
    for mname, wdram, nrt, ncc, midx in p0_mats:
        inv_size = 1.0 / float(nrt * 128 * ncc * 512)
        for c in range(nrt * ncc):
            p0_tasks.append(("l1", mname, wdram, nrt, ncc, midx, c, inv_size))
        p0_tasks.append(("mean", mname, wdram, nrt, ncc, midx, None, inv_size))
        for c in range(nrt * ncc):
            p0_tasks.append(("l2", mname, wdram, nrt, ncc, midx, c, inv_size))
        p0_tasks.append(("fin", mname, wdram, nrt, ncc, midx, None, inv_size))
    p0_state = {"i": 0, "nl": 0, "nsp": 0,
                "slot_free": [("none", 0)] * 3}

    def _p0_wait(eng, tag_val):
        tag, val = tag_val
        if tag == "P0V":
            eng.wait_ge(P0V.h, val)

    def emit_p0(n_tasks):
        i0 = p0_state["i"]
        for task in p0_tasks[i0:i0 + n_tasks]:
            kind, mname, wdram, nrt, ncc, midx, c, inv_size = task
            if kind in ("l1", "l2"):
                rt, cc = divmod(c, ncc)
                nl = p0_state["nl"]
                lp = nl % 3
                freev = p0_state["slot_free"][lp]

                @block.scalar
                def _(s, nl=nl, lp=lp, wdram=wdram, rt=rt, cc=cc, freev=freev):
                    if nl == 0:
                        s.wait_ge(PB.h, 1)
                    _p0_wait(s, freev)
                    dma(s, P0L[lp], p0w[lp][:, :],
                        wdram[rt * 128:(rt + 1) * 128, cc * 512:(cc + 1) * 512])
                p0_state["nl"] += 1

                if kind == "l1":
                    @block.vector
                    def _(v, c=c, lp=lp, lv=P0L[lp].n):
                        v.wait_ge(P0L[lp].h, lv)
                        v.reduce_sum(p0stat[:, c:c + 1], p0w[lp][:, :], AX.X,
                                     apply_absolute_value=True)
                        v.engine_nop().then_inc(P0V.h, 1)
                        P0V.n += 1
                    p0_state["slot_free"][lp] = ("P0V", P0V.n)
                else:
                    nsp = p0_state["nsp"]

                    @block.vector
                    def _(v, lp=lp, midx=midx, lv=P0L[lp].n, nsp=nsp):
                        v.wait_ge(P0L[lp].h, lv)
                        if True:
                            v.wait_ge(P0M.h, midx + 1)
                        if nsp >= 2:
                            v.wait_ge(P0SP[nsp % 2].h, 16 * (nsp // 2))
                        c2b = [p0c2, p0c1][nsp % 2]
                        v.tensor_scalar(c2b[:, :], p0w[lp][:, :],
                                        p0thr[:, 1:2], None, ALU.is_le)
                        v.drain()
                        v.scalar_tensor_tensor(p0q[nsp % 2][:, :], p0w[lp][:, :],
                                               p0thr[:, 0:1], c2b[:, :],
                                               ALU.is_ge, ALU.subtract)
                        v.engine_nop().then_inc(P0V.h, 1)
                        P0V.n += 1
                    p0_state["slot_free"][lp] = ("P0V", P0V.n)

                    @block.gpsimd
                    def _(g, mname=mname, rt=rt, cc=cc, nsp=nsp, nv=P0V.n):
                        g.wait_ge(P0V.h, nv)
                        dma(g, P0SP[nsp % 2],
                            wq_sp[mname][rt * 128:(rt + 1) * 128,
                                         cc * 512:(cc + 1) * 512],
                            p0q[nsp % 2][:, :])
                    p0_state["nsp"] += 1
            elif kind == "mean":
                nch = nrt * ncc

                @block.vector
                def _(v, nch=nch, nv=P0V.n):
                    v.wait_ge(P0V.h, nv)
                    v.reduce_sum(p0stat[:, 63:64], p0stat[:, 0:nch], AX.X)
                    v.engine_nop().then_inc(P0V.h, 1)
                    P0V.n += 1

                @block.gpsimd
                def _(g, midx=midx, nv=P0V.n):
                    g.wait_ge(P0V.h, nv)
                    dma(g, P0D, mean_sp[midx:midx + 1, :],
                        p0stat[:, 63:64], slow=True)
                    g.wait_ge(P0D.h, P0D.n)
                    dma(g, P0D, p0row[:, :], mean_sp[midx:midx + 1, :])
                    g.wait_ge(P0D.h, P0D.n)
                    g.engine_nop().then_inc(P0G.h, 1)
                    P0G.n += 1

                @block.vector
                def _(v, midx=midx, inv_size=inv_size, gv=P0G.n):
                    v.wait_ge(P0G.h, gv)
                    v.reduce_sum(p0stat[0:1, 0:8],
                                 p0row[:, :].rearrange("o (a b) -> o a b", b=16),
                                 AX.X)
                    v.drain()
                    v.reduce_sum(p0stat[0:1, 8:9], p0stat[0:1, 0:8], AX.X)
                    v.drain()
                    v.tensor_scalar(p0stat[0:1, 9:10], p0stat[0:1, 8:9],
                                    inv_size, 1e-5, ALU.mult, ALU.max)
                    v.engine_nop().then_inc(P0V.h, 1)
                    P0V.n += 1

                @block.gpsimd
                def _(g, midx=midx, nv=P0V.n):
                    g.wait_ge(P0V.h, nv)
                    dma(g, P0D, mean_sp[midx:midx + 1, 0:1], p0stat[0:1, 9:10])
                    g.wait_ge(P0D.h, P0D.n)
                    dma(g, P0D, m_cols[:, midx:midx + 1],
                        bass.AP(mean_sp, midx * 128, [[0, 128], [1, 1]]))
                    g.wait_ge(P0D.h, P0D.n)
                    g.engine_nop().then_inc(P0G.h, 1)
                    P0G.n += 1

                @block.vector
                def _(v, midx=midx, gv=P0G.n):
                    v.wait_ge(P0G.h, gv)
                    v.tensor_scalar(p0thr[:, 0:1], m_cols[:, midx:midx + 1],
                                    0.5, None, ALU.mult)
                    v.tensor_scalar(p0thr[:, 1:2], m_cols[:, midx:midx + 1],
                                    -0.5, None, ALU.mult)
                    v.engine_nop().then_inc(P0M.h, 1)
                    P0M.n += 1
            else:
                @block.vector
                def _(v, a=P0SP[0].n, b=P0SP[1].n):
                    v.wait_ge(P0SP[0].h, a)
                    v.wait_ge(P0SP[1].h, b)
                    v.engine_nop().then_inc(P0V.h, 1)
                    P0V.n += 1
        p0_state["i"] = i0 + n_tasks

    P0_PER_TILE = (len(p0_tasks) + NTILE - 1) // NTILE

    for ti in range(NTILE):
        t0 = ti * TILE
        par = ti % 2
        batch_start = (t0 % T) == 0

        @block.vector
        def _(v, ti=ti, par=par):
            v.wait_ge(XL[par].h, x_after[ti])
            for ss in range(SUB):
                v.reduce_sum(st_s1[:, ss:ss + 1],
                             x_t[par][:, ss * D:(ss + 1) * D], AX.X)
            fence(v, VSC)

        @block.scalar
        def _(s, ti=ti, par=par):
            s.wait_ge(XL[par].h, x_after[ti])
            for ss in range(SUB):
                if ss:
                    s.drain()
                s.activation(sq_scr[:, :], x_t[par][:, ss * D:(ss + 1) * D],
                             AF.Square, bias=0.0, scale=1.0,
                             accum_out=st_s2[:, ss:ss + 1])
            fence(s, SSC)

        @block.vector
        def _(v, ti=ti):
            v.wait_ge(SSC.h, 13 * ti + 1)
            v.tensor_scalar(st_mu[:, :], st_s1[:, :], 1.0 / D, None, ALU.mult)
            v.drain()
            v.scalar_tensor_tensor(st_t1[:, :], st_mu[:, :], -1.0, st_mu[:, :],
                                   ALU.mult, ALU.mult)
            v.drain()
            v.scalar_tensor_tensor(st_var[:, :], st_s2[:, :], 1.0 / D, st_t1[:, :],
                                   ALU.mult, ALU.add)
            fence(v, VSC)

        @block.scalar
        def _(s, ti=ti):
            s.wait_ge(VSC.h, 15 * ti + 2)
            s.activation(st_t1[:, :], st_var[:, :], AF.Sqrt, bias=c_eps5[:, :],
                         scale=1.0)
            fence(s, SSC)

        @block.vector
        def _(v, ti=ti):
            v.wait_ge(SSC.h, 13 * ti + 2)
            v.reciprocal(st_r[:, :], st_t1[:, :])
            v.drain()
            v.scalar_tensor_tensor(st_nmr[:, :], st_mu[:, :], -1.0, st_r[:, :],
                                   ALU.mult, ALU.mult)
            fence(v, VSC)

        @block.scalar
        def _(s, ti=ti, par=par):
            s.wait_ge(VSC.h, 15 * ti + 3)
            s.wait_ge(XNS.h, 64 * ti)
            for ss in range(SUB):
                s.activation(xn_tm[:, ss * D:(ss + 1) * D],
                             x_t[par][:, ss * D:(ss + 1) * D], AF.Identity,
                             bias=st_nmr[:, ss:ss + 1], scale=st_r[:, ss:ss + 1])
            fence(s, SSC)

        @block.sync
        def _(sync, ti=ti, par=par, t0=t0):
            sync.wait_ge(SSC.h, 13 * ti + 3)
            for ss in range(SUB):
                dma(sync, XNS, xn_sp[t0 + ss * 128:t0 + (ss + 1) * 128, :],
                    xn_tm[:, ss * D:(ss + 1) * D])
            sync.wait_ge(XNS.h, XNS.n)
            if ti >= 2:
                sync.wait_ge(TSC.h, 20 * (ti - 1))
            for c in range(DCH):
                dma(sync, XNT[par], xn_fm[par][c][:, :],
                    xn_sp[t0:t0 + TILE, c * 128:(c + 1) * 128], transpose=True)

        for m in range(DCH):
            @block.tensor
            def _(t, ti=ti, par=par, m=m):
                if m == 0:
                    t.wait_ge(XNT[par].h, XNT[par].n)
                    if ti == 0:
                        t.wait_ge(WL[0].h, 256)
                if m >= 2:
                    t.wait_ge(SSC.h, 13 * ti + (m - 2) + 4)
                    t.wait_ge(VSC.h, 15 * ti + (m - 2) + 4)
                elif ti >= 1:
                    t.wait_ge(SSC.h, 13 * (ti - 1) + (m + 6) + 4)
                    t.wait_ge(VSC.h, 15 * (ti - 1) + (m + 6) + 4)
                for k in range(DCH):
                    i1 = t.matmul(pg[m % 2][:, :],
                                  wg_sb[k][:, m * 128:(m + 1) * 128],
                                  xn_fm[par][k][:, :],
                                  start=(k == 0), stop=(k == DCH - 1))
                i1.then_inc(TSC.h, 1)
                TSC.n += 1
                for k in range(DCH):
                    i2 = t.matmul(pi_[m % 2][:, :],
                                  wi_sb[k][:, m * 128:(m + 1) * 128],
                                  xn_fm[par][k][:, :],
                                  start=(k == 0), stop=(k == DCH - 1))
                i2.then_inc(TSC.h, 1)
                TSC.n += 1

            @block.scalar
            def _(s, ti=ti, m=m):
                s.wait_ge(TSC.h, 20 * ti + 2 * m + 1)
                if m == 0 and ti >= 1:
                    s.wait_ge(VSC.h, 15 * (ti - 1) + 12)
                s.activation(gate_t[m][:, :], pg[m % 2][:, :], AF.Silu,
                             bias=cols["bg"][:, m:m + 1],
                             scale=1.0).then_inc(SSC.h, 1)
                SSC.n += 1

            @block.vector
            def _(v, ti=ti, m=m):
                v.wait_ge(TSC.h, 20 * ti + 2 * m + 2)
                v.tensor_scalar(ui_t[m][:, :], pi_[m % 2][:, :],
                                cols["bi"][:, m:m + 1], None,
                                ALU.add).then_inc(VSC.h, 1)
                VSC.n += 1

        emit_p0(P0_PER_TILE // 3)

        @block.vector
        def _(v, ti=ti, batch_start=batch_start):
            v.drain()
            v.wait_ge(SSC.h, 13 * ti + 11)
            for m in range(DCH):
                v.tensor_tensor(gate_t[m][:, :], gate_t[m][:, :], ui_t[m][:, :],
                                ALU.mult)
            v.drain()
            for m in range(DCH):
                v.tensor_scalar(ui_t[m][:, :], gate_t[m][:, :],
                                cpbp[:, m:m + 1], None, ALU.mult)
            v.drain()
            if batch_start:
                v.memset(carry[:, :], 0.0)
                v.drain()
            if ti == 0:
                v.wait_ge(GSC.h, 1) if False else None
            for m in range(DCH):
                v.tensor_tensor_scan(yc_t[m][:, :], a_bc[m], ui_t[m][:, :],
                                     carry[:, m:m + 1], ALU.mult, ALU.add)
            v.drain()
            for m in range(DCH):
                v.tensor_copy(carry[:, m:m + 1], yc_t[m][:, TILE - 1:TILE])
            v.drain()
            for m in range(DCH):
                v.scalar_tensor_tensor(y_t[m][:, :], gate_t[m][:, :],
                                       cols["Dp"][:, m:m + 1], yc_t[m][:, :],
                                       ALU.mult, ALU.add)
            fence(v, VSC)

        emit_p0(P0_PER_TILE // 3)

        @block.scalar
        def _(s, ti=ti):
            s.wait_ge(VSC.h, 15 * ti + 12)
            if ti >= 1:
                s.wait_ge(TSC.h, 20 * (ti - 1) + 18)
            for m in range(DCH):
                s.activation(ysq_t[m][:, :], y_t[m][:, :], AF.Square,
                             bias=0.0, scale=1.0)
            fence(s, SSC)

        @block.tensor
        def _(t, ti=ti):
            t.wait_ge(VSC.h, 15 * ti + 12)
            if ti >= 1:
                t.wait_ge(VSC.h, 15 * (ti - 1) + 13)
            for m in range(DCH):
                i1 = t.matmul(ps1r[:, :], onescol_f[:, :], y_t[m][:, :],
                              start=(m == 0), stop=(m == DCH - 1))
            i1.then_inc(TSC.h, 1)
            TSC.n += 1

        @block.tensor
        def _(t, ti=ti):
            t.wait_ge(SSC.h, 13 * ti + 12)
            for m in range(DCH):
                i2 = t.matmul(ps2r[:, :], onescol_b[:, :], ysq_t[m][:, :],
                              start=(m == 0), stop=(m == DCH - 1))
            i2.then_inc(TSC.h, 1)
            TSC.n += 1

        @block.vector
        def _(v, ti=ti):
            v.wait_ge(TSC.h, 20 * ti + 18)
            v.tensor_scalar(r2_sc[:, 0:TILE], ps1r[:, :], 1.0 / D, None, ALU.mult)
            v.tensor_scalar(r2_sc[:, TILE:2 * TILE], ps2r[:, :], 1.0 / D, None,
                            ALU.mult)
            v.drain()
            v.tensor_tensor(r2_sc[:, 2 * TILE:3 * TILE], r2_sc[:, 0:TILE],
                            r2_sc[:, 0:TILE], ALU.mult)
            v.drain()
            v.tensor_tensor(r2_sc[:, 3 * TILE:4 * TILE], r2_sc[:, TILE:2 * TILE],
                            r2_sc[:, 2 * TILE:3 * TILE], ALU.subtract)
            fence(v, VSC)

        @block.scalar
        def _(s, ti=ti):
            s.wait_ge(VSC.h, 15 * ti + 13)
            s.activation(r2_sc[:, 2 * TILE:3 * TILE], r2_sc[:, 3 * TILE:4 * TILE],
                         AF.Sqrt, bias=c_eps3[0:1, :], scale=1.0)
            fence(s, SSC)

        @block.vector
        def _(v, ti=ti):
            v.wait_ge(SSC.h, 13 * ti + 13)
            v.reciprocal(r2_sc[:, TILE:2 * TILE], r2_sc[:, 2 * TILE:3 * TILE])
            v.drain()
            v.scalar_tensor_tensor(r2_sc[:, 2 * TILE:3 * TILE], r2_sc[:, 0:TILE],
                                   -1.0, r2_sc[:, TILE:2 * TILE],
                                   ALU.mult, ALU.mult)
            fence(v, VSC)

        @block.tensor
        def _(t, ti=ti):
            t.wait_ge(VSC.h, 15 * ti + 14)
            if ti >= 1:
                t.wait_ge(VSC.h, 15 * (ti - 1) + 15)
            i1 = t.matmul(pR[:, :], ones_k1[:, :], r2_sc[:, TILE:2 * TILE],
                          start=True, stop=True)
            i1.then_inc(TSC.h, 1)
            TSC.n += 1
            i2 = t.matmul(pNM[:, :], ones_k1[:, :], r2_sc[:, 2 * TILE:3 * TILE],
                          start=True, stop=True)
            i2.then_inc(TSC.h, 1)
            TSC.n += 1

        @block.vector
        def _(v, ti=ti):
            v.wait_ge(TSC.h, 20 * ti + 20)
            v.wait_ge(YNS.h, 128 * ti)
            for m in range(DCH):
                v.scalar_tensor_tensor(yc_t[m][:, :], y_t[m][:, :], 1.0,
                                       pR[:, :], ALU.mult, ALU.mult)
            v.drain()
            for m in range(DCH):
                v.tensor_tensor(yn_t[m][:, :], yc_t[m][:, :], pNM[:, :], ALU.add)
            fence(v, VSC)

        @block.sync
        def _(sync, ti=ti, t0=t0):
            sync.wait_ge(VSC.h, 15 * ti + 15)
            for m in range(DCH):
                dma(sync, YNS, yn_sp[m * 128:(m + 1) * 128, t0:t0 + TILE],
                    yn_t[m][:, :])

        if ti + 2 < NTILE:
            emit_x_loads(ti + 2)
        emit_p0(P0_PER_TILE - 2 * (P0_PER_TILE // 3))

    @block.sync
    def _(sync):
        wait_all(sync, VS, SS, GS, TS_, DS, SD, GD)
        wait_all(sync, WL[0], WL[1], XL[0], XL[1], XNT[0], XNT[1], XNS, YNS,
                 TSC, VSC, SSC, GSC)
        wait_all(sync, P0L[0], P0L[1], P0L[2], P0V, P0G, P0SP[0], P0SP[1], P0D, P0M)
        sync.nop().then_inc(PB.h, 1)
    PB.n += 1
    p1.close()

    p2 = ExitStack()
    wfg_sb = [p2.enter_context(nc.sbuf_tensor(f"wfg{k}", [128, D], BF16)) for k in range(DCH)]
    wfu_sb = [p2.enter_context(nc.sbuf_tensor(f"wfu{k}", [128, D], BF16)) for k in range(DCH)]
    wfd_sb = [p2.enter_context(nc.sbuf_tensor(f"wfd{k}", [128, D], BF16)) for k in range(DCH)]
    wo_sb = [p2.enter_context(nc.sbuf_tensor(f"wo{k}", [128, D], BF16)) for k in range(DCH)]
    wf32b = [p2.enter_context(nc.sbuf_tensor(f"wf32b{p}", [128, D], F32))
             for p in range(2)]
    WLB = [SemN("WLB0"), SemN("WLB1")]
    WPV = SemN("WPV")
    WPT = SemN("WPT")
    yn_in = [[p2.enter_context(nc.sbuf_tensor(f"ynin{p}_{c}", [128, TILE], BF16))
              for c in range(DCH)] for p in range(2)]
    big_a = p2.enter_context(nc.sbuf_tensor("big_a", [128, SUB * D], F32))
    big_b = p2.enter_context(nc.sbuf_tensor("big_b", [128, SUB * D], F32))
    hg_t = [big_a[:, c * TILE:(c + 1) * TILE] for c in range(DCH)]
    hu_t = [big_b[:, c * TILE:(c + 1) * TILE] for c in range(DCH)]
    h_t = [p2.enter_context(nc.sbuf_tensor(f"h{c}", [128, TILE], BF16)) for c in range(DCH)]
    fd_t = [p2.enter_context(nc.sbuf_tensor(f"fd{c}", [128, TILE], BF16)) for c in range(DCH)]
    xre_t = p2.enter_context(nc.sbuf_tensor("xre", [128, SUB * D], F32))
    x2_t = [p2.enter_context(nc.sbuf_tensor(f"x2t{p}", [128, SUB * D], F32))
            for p in range(2)]
    xn3_t = big_a
    tq_t = big_b
    mq_t = big_a
    xq_t = p2.enter_context(nc.sbuf_tensor("xq", [128, SUB * D], BF16))
    sq_b = big_b[:, 0:D]
    s1_b = p2.enter_context(nc.sbuf_tensor("s1_b", [128, SUB], F32))
    s2_b = p2.enter_context(nc.sbuf_tensor("s2_b", [128, SUB], F32))
    mu_b = p2.enter_context(nc.sbuf_tensor("mu_b", [128, SUB], F32))
    t1_b = p2.enter_context(nc.sbuf_tensor("t1_b", [128, SUB], F32))
    var_b = p2.enter_context(nc.sbuf_tensor("var_b", [128, SUB], F32))
    r_b = p2.enter_context(nc.sbuf_tensor("r_b", [128, SUB], F32))
    nmr_b = p2.enter_context(nc.sbuf_tensor("nmr_b", [128, SUB], F32))
    mx_b = p2.enter_context(nc.sbuf_tensor("mx_b", [128, SUB], F32))
    pfg = [p2.enter_context(nc.psum_tensor(f"pfg{p}", [128, TILE], F32)) for p in range(3)]
    pfu = [p2.enter_context(nc.psum_tensor(f"pfu{p}", [128, TILE], F32)) for p in range(3)]
    pot = [p2.enter_context(nc.psum_tensor(f"pot{p}", [128, 512], F32)) for p in range(2)]
    pbias = pot[0][:, 0:2 * DCH]
    YNL = [SemN("YNL0"), SemN("YNL1")]
    X2P = [SemN("X2P0"), SemN("X2P1")]
    XQS = SemN("XQS")
    XRL = SemN("XRL")
    TSB = SemN("TSB")
    VSB = SemN("VSB")
    SSB = SemN("SSB")
    GSB = SemN("GSB")

    WCB = SemN("WCB")
    WCO = SemN("WCO")
    @block.gpsimd
    def _(g):
        g.wait_ge(PB.h, PB.n)
        for k in range(DCH):
            dma(g, WCB, wfd_sb[k][:, :], w_d["Wfd"][k * 128:(k + 1) * 128, :])
        for k in range(DCH):
            dma(g, WCO, wo_sb[k][:, :], w_d["Wo"][k * 128:(k + 1) * 128, :])

    wlist_b = ([("Wfg", wfg_sb, True, k) for k in range(DCH)]
               + [("Wfu", wfu_sb, True, k) for k in range(DCH)])
    wlb_after = {}
    for j, (wname, sb, scale_gamma, k) in enumerate(wlist_b):
        @block.sync
        def _(sync, j=j, wname=wname, k=k):
            if j < 2:
                sync.wait_ge(PB.h, PB.n)
            else:
                sync.wait_ge(WPV.h, j - 1)
                sync.wait_ge(WPT.h, j - 1)
            dma(sync, WLB[j % 2], wf32b[j % 2][:, :],
                w_d[wname][k * 128:(k + 1) * 128, :])
            wlb_after[j] = WLB[j % 2].n

        if scale_gamma:
            @block.tensor
            def _(t, j=j, k=k):
                t.wait_ge(WLB[j % 2].h, wlb_after[j])
                base = (j // DCH) * DCH
                for m in range(DCH):
                    t.matmul(pbias[:, base + m:base + m + 1],
                             wf32b[j % 2][:, m * 128:(m + 1) * 128],
                             cols["beta"][:, k:k + 1],
                             start=(j == 0 and m == 0),
                             stop=(j == 15 and m == DCH - 1),
                             skip_group_check=True)
                fence(t, WPT)

        @block.vector
        def _(v, j=j, sb=sb, k=k, scale_gamma=scale_gamma):
            v.wait_ge(WLB[j % 2].h, wlb_after[j])
            if scale_gamma:
                v.tensor_scalar(sb[k][:, :], wf32b[j % 2][:, :],
                                cols["gamma"][:, k:k + 1], None, ALU.mult)
            else:
                v.tensor_copy(sb[k][:, :], wf32b[j % 2][:, :])
            v.engine_nop().then_inc(WPV.h, 1)
            WPV.n += 1

    @block.vector
    def _(v):
        v.wait_ge(WPT.h, 16)
        v.tensor_tensor(bfgp[:, :], pbias[:, 0:DCH], cols["bfg"][:, :], ALU.add)
        v.tensor_tensor(bfup[:, :], pbias[:, DCH:2 * DCH], cols["bfu"][:, :],
                        ALU.add)
        fence(v, VS)
    bias_ready = VS.n

    yn_after = {}
    yn_after = {}

    def emit_yn_loads(ti):
        par = ti % 2
        t0 = ti * TILE

        @block.sync
        def _(sync, ti=ti, par=par, t0=t0):
            if ti >= 2:
                sync.wait_ge(TSB.h, 32 * (ti - 2) + 16)
            for c in range(DCH):
                dma(sync, YNL[par], yn_in[par][c][:, :],
                    yn_sp[c * 128:(c + 1) * 128, t0:t0 + TILE])
            yn_after[ti] = YNL[par].n

    emit_yn_loads(0)
    emit_yn_loads(1)

    @block.sync
    def _(sync):
        for ss in range(SUB):
            dma(sync, XRL, xre_t[:, ss * D:(ss + 1) * D],
                xr[ss * 128:(ss + 1) * 128, :])

    for ti in range(NTILE):
        t0 = ti * TILE
        par = ti % 2

        for m in range(DCH):
            @block.tensor
            def _(t, ti=ti, par=par, m=m):
                if m == 0:
                    t.wait_ge(YNL[par].h, yn_after[ti])
                    if ti == 0:
                        t.wait_ge(WPV.h, len(wlist_b))
                if m >= 3:
                    t.wait_ge(SSB.h, 20 * ti + (m - 3) + 1)
                    t.wait_ge(VSB.h, 22 * ti + (m - 3) + 1)
                elif ti >= 1:
                    t.wait_ge(SSB.h, 20 * (ti - 1) + 8 + {0: 6, 1: 7, 2: 5}[m] + 1)
                for k in range(DCH):
                    i1 = t.matmul(pfg[m % 3][:, :],
                                  wfg_sb[k][:, m * 128:(m + 1) * 128],
                                  yn_in[par][k][:, :],
                                  start=(k == 0), stop=(k == DCH - 1))
                i1.then_inc(TSB.h, 1)
                TSB.n += 1
                for k in range(DCH):
                    i2 = t.matmul(pfu[m % 3][:, :],
                                  wfu_sb[k][:, m * 128:(m + 1) * 128],
                                  yn_in[par][k][:, :],
                                  start=(k == 0), stop=(k == DCH - 1))
                i2.then_inc(TSB.h, 1)
                TSB.n += 1

            @block.scalar
            def _(s, ti=ti, m=m):
                s.wait_ge(TSB.h, 32 * ti + 2 * m + 1)
                if ti == 0 and m == 0:
                    s.wait_ge(VS.h, bias_ready)
                s.activation(hg_t[m], pfg[m % 3][:, :], AF.Silu,
                             bias=bfgp[:, m:m + 1], scale=1.0).then_inc(SSB.h, 1)
                SSB.n += 1

            @block.vector
            def _(v, ti=ti, m=m):
                v.wait_ge(TSB.h, 32 * ti + 2 * m + 2)
                if ti == 0 and m == 0:
                    v.wait_ge(VS.h, bias_ready)
                v.tensor_scalar(hu_t[m], pfu[m % 3][:, :], bfup[:, m:m + 1],
                                None, ALU.add).then_inc(VSB.h, 1)
                VSB.n += 1

        @block.vector
        def _(v, ti=ti):
            v.drain()
            v.wait_ge(SSB.h, 20 * ti + 8)
            v.wait_ge(TSB.h, 32 * ti)
            for m in range(DCH):
                v.tensor_tensor(h_t[m][:, :], hg_t[m], hu_t[m], ALU.mult)
            fence(v, VSB)

        for m in range(DCH):
            @block.tensor
            def _(t, ti=ti, m=m):
                if m == 0:
                    t.wait_ge(VSB.h, 22 * ti + 9)
                    if ti == 0:
                        t.wait_ge(WCB.h, 128)
                if m >= 3:
                    t.wait_ge(SSB.h, 20 * ti + 8 + (m - 3) + 1)
                else:
                    t.wait_ge(SSB.h, 20 * ti + {0: 6, 1: 7, 2: 5}[m] + 1)
                for k in range(DCH):
                    i1 = t.matmul(pfg[m % 3][:, :],
                                  wfd_sb[k][:, m * 128:(m + 1) * 128],
                                  h_t[k][:, :],
                                  start=(k == 0), stop=(k == DCH - 1))
                i1.then_inc(TSB.h, 1)
                TSB.n += 1

            @block.scalar
            def _(s, ti=ti, m=m):
                s.wait_ge(TSB.h, 32 * ti + 16 + m + 1)
                s.activation(fd_t[m][:, :], pfg[m % 3][:, :], AF.Identity,
                             bias=cols["bfd"][:, m:m + 1],
                             scale=1.0).then_inc(SSB.h, 1)
                SSB.n += 1

        for g in range(8):
            ss_, nh = divmod(g, 2)

            @block.tensor
            def _(t, ti=ti, g=g, ss_=ss_, nh=nh):
                if g == 0:
                    t.wait_ge(SSB.h, 20 * ti + 16)
                    if ti == 0:
                        t.wait_ge(WCO.h, 128)
                if g >= 2:
                    t.wait_ge(VSB.h, 22 * ti + 10 + (g - 2))
                elif ti >= 1:
                    t.wait_ge(VSB.h, 22 * (ti - 1) + 10 + (g + 6))
                for k in range(DCH):
                    i1 = t.matmul(pot[g % 2][:, :],
                                  fd_t[k][:, ss_ * 128:(ss_ + 1) * 128],
                                  wo_sb[k][:, nh * 512:(nh + 1) * 512],
                                  start=(k == 0), stop=(k == DCH - 1))
                i1.then_inc(TSB.h, 1)
                TSB.n += 1

            @block.vector
            def _(v, ti=ti, par=par, g=g, ss_=ss_, nh=nh):
                v.wait_ge(TSB.h, 32 * ti + 24 + g + 1)
                if g == 0:
                    if ti >= 2:
                        v.wait_ge(X2P[par].h, 64 * (ti // 2))
                        v.wait_ge(SSB.h, 20 * (ti - 1) + 19)
                v.scalar_tensor_tensor(
                    x2_t[par][:, ss_ * D + nh * 512:ss_ * D + (nh + 1) * 512],
                    pot[g % 2][:, :], 1.0, bo_bc[:, nh * 512:(nh + 1) * 512],
                    ALU.mult, ALU.add).then_inc(VSB.h, 1)
                VSB.n += 1

        @block.gpsimd
        def _(g_, ti=ti, par=par):
            g_.wait_ge(VSB.h, 22 * ti + 17)
            g_.wait_ge(XRL.h, 64 * (ti + 1))
            for ss in range(SUB):
                g_.tensor_tensor(x2_t[par][:, ss * D:(ss + 1) * D],
                                 x2_t[par][:, ss * D:(ss + 1) * D],
                                 xre_t[:, ss * D:(ss + 1) * D], ALU.add)
            fence(g_, GSB)

        @block.sync
        def _(sync, ti=ti, par=par, t0=t0):
            if ti + 1 < NTILE:
                sync.wait_ge(GSB.h, ti + 1)
                for ss in range(SUB):
                    dma(sync, XRL, xre_t[:, ss * D:(ss + 1) * D],
                        xr[t0 + TILE + ss * 128:t0 + TILE + (ss + 1) * 128, :])
            sync.wait_ge(GSB.h, ti + 1)
            for ss in range(SUB):
                dma(sync, X2P[par], x2_sp[t0 + ss * 128:t0 + (ss + 1) * 128, :],
                    x2_t[par][:, ss * D:(ss + 1) * D])

        @block.vector
        def _(v, ti=ti, par=par):
            v.wait_ge(GSB.h, ti + 1)
            for ss in range(SUB):
                v.reduce_sum(s1_b[:, ss:ss + 1], x2_t[par][:, ss * D:(ss + 1) * D],
                             AX.X)
            fence(v, VSB)

        @block.scalar
        def _(s, ti=ti, par=par):
            s.wait_ge(GSB.h, ti + 1)
            for ss in range(SUB):
                if ss:
                    s.drain()
                s.activation(sq_b, x2_t[par][:, ss * D:(ss + 1) * D],
                             AF.Square, bias=0.0, scale=1.0,
                             accum_out=s2_b[:, ss:ss + 1])
            fence(s, SSB)

        @block.vector
        def _(v, ti=ti):
            v.wait_ge(SSB.h, 20 * ti + 17)
            v.tensor_scalar(mu_b[:, :], s1_b[:, :], 1.0 / D, None, ALU.mult)
            v.drain()
            v.scalar_tensor_tensor(t1_b[:, :], mu_b[:, :], -1.0, mu_b[:, :],
                                   ALU.mult, ALU.mult)
            v.drain()
            v.scalar_tensor_tensor(var_b[:, :], s2_b[:, :], 1.0 / D, t1_b[:, :],
                                   ALU.mult, ALU.add)
            fence(v, VSB)

        @block.scalar
        def _(s, ti=ti):
            s.wait_ge(VSB.h, 22 * ti + 19)
            s.activation(t1_b[:, :], var_b[:, :], AF.Sqrt, bias=c_eps5[:, :],
                         scale=1.0)
            fence(s, SSB)

        @block.vector
        def _(v, ti=ti):
            v.wait_ge(SSB.h, 20 * ti + 18)
            v.reciprocal(r_b[:, :], t1_b[:, :])
            v.drain()
            v.scalar_tensor_tensor(nmr_b[:, :], mu_b[:, :], -1.0, r_b[:, :],
                                   ALU.mult, ALU.mult)
            fence(v, VSB)

        @block.scalar
        def _(s, ti=ti, par=par):
            s.wait_ge(VSB.h, 22 * ti + 20)
            for ss in range(SUB):
                s.activation(xn3_t[:, ss * D:(ss + 1) * D],
                             x2_t[par][:, ss * D:(ss + 1) * D], AF.Identity,
                             bias=nmr_b[:, ss:ss + 1], scale=r_b[:, ss:ss + 1])
            fence(s, SSB)

        @block.vector
        def _(v, ti=ti):
            v.wait_ge(SSB.h, 20 * ti + 19)
            for ss in range(SUB):
                v.reduce_max(mx_b[:, ss:ss + 1], xn3_t[:, ss * D:(ss + 1) * D],
                             AX.X, apply_absolute_value=True)
            v.drain()
            v.tensor_scalar(mx_b[:, :], mx_b[:, :], 1e-5, None, ALU.max)
            v.drain()
            v.reciprocal(t1_b[:, :], mx_b[:, :])
            v.drain()
            v.tensor_scalar(s3_cols[:, ti * SUB:(ti + 1) * SUB], t1_b[:, :],
                            127.0, None, ALU.mult)
            v.drain()
            v.tensor_scalar(t1_b[:, :], mx_b[:, :], 1.0 / 127.0, None, ALU.mult)
            v.drain()
            v.tensor_scalar(e_gs[:, ti * SUB:(ti + 1) * SUB], t1_b[:, :],
                            m_cols[:, 0:1], None, ALU.mult)
            v.tensor_scalar(e_us[:, ti * SUB:(ti + 1) * SUB], t1_b[:, :],
                            m_cols[:, 1:2], None, ALU.mult)
            fence(v, VSB)

        @block.scalar
        def _(s, ti=ti):
            s.wait_ge(VSB.h, 22 * ti + 21)
            for ss in range(SUB):
                s.activation(tq_t[:, ss * D:(ss + 1) * D],
                             xn3_t[:, ss * D:(ss + 1) * D], AF.Identity,
                             bias=(c_1275[:, :] if ROUND_MODE == "ladder"
                                   else c_magic[:, :]),
                             scale=s3_cols[:, ti * SUB + ss:ti * SUB + ss + 1])
            fence(s, SSB)

        @block.vector
        def _(v, ti=ti):
            v.wait_ge(SSB.h, 20 * ti + 20)
            v.wait_ge(XQS.h, 64 * ti)
            if ROUND_MODE == "ladder":
                emit_round_ladder(v, tq_t[:, :], mq_t[:, :], xq_t[:, :], 127.0)
            else:
                v.tensor_scalar(xq_t[:, :], tq_t[:, :], MAGIC, None,
                                ALU.subtract)
            fence(v, VSB)

        @block.sync
        def _(sync, ti=ti, t0=t0):
            sync.wait_ge(VSB.h, 22 * ti + 22)
            for ss in range(SUB):
                dma(sync, XQS, xq_sp[t0 + ss * 128:t0 + (ss + 1) * 128, :],
                    xq_t[:, ss * D:(ss + 1) * D])

        if ti + 2 < NTILE:
            emit_yn_loads(ti + 2)

    @block.sync
    def _(sync):
        wait_all(sync, VS, SS, GS, TS_, DS, SD, GD)
        wait_all(sync, YNL[0], YNL[1], X2P[0], X2P[1], XQS, XRL,
                 TSB, VSB, SSB, GSB, WCB, WCO)
        sync.nop().then_inc(PB.h, 1)
    PB.n += 1
    p2.close()

    p3 = ExitStack()
    wqg_sb = [p3.enter_context(nc.sbuf_tensor(f"wqg{k}", [128, H], BF16)) for k in range(DCH)]
    wqu_sb = [p3.enter_context(nc.sbuf_tensor(f"wqu{k}", [128, H], BF16)) for k in range(DCH)]
    xqf = [[p3.enter_context(nc.sbuf_tensor(f"xqf{p}_{k}", [128, TILE], BF16))
            for k in range(DCH)] for p in range(2)]
    sg_m = p3.enter_context(nc.sbuf_tensor("sg_m", [128, H], F32))
    h_m = p3.enter_context(nc.sbuf_tensor("h_m", [128, H], F32))
    hq_m = [p3.enter_context(nc.sbuf_tensor(f"hq_m{p}", [128, H], BF16))
            for p in range(2)]
    mx2_b = p3.enter_context(nc.sbuf_tensor("mx2_b", [128, 2], F32))
    s4_b = p3.enter_context(nc.sbuf_tensor("s4_b", [128, 2], F32))
    t2_b = p3.enter_context(nc.sbuf_tensor("t2_b", [128, 2], F32))
    pga = [p3.enter_context(nc.psum_tensor(f"pga{p}", [128, 512], F32)) for p in range(4)]
    pua = [p3.enter_context(nc.psum_tensor(f"pua{p}", [128, 512], F32)) for p in range(4)]
    XQ = [SemN("XQ0"), SemN("XQ1")]
    HQS = [SemN("HQS0"), SemN("HQS1")]
    TSA = SemN("TSA")
    VSA = SemN("VSA")
    SSA = SemN("SSA")

    @block.sync
    def _(sync):
        for k in range(DCH):
            dma(sync, DS, wqg_sb[k][:, :], wq_sp["g"][k * 128:(k + 1) * 128, :])
            dma(sync, DS, wqu_sb[k][:, :], wq_sp["u"][k * 128:(k + 1) * 128, :])
        sync.wait_ge(DS.h, DS.n)

    xq_after = {}

    def emit_xq_loads(gi):
        par = gi % 2
        t0 = gi * TILE

        @block.sync
        def _(sync, gi=gi, par=par, t0=t0):
            if gi >= 2:
                sync.wait_ge(TSA.h, 2 * 8 * SUB * (gi - 1))
            for k in range(DCH):
                dma(sync, XQ[par], xqf[par][k][:, :],
                    xq_sp[t0:t0 + TILE, k * 128:(k + 1) * 128], transpose=True)
            xq_after[gi] = XQ[par].n

    emit_xq_loads(0)
    emit_xq_loads(1)

    SPC = 9 if ROUND_MODE == "ladder" else 8
    for ci in range(NCHUNK):
        gi = ci // SUB
        par = gi % 2
        so = (ci % SUB) * 128
        cpar = ci % 2
        t0 = ci * 128

        for ob in range(8):
            G = ci * 16 + ob * 2

            @block.tensor
            def _(t, ci=ci, gi=gi, par=par, so=so, ob=ob):
                if ob == 0 and ci % SUB == 0:
                    t.wait_ge(XQ[par].h, xq_after[gi])
                slot = ob % 4
                if ob >= 4:
                    t.wait_ge(SSA.h, SPC * ci + (ob - 4) + 1)
                    t.wait_ge(VSA.h, 10 * ci + (ob - 4) + 1)
                elif ci >= 1:
                    t.wait_ge(SSA.h, SPC * (ci - 1) + (ob + 4) + 1)
                    t.wait_ge(VSA.h, 10 * (ci - 1) + (ob + 4) + 1)
                for k in range(DCH):
                    i1 = t.matmul(pga[slot][:, :], xqf[par][k][:, so:so + 128],
                                  wqg_sb[k][:, ob * 512:(ob + 1) * 512],
                                  start=(k == 0), stop=(k == DCH - 1))
                    i2 = t.matmul(pua[slot][:, :], xqf[par][k][:, so:so + 128],
                                  wqu_sb[k][:, ob * 512:(ob + 1) * 512],
                                  start=(k == 0), stop=(k == DCH - 1))
                i1.then_inc(TSA.h, 1)
                TSA.n += 1
                i2.then_inc(TSA.h, 1)
                TSA.n += 1

            @block.scalar
            def _(s, ci=ci, ob=ob):
                slot = ob % 4
                s.wait_ge(TSA.h, 2 * (ci * 8 + ob) + 2)
                if ob == 0 and ci >= 1:
                    s.wait_ge(VSA.h, 10 * ci)
                s.activation(sg_m[:, ob * 512:(ob + 1) * 512], pga[slot][:, :],
                             AF.Silu, bias=0.0,
                             scale=e_gs[:, ci:ci + 1]).then_inc(SSA.h, 1)
                SSA.n += 1

            @block.vector
            def _(v, ci=ci, ob=ob):
                slot = ob % 4
                v.wait_ge(SSA.h, SPC * ci + ob + 1)
                v.wait_ge(TSA.h, 2 * (ci * 8 + ob) + 2)
                v.scalar_tensor_tensor(h_m[:, ob * 512:(ob + 1) * 512],
                                       pua[slot][:, :], e_us[:, ci:ci + 1],
                                       sg_m[:, ob * 512:(ob + 1) * 512],
                                       ALU.mult, ALU.mult).then_inc(VSA.h, 1)
                VSA.n += 1

        @block.vector
        def _(v, ci=ci, cpar=cpar):
            v.wait_ge(VSA.h, 10 * ci + 8)
            v.drain()
            v.reduce_max(mx2_b[:, cpar:cpar + 1], h_m[:, :], AX.X,
                         apply_absolute_value=True)
            v.drain()
            v.tensor_scalar(mx2_b[:, cpar:cpar + 1], mx2_b[:, cpar:cpar + 1],
                            1e-5, None, ALU.max)
            v.drain()
            v.reciprocal(t2_b[:, cpar:cpar + 1], mx2_b[:, cpar:cpar + 1])
            v.drain()
            v.tensor_scalar(s4_b[:, cpar:cpar + 1], t2_b[:, cpar:cpar + 1],
                            127.0, None, ALU.mult)
            v.drain()
            v.tensor_scalar(t2_b[:, cpar:cpar + 1], mx2_b[:, cpar:cpar + 1],
                            1.0 / 127.0, None, ALU.mult)
            v.drain()
            v.tensor_scalar(e_ds[:, ci:ci + 1], t2_b[:, cpar:cpar + 1],
                            m_cols[:, 2:3], None, ALU.mult)
            v.engine_nop().then_inc(VSA.h, 1)
            VSA.n += 1

        if ROUND_MODE == "ladder":
            @block.scalar
            def _(s, ci=ci, cpar=cpar):
                s.wait_ge(VSA.h, 10 * ci + 9)
                s.activation(sg_m[:, :], h_m[:, :], AF.Identity,
                             bias=c_1275[:, :],
                             scale=s4_b[:, cpar:cpar + 1]).then_inc(SSA.h, 1)
                SSA.n += 1

            @block.vector
            def _(v, ci=ci, cpar=cpar):
                v.wait_ge(SSA.h, SPC * ci + 9)
                if ci >= 2:
                    v.wait_ge(HQS[cpar].h, 16 * (ci // 2))
                emit_round_ladder(v, sg_m[:, :], h_m[:, :], hq_m[cpar][:, :], 127.0)
                v.engine_nop().then_inc(VSA.h, 1)
                VSA.n += 1
        else:
            @block.vector
            def _(v, ci=ci, cpar=cpar):
                v.drain()
                if ci >= 2:
                    v.wait_ge(HQS[cpar].h, 16 * (ci // 2))
                v.tensor_scalar(sg_m[:, :], h_m[:, :],
                                s4_b[:, cpar:cpar + 1], MAGIC,
                                ALU.mult, ALU.add)
                v.drain()
                v.tensor_scalar(hq_m[cpar][:, :], sg_m[:, :], MAGIC, None,
                                ALU.subtract).then_inc(VSA.h, 1)
                VSA.n += 1

        @block.sync
        def _(sync, ci=ci, cpar=cpar, t0=t0):
            sync.wait_ge(VSA.h, 10 * ci + 10)
            dma(sync, HQS[cpar], hq_sp[t0:t0 + 128, :], hq_m[cpar][:, :])

        if ci % SUB == 0 and ci // SUB + 2 < NTILE:
            emit_xq_loads(ci // SUB + 2)

    @block.sync
    def _(sync):
        sync.wait_ge(HQS[0].h, HQS[0].n)
        sync.wait_ge(HQS[1].h, HQS[1].n)
        wait_all(sync, VS, SS, GS, TS_, DS, SD, GD)
        wait_all(sync, XQ[0], XQ[1], TSA, VSA, SSA)
        sync.nop().then_inc(PB.h, 1)
    PB.n += 1
    p3.close()

    p4 = ExitStack()
    wqd_sb = [p4.enter_context(nc.sbuf_tensor(f"wqd{k}", [128, D], BF16)) for k in range(HCH)]
    hqf = [[p4.enter_context(nc.sbuf_tensor(f"hqf{p}_{k}", [128, TILE], BF16))
            for k in range(HCH)] for p in range(2)]
    x2re = [p4.enter_context(nc.sbuf_tensor(f"x2re{p}", [128, SUB * D], F32))
            for p in range(2)]
    o_t = [p4.enter_context(nc.sbuf_tensor(f"o_t{p}", [128, SUB * D], F32))
           for p in range(2)]
    pdn = [p4.enter_context(nc.psum_tensor(f"pdn{p}", [128, 512], F32))
           for p in range(4)]
    HQ = [SemN("HQ0"), SemN("HQ1")]
    HS = [SemN("HS0"), SemN("HS1")]
    X2 = [SemN("X20"), SemN("X21")]
    OG = SemN("OG")

    vs0 = VS.n
    ts0 = TS_.n
    gs0 = GS.n

    @block.sync
    def _(sync):
        for k in range(HCH):
            dma(sync, DS, wqd_sb[k][:, :], wq_sp["d"][k * 128:(k + 1) * 128, :])
        sync.wait_ge(DS.h, DS.n)

    hq_after = {}
    hs_after = {}
    x2_after = {}

    def emit_loads(ti):
        par = ti % 2
        t0 = ti * TILE

        @block.sync
        def _(sync, ti=ti, par=par, t0=t0):
            if ti < 2:
                sync.wait_ge(PB.h, PB.n)
            else:
                sync.wait_ge(TS_.h, ts0 + 8 * (ti - 1))
                sync.wait_ge(VS.h, vs0 + 8 * (ti - 1))
            for k in range(0, HCH, 1):
                dma(sync, HQ[par], hqf[par][k][:, :],
                    hq_sp[t0:t0 + TILE, k * 128:(k + 1) * 128], transpose=True)
            hq_after[ti] = HQ[par].n

        @block.scalar
        def _(s, ti=ti, par=par, t0=t0):
            if ti < 2:
                s.wait_ge(PB.h, PB.n)
            else:
                s.wait_ge(VS.h, vs0 + 8 * (ti - 1))
            for ss in range(SUB):
                dma(s, X2[par], x2re[par][:, ss * D:(ss + 1) * D],
                    x2_sp[t0 + ss * 128:t0 + (ss + 1) * 128, :])
            x2_after[ti] = X2[par].n
            hs_after[ti] = HS[par].n

    emit_loads(0)
    emit_loads(1)

    for ti in range(NTILE):
        par = ti % 2
        t0 = ti * TILE

        for ss in range(SUB):
            G = ti * 8 + 2 * ss

            @block.tensor
            def _(t, ti=ti, par=par, ss=ss, G=G):
                if ss == 0:
                    t.wait_ge(HQ[par].h, hq_after[ti])
                    t.wait_ge(HS[par].h, hs_after[ti])
                if G >= 3:
                    t.wait_ge(VS.h, vs0 + G - 2)
                for k in range(HCH):
                    i1 = t.matmul(pdn[G % 4][:, :],
                                  hqf[par][k][:, ss * 128:(ss + 1) * 128],
                                  wqd_sb[k][:, 0:512],
                                  start=(k == 0), stop=(k == HCH - 1))
                    i2 = t.matmul(pdn[(G + 1) % 4][:, :],
                                  hqf[par][k][:, ss * 128:(ss + 1) * 128],
                                  wqd_sb[k][:, 512:1024],
                                  start=(k == 0), stop=(k == HCH - 1))
                i1.then_inc(TS_.h, 1)
                TS_.n += 1
                i2.then_inc(TS_.h, 1)
                TS_.n += 1

            for nh in range(2):
                ci = ti * SUB + ss

                @block.vector
                def _(v, ti=ti, par=par, ss=ss, nh=nh, G=G, ci=ci):
                    v.wait_ge(TS_.h, ts0 + G + nh + 1)
                    if ss == 0 and nh == 0:
                        v.wait_ge(X2[par].h, x2_after[ti])
                        if ti >= 2:
                            v.wait_ge(GS.h, gs0 + (ti - 1))
                    v.scalar_tensor_tensor(
                        o_t[par][:, ss * D + nh * 512:ss * D + (nh + 1) * 512],
                        pdn[(G + nh) % 4][:, :], e_ds[:, ci:ci + 1],
                        x2re[par][:, ss * D + nh * 512:ss * D + (nh + 1) * 512],
                        ALU.mult, ALU.add).then_inc(VS.h, 1)
                    VS.n += 1

        @block.gpsimd
        def _(g_, ti=ti, par=par, t0=t0):
            g_.wait_ge(VS.h, vs0 + 8 * (ti + 1))
            for ss in range(SUB):
                dma(g_, OG, outr[t0 + ss * 128:t0 + (ss + 1) * 128, :],
                    o_t[par][:, ss * D:(ss + 1) * D])
            g_.wait_ge(OG.h, OG.n)
            g_.engine_nop().then_inc(GS.h, 1)
            GS.n += 1

        if ti + 2 < NTILE:
            emit_loads(ti + 2)

    @block.sync
    def _(sync):
        sync.wait_ge(GS.h, gs0 + NTILE)

    @block.sync
    def _(sync):
        wait_all(sync, DS, SD, GD, VS, SS, GS, TS_)

    p4.close()
    es.close()
    return nc


_NC_CACHE = None


def _get_nc():
    global _NC_CACHE
    if _NC_CACHE is None:
        _NC_CACHE = build_program()
    return _NC_CACHE


def kernel(**inputs):
    n_cores = 8
    x_full = np.ascontiguousarray(np.asarray(inputs["L0_attn_x"]))
    wgT = np.ascontiguousarray(np.asarray(inputs["L0_ffn_w_gate"]).T)
    wuT = np.ascontiguousarray(np.asarray(inputs["L0_ffn_w_up"]).T)
    wdT = np.ascontiguousarray(np.asarray(inputs["L0_ffn_w_down"]).T)
    common = {
        "Wg": np.ascontiguousarray(inputs["L0_attn_Wg"]),
        "Wi": np.ascontiguousarray(inputs["L0_attn_Wi"]),
        "Wfg": np.ascontiguousarray(inputs["L0_attn_Wfg"]),
        "Wfu": np.ascontiguousarray(inputs["L0_attn_Wfu"]),
        "Wfd": np.ascontiguousarray(inputs["L0_attn_Wfd"]),
        "Wo": np.ascontiguousarray(inputs["L0_attn_Wo"]),
        "bg": np.ascontiguousarray(inputs["L0_attn_bg"]),
        "bi": np.ascontiguousarray(inputs["L0_attn_bi"]),
        "A": np.ascontiguousarray(inputs["L0_attn_A"]),
        "Bp": np.ascontiguousarray(inputs["L0_attn_Bp"]),
        "Cp": np.ascontiguousarray(inputs["L0_attn_Cp"]),
        "Dp": np.ascontiguousarray(inputs["L0_attn_Dp"]),
        "gamma": np.ascontiguousarray(inputs["L0_attn_gamma"]),
        "beta": np.ascontiguousarray(inputs["L0_attn_beta"]),
        "bfg": np.ascontiguousarray(inputs["L0_attn_bfg"]),
        "bfu": np.ascontiguousarray(inputs["L0_attn_bfu"]),
        "bfd": np.ascontiguousarray(inputs["L0_attn_bfd"]),
        "bo": np.ascontiguousarray(inputs["L0_attn_bo"]),
        "wgT": wgT, "wuT": wuT, "wdT": wdT,
    }
    in_maps = []
    for i in range(n_cores):
        m = dict(common)
        m["x"] = np.ascontiguousarray(x_full[2 * i:2 * i + 2])
        in_maps.append(m)
    nc = _get_nc()
    res = run_bass_kernel_spmd(nc, in_maps, core_ids=list(range(n_cores)))
    return np.concatenate([np.asarray(r["out"]) for r in res.results], axis=0)


# version 26 (final)
# speedup vs baseline: 1.0414x; 1.0080x over previous
import numpy as np
from contextlib import ExitStack

import concourse.bass as bass
import concourse.mybir as mybir
from concourse.bass_utils import run_bass_kernel_spmd

import concourse.bass_utils as _bu

if not getattr(_bu, "_ldw_patch", False):
    _orig_run_command = _bu.run_command

    def _run_command_ldw(argv, **kwargs):
        argv = ["--enable-ldw-opt=true" if a == "--enable-ldw-opt=false" else a
                for a in argv]
        return _orig_run_command(argv, **kwargs)

    _bu.run_command = _run_command_ldw
    _bu._ldw_patch = True


F32 = mybir.dt.float32
BF16 = mybir.dt.bfloat16
AF = mybir.ActivationFunctionType
ALU = mybir.AluOpType
AX = mybir.AxisListType

B = 2
T = 2048
NT = B * T
D = 1024
H = 4096
DCH = D // 128
HCH = H // 128
NCHUNK = NT // 128
TILE = 512
NTILE = NT // TILE
SUB = TILE // 128


class Sem:
    def __init__(self, nc, name):
        self.h = nc.semaphore(name)
        self.n = 0

    def __enter__(self):
        self.h = self.h.__enter__()
        return self

    def __exit__(self, *a):
        return False


def build_program():
    nc = bass.Bass("TRN2", target_bir_lowering=False)

    x_d = nc.dram_tensor("x", [B, T, D], F32, kind="ExternalInput")
    out_d = nc.dram_tensor("out", [B, T, D], F32, kind="ExternalOutput")
    wnames = ["Wg", "Wi", "Wfg", "Wfu", "Wfd", "Wo"]
    w_d = {n: nc.dram_tensor(n, [D, D], F32, kind="ExternalInput") for n in wnames}
    vnames = ["bg", "bi", "A", "Bp", "Cp", "Dp", "gamma", "beta",
              "bfg", "bfu", "bfd", "bo"]
    v_d = {n: nc.dram_tensor(n, [D], F32, kind="ExternalInput") for n in vnames}
    wgT_d = nc.dram_tensor("wgT", [D, H], F32, kind="ExternalInput")
    wuT_d = nc.dram_tensor("wuT", [D, H], F32, kind="ExternalInput")
    wdT_d = nc.dram_tensor("wdT", [H, D], F32, kind="ExternalInput")

    xn_sp = nc.dram_tensor("xn_sp", [NT, D], BF16)
    mean_sp = nc.dram_tensor("mean_sp", [3, 128], F32)
    yn_sp = nc.dram_tensor("yn_sp", [D, NT], BF16)
    x2_sp = nc.dram_tensor("x2_sp", [NT, D], F32)
    xq_sp = nc.dram_tensor("xq_sp", [NT, D], BF16)
    hq_sp = nc.dram_tensor("hq_sp", [NT, H], BF16)
    wq_sp = {"g": nc.dram_tensor("wqg_sp", [D, H], BF16),
             "u": nc.dram_tensor("wqu_sp", [D, H], BF16),
             "d": nc.dram_tensor("wqd_sp", [H, D], BF16)}

    es = ExitStack()
    Sb = lambda name, shape, dt=F32: es.enter_context(nc.sbuf_tensor(name, shape, dt))
    Ps = lambda name, shape: es.enter_context(nc.psum_tensor(name, shape, F32))
    SemN = lambda name: es.enter_context(Sem(nc, name))

    cols = {n: Sb("c_" + n, [128, DCH]) for n in vnames}
    cpbp = Sb("c_cpbp", [128, DCH])
    bfgp = Sb("c_bfgp", [128, DCH])
    bfup = Sb("c_bfup", [128, DCH])
    onescol_f = Sb("onescol_f", [128, 1])
    onescol_b = Sb("onescol_b", [128, 1], BF16)
    ones_k1 = Sb("ones_k1", [1, 128])
    ident = Sb("ident", [128, 128])
    iota_f = Sb("iota_f", [128, 128])
    pidx = Sb("pidx", [128, 1])
    bo_bc = Sb("bo_bc", [128, D])
    cs127d = Sb("cs127d", [128, D])
    m_cols = Sb("m_cols", [128, 6])
    c_eps5 = Sb("c_eps5", [128, 1])
    c_eps3 = Sb("c_eps3", [128, 1])
    c_1275 = Sb("c_1275", [128, 1])
    c_magic = Sb("c_magic", [128, 1])
    e_gs = Sb("e_gs", [128, NCHUNK])
    e_us = Sb("e_us", [128, NCHUNK])
    e_ds = Sb("e_ds", [128, NCHUNK])
    s3_cols = Sb("s3_cols", [128, NCHUNK])

    PB = SemN("PB")
    DS = SemN("DS")
    SD = SemN("SD")
    GD = SemN("GD")
    VS = SemN("VS")
    SS = SemN("SS")
    GS = SemN("GS")
    TS_ = SemN("TS")

    block = es.enter_context(nc.Block())

    def fence(eng, sem):
        eng.drain().then_inc(sem.h, 1)
        sem.n += 1

    def dma(eng, sem, dst, src, transpose=False, slow=False):
        if transpose:
            eng.dma_start(dst, src, transpose=True).then_inc(sem.h, 16)
        elif slow:
            eng.dma_start(dst, src,
                          allow_slow_non_contiguous=True).then_inc(sem.h, 16)
        else:
            eng.dma_start(dst, src).then_inc(sem.h, 16)
        sem.n += 16

    def wait_all(eng, *sems):
        for s in sems:
            eng.wait_ge(s.h, s.n)

    ROUND_MODE = "magic"
    MAGIC = 12582912.0

    def emit_round_ladder(v, t_ap, scr_ap, out_ap, out_shift):
        v.tensor_copy(scr_ap, t_ap)
        for k in range(7, -1, -1):
            c = float(1 << k)
            v.drain()
            v.tensor_scalar(out_ap, scr_ap, c, None, ALU.is_ge)
            v.drain()
            v.scalar_tensor_tensor(scr_ap, out_ap, -c, scr_ap, ALU.mult, ALU.add)
        v.drain()
        v.tensor_tensor(scr_ap, t_ap, scr_ap, ALU.subtract)
        v.drain()
        v.tensor_scalar(out_ap, scr_ap, float(out_shift), None, ALU.subtract)

    xr = x_d[:, :, :].rearrange("b t d -> (b t) d")
    outr = out_d[:, :, :].rearrange("b t d -> (b t) d")

    @block.sync
    def _(sync):
        for n in vnames:
            dma(sync, DS, cols[n][:, :],
                v_d[n][:].rearrange("(c p) -> p c", p=128), slow=True)
        sync.wait_ge(DS.h, DS.n)

    @block.gpsimd
    def _(g):
        g.iota(iota_f[:], [[1, 128]], channel_multiplier=0,
               allow_small_or_imprecise_dtypes=True)
        g.iota(pidx[:], [[0, 1]], channel_multiplier=1,
               allow_small_or_imprecise_dtypes=True)
        g.memset(onescol_f[:], 1.0)
        g.memset(onescol_b[:], 1.0)
        g.memset(ones_k1[:], 1.0)
        g.memset(c_eps5[:], 1e-5)
        g.memset(c_eps3[:], 1e-3)
        g.memset(c_1275[:], 127.5)
        g.memset(c_magic[:], 12582912.0)
        g.drain()
        g.tensor_scalar(ident[:], iota_f[:], pidx[:], None, ALU.is_equal)
        fence(g, GS)

    @block.vector
    def _(v):
        wait_all(v, DS)
        v.tensor_tensor(cpbp[:], cols["Cp"][:], cols["Bp"][:], ALU.mult)
        fence(v, VS)

    bo_row = Sb("bo_row", [1, D])

    @block.sync
    def _(sync):
        dma(sync, DS, bo_row[:, :], v_d["bo"][:].rearrange("(o d) -> o d", o=1))
        sync.wait_ge(DS.h, DS.n)

    pC = ExitStack()
    pbc = pC.enter_context(nc.psum_tensor("pbc", [128, 512], F32))

    @block.tensor
    def _(t):
        wait_all(t, DS, GS)
        t.matmul(pbc[:, :], ones_k1[:, :], bo_row[:, 0:512],
                 start=True, stop=True).then_inc(TS_.h, 1)
        TS_.n += 1

    @block.vector
    def _(v):
        wait_all(v, TS_)
        v.tensor_copy(bo_bc[:, 0:512], pbc[:, :]).then_inc(VS.h, 1)
        VS.n += 1

    @block.tensor
    def _(t):
        wait_all(t, VS)
        t.matmul(pbc[:, :], ones_k1[:, :], bo_row[:, 512:1024],
                 start=True, stop=True).then_inc(TS_.h, 1)
        TS_.n += 1

    @block.vector
    def _(v):
        wait_all(v, TS_)
        v.tensor_copy(bo_bc[:, 512:1024], pbc[:, :])
        fence(v, VS)

    pC.close()

    @block.sync
    def _(sync):
        wait_all(sync, VS, SS, GS, TS_, DS, SD, GD)
        sync.nop().then_inc(PB.h, 1)
    PB.n += 1

    p1 = ExitStack()
    wg_big = p1.enter_context(nc.sbuf_tensor("wg_big", [128, DCH * D], BF16))
    wi_big = p1.enter_context(nc.sbuf_tensor("wi_big", [128, DCH * D], BF16))
    wg_sb = [wg_big[:, k * D:(k + 1) * D] for k in range(DCH)]
    wi_sb = [wi_big[:, k * D:(k + 1) * D] for k in range(DCH)]
    a_bc = [bass.AP(cols["A"].tensor if hasattr(cols["A"], "tensor") else cols["A"],
                    c, [[DCH, 128], [0, TILE]]) for c in range(DCH)]
    x_t = [p1.enter_context(nc.sbuf_tensor(f"x_t{p}", [128, SUB * D], F32))
           for p in range(2)]
    xn_tm = p1.enter_context(nc.sbuf_tensor("xn_tm", [128, SUB * D], BF16))
    xn_fm = [[p1.enter_context(nc.sbuf_tensor(f"xnfm{p}_{c}", [128, TILE], BF16))
              for c in range(DCH)] for p in range(2)]
    gate_t = [p1.enter_context(nc.sbuf_tensor(f"gate{c}", [128, TILE], F32)) for c in range(DCH)]
    ui_t = [p1.enter_context(nc.sbuf_tensor(f"ui{c}", [128, TILE], F32)) for c in range(DCH)]
    yc_t = [p1.enter_context(nc.sbuf_tensor(f"yc{c}", [128, TILE], F32)) for c in range(DCH)]
    y_t = [p1.enter_context(nc.sbuf_tensor(f"y{c}", [128, TILE], F32)) for c in range(DCH)]
    ysq_t = [p1.enter_context(nc.sbuf_tensor(f"ysq{c}", [128, TILE], BF16)) for c in range(DCH)]
    yn_t = [p1.enter_context(nc.sbuf_tensor(f"yn{c}", [128, TILE], BF16)) for c in range(DCH)]
    sq_scr = p1.enter_context(nc.sbuf_tensor("sq_scr", [128, D], F32))
    st_s1 = p1.enter_context(nc.sbuf_tensor("st_s1", [128, SUB], F32))
    st_s2 = p1.enter_context(nc.sbuf_tensor("st_s2", [128, SUB], F32))
    st_mu = p1.enter_context(nc.sbuf_tensor("st_mu", [128, SUB], F32))
    st_t1 = p1.enter_context(nc.sbuf_tensor("st_t1", [128, SUB], F32))
    st_var = p1.enter_context(nc.sbuf_tensor("st_var", [128, SUB], F32))
    st_r = p1.enter_context(nc.sbuf_tensor("st_r", [128, SUB], F32))
    st_nmr = p1.enter_context(nc.sbuf_tensor("st_nmr", [128, SUB], F32))
    carry = p1.enter_context(nc.sbuf_tensor("carry", [128, DCH], F32))
    r2_sc = p1.enter_context(nc.sbuf_tensor("r2_sc", [1, 4 * TILE], F32))
    p0w = [p1.enter_context(nc.sbuf_tensor(f"p0w{p}", [128, 512], F32)) for p in range(3)]
    p0q = [p1.enter_context(nc.sbuf_tensor(f"p0q{p}", [128, 512], BF16)) for p in range(2)]
    p0c1 = p1.enter_context(nc.sbuf_tensor("p0c1", [128, 512], F32))
    p0c2 = p1.enter_context(nc.sbuf_tensor("p0c2", [128, 512], F32))
    p0stat = p1.enter_context(nc.sbuf_tensor("p0stat", [128, 64], F32))
    p0row = p1.enter_context(nc.sbuf_tensor("p0row", [1, 128], F32))
    p0thr = p1.enter_context(nc.sbuf_tensor("p0thr", [128, 2], F32))
    P0L = [SemN(f"P0L{i}") for i in range(3)]
    P0V = SemN("P0V")
    P0G = SemN("P0G")
    P0SP = [SemN("P0SPa"), SemN("P0SPb")]
    P0D = SemN("P0D")
    P0M = SemN("P0M")
    pg = [p1.enter_context(nc.psum_tensor(f"pg{p}", [128, TILE], F32)) for p in range(2)]
    pi_ = [p1.enter_context(nc.psum_tensor(f"pi{p}", [128, TILE], F32)) for p in range(2)]
    WL = [SemN("WL0"), SemN("WL1")]
    XL = [SemN("XL0"), SemN("XL1")]
    XNT = [SemN("XNT0"), SemN("XNT1")]
    XNS = SemN("XNS")
    YNS = SemN("YNS")
    TSC = SemN("TSC")
    VSC = SemN("VSC")
    SSC = SemN("SSC")
    GSC = SemN("GSC")
    ps1r = p1.enter_context(nc.psum_tensor("ps1r", [1, TILE], F32))
    ps2r = p1.enter_context(nc.psum_tensor("ps2r", [1, TILE], F32))
    pR = p1.enter_context(nc.psum_tensor("pR", [128, TILE], F32))
    pNM = p1.enter_context(nc.psum_tensor("pNM", [128, TILE], F32))

    x_after = {}

    def emit_x_loads(ti):
        par = ti % 2
        t0 = ti * TILE

        @block.scalar
        def _(s, ti=ti, par=par, t0=t0):
            if ti < 2:
                s.wait_ge(PB.h, PB.n)
            else:
                s.wait_ge(SSC.h, 13 * (ti - 2) + 3)
            for ss in range(SUB):
                dma(s, XL[par], x_t[par][:, ss * D:(ss + 1) * D],
                    xr[t0 + ss * 128:t0 + (ss + 1) * 128, :])
            x_after[ti] = XL[par].n

    emit_x_loads(0)
    emit_x_loads(1)

    @block.gpsimd
    def _(g):
        g.wait_ge(PB.h, PB.n)
        g.memset(carry[:, :], 0.0)
        for k in range(DCH):
            dma(g, WL[0], wg_sb[k][:, :], w_d["Wg"][k * 128:(k + 1) * 128, :])
        for k in range(DCH):
            dma(g, WL[0], wi_sb[k][:, :], w_d["Wi"][k * 128:(k + 1) * 128, :])

    p0_mats = [("g", wgT_d, D // 128, 8, 0), ("u", wuT_d, D // 128, 8, 1),
               ("d", wdT_d, H // 128, 2, 2)]
    p0_tasks = []
    for mname, wdram, nrt, ncc, midx in p0_mats:
        inv_size = 1.0 / float(nrt * 128 * ncc * 512)
        for c in range(nrt * ncc):
            p0_tasks.append(("l1", mname, wdram, nrt, ncc, midx, c, inv_size))
        p0_tasks.append(("mean", mname, wdram, nrt, ncc, midx, None, inv_size))
        for c in range(nrt * ncc):
            p0_tasks.append(("l2", mname, wdram, nrt, ncc, midx, c, inv_size))
        p0_tasks.append(("fin", mname, wdram, nrt, ncc, midx, None, inv_size))
    p0_state = {"i": 0, "nl": 0, "nsp": 0,
                "slot_free": [("none", 0)] * 3}

    def _p0_wait(eng, tag_val):
        tag, val = tag_val
        if tag == "P0V":
            eng.wait_ge(P0V.h, val)

    def emit_p0(n_tasks):
        i0 = p0_state["i"]
        for task in p0_tasks[i0:i0 + n_tasks]:
            kind, mname, wdram, nrt, ncc, midx, c, inv_size = task
            if kind in ("l1", "l2"):
                rt, cc = divmod(c, ncc)
                nl = p0_state["nl"]
                lp = nl % 3
                freev = p0_state["slot_free"][lp]

                @block.scalar
                def _(s, nl=nl, lp=lp, wdram=wdram, rt=rt, cc=cc, freev=freev):
                    if nl == 0:
                        s.wait_ge(PB.h, 1)
                    _p0_wait(s, freev)
                    dma(s, P0L[lp], p0w[lp][:, :],
                        wdram[rt * 128:(rt + 1) * 128, cc * 512:(cc + 1) * 512])
                p0_state["nl"] += 1

                if kind == "l1":
                    @block.vector
                    def _(v, c=c, lp=lp, lv=P0L[lp].n):
                        v.wait_ge(P0L[lp].h, lv)
                        v.reduce_sum(p0stat[:, c:c + 1], p0w[lp][:, :], AX.X,
                                     apply_absolute_value=True)
                        v.engine_nop().then_inc(P0V.h, 1)
                        P0V.n += 1
                    p0_state["slot_free"][lp] = ("P0V", P0V.n)
                else:
                    nsp = p0_state["nsp"]

                    @block.vector
                    def _(v, lp=lp, midx=midx, lv=P0L[lp].n, nsp=nsp):
                        v.wait_ge(P0L[lp].h, lv)
                        if True:
                            v.wait_ge(P0M.h, midx + 1)
                        if nsp >= 2:
                            v.wait_ge(P0SP[nsp % 2].h, 16 * (nsp // 2))
                        c2b = [p0c2, p0c1][nsp % 2]
                        v.tensor_scalar(c2b[:, :], p0w[lp][:, :],
                                        p0thr[:, 1:2], None, ALU.is_le)
                        v.drain()
                        v.scalar_tensor_tensor(p0q[nsp % 2][:, :], p0w[lp][:, :],
                                               p0thr[:, 0:1], c2b[:, :],
                                               ALU.is_ge, ALU.subtract)
                        v.engine_nop().then_inc(P0V.h, 1)
                        P0V.n += 1
                    p0_state["slot_free"][lp] = ("P0V", P0V.n)

                    @block.gpsimd
                    def _(g, mname=mname, rt=rt, cc=cc, nsp=nsp, nv=P0V.n):
                        g.wait_ge(P0V.h, nv)
                        dma(g, P0SP[nsp % 2],
                            wq_sp[mname][rt * 128:(rt + 1) * 128,
                                         cc * 512:(cc + 1) * 512],
                            p0q[nsp % 2][:, :])
                    p0_state["nsp"] += 1
            elif kind == "mean":
                nch = nrt * ncc

                @block.vector
                def _(v, nch=nch, nv=P0V.n):
                    v.wait_ge(P0V.h, nv)
                    v.reduce_sum(p0stat[:, 63:64], p0stat[:, 0:nch], AX.X)
                    v.engine_nop().then_inc(P0V.h, 1)
                    P0V.n += 1

                @block.gpsimd
                def _(g, midx=midx, nv=P0V.n):
                    g.wait_ge(P0V.h, nv)
                    dma(g, P0D, mean_sp[midx:midx + 1, :],
                        p0stat[:, 63:64], slow=True)
                    g.wait_ge(P0D.h, P0D.n)
                    dma(g, P0D, p0row[:, :], mean_sp[midx:midx + 1, :])
                    g.wait_ge(P0D.h, P0D.n)
                    g.engine_nop().then_inc(P0G.h, 1)
                    P0G.n += 1

                @block.vector
                def _(v, midx=midx, inv_size=inv_size, gv=P0G.n):
                    v.wait_ge(P0G.h, gv)
                    v.reduce_sum(p0stat[0:1, 0:8],
                                 p0row[:, :].rearrange("o (a b) -> o a b", b=16),
                                 AX.X)
                    v.drain()
                    v.reduce_sum(p0stat[0:1, 8:9], p0stat[0:1, 0:8], AX.X)
                    v.drain()
                    v.tensor_scalar(p0stat[0:1, 9:10], p0stat[0:1, 8:9],
                                    inv_size, 1e-5, ALU.mult, ALU.max)
                    v.engine_nop().then_inc(P0V.h, 1)
                    P0V.n += 1

                @block.gpsimd
                def _(g, midx=midx, nv=P0V.n):
                    g.wait_ge(P0V.h, nv)
                    dma(g, P0D, mean_sp[midx:midx + 1, 0:1], p0stat[0:1, 9:10])
                    g.wait_ge(P0D.h, P0D.n)
                    dma(g, P0D, m_cols[:, midx:midx + 1],
                        bass.AP(mean_sp, midx * 128, [[0, 128], [1, 1]]))
                    g.wait_ge(P0D.h, P0D.n)
                    g.engine_nop().then_inc(P0G.h, 1)
                    P0G.n += 1

                @block.vector
                def _(v, midx=midx, gv=P0G.n):
                    v.wait_ge(P0G.h, gv)
                    v.tensor_scalar(p0thr[:, 0:1], m_cols[:, midx:midx + 1],
                                    0.5, None, ALU.mult)
                    v.tensor_scalar(p0thr[:, 1:2], m_cols[:, midx:midx + 1],
                                    -0.5, None, ALU.mult)
                    v.engine_nop().then_inc(P0M.h, 1)
                    P0M.n += 1
            else:
                @block.vector
                def _(v, a=P0SP[0].n, b=P0SP[1].n):
                    v.wait_ge(P0SP[0].h, a)
                    v.wait_ge(P0SP[1].h, b)
                    v.engine_nop().then_inc(P0V.h, 1)
                    P0V.n += 1
        p0_state["i"] = i0 + n_tasks

    P0_PER_TILE = (len(p0_tasks) + NTILE - 1) // NTILE

    for ti in range(NTILE):
        t0 = ti * TILE
        par = ti % 2
        batch_start = (t0 % T) == 0

        @block.vector
        def _(v, ti=ti, par=par):
            v.wait_ge(XL[par].h, x_after[ti])
            for ss in range(SUB):
                v.reduce_sum(st_s1[:, ss:ss + 1],
                             x_t[par][:, ss * D:(ss + 1) * D], AX.X)
            fence(v, VSC)

        @block.scalar
        def _(s, ti=ti, par=par):
            s.wait_ge(XL[par].h, x_after[ti])
            for ss in range(SUB):
                if ss:
                    s.drain()
                s.activation(sq_scr[:, :], x_t[par][:, ss * D:(ss + 1) * D],
                             AF.Square, bias=0.0, scale=1.0,
                             accum_out=st_s2[:, ss:ss + 1])
            fence(s, SSC)

        @block.vector
        def _(v, ti=ti):
            v.wait_ge(SSC.h, 13 * ti + 1)
            v.tensor_scalar(st_mu[:, :], st_s1[:, :], 1.0 / D, None, ALU.mult)
            v.drain()
            v.scalar_tensor_tensor(st_t1[:, :], st_mu[:, :], -1.0, st_mu[:, :],
                                   ALU.mult, ALU.mult)
            v.drain()
            v.scalar_tensor_tensor(st_var[:, :], st_s2[:, :], 1.0 / D, st_t1[:, :],
                                   ALU.mult, ALU.add)
            fence(v, VSC)

        @block.scalar
        def _(s, ti=ti):
            s.wait_ge(VSC.h, 15 * ti + 2)
            s.activation(st_t1[:, :], st_var[:, :], AF.Sqrt, bias=c_eps5[:, :],
                         scale=1.0)
            fence(s, SSC)

        @block.vector
        def _(v, ti=ti):
            v.wait_ge(SSC.h, 13 * ti + 2)
            v.reciprocal(st_r[:, :], st_t1[:, :])
            v.drain()
            v.scalar_tensor_tensor(st_nmr[:, :], st_mu[:, :], -1.0, st_r[:, :],
                                   ALU.mult, ALU.mult)
            fence(v, VSC)

        @block.scalar
        def _(s, ti=ti, par=par):
            s.wait_ge(VSC.h, 15 * ti + 3)
            s.wait_ge(XNS.h, 64 * ti)
            for ss in range(SUB):
                s.activation(xn_tm[:, ss * D:(ss + 1) * D],
                             x_t[par][:, ss * D:(ss + 1) * D], AF.Identity,
                             bias=st_nmr[:, ss:ss + 1], scale=st_r[:, ss:ss + 1])
            fence(s, SSC)

        @block.sync
        def _(sync, ti=ti, par=par, t0=t0):
            sync.wait_ge(SSC.h, 13 * ti + 3)
            for ss in range(SUB):
                dma(sync, XNS, xn_sp[t0 + ss * 128:t0 + (ss + 1) * 128, :],
                    xn_tm[:, ss * D:(ss + 1) * D])
            sync.wait_ge(XNS.h, XNS.n)
            if ti >= 2:
                sync.wait_ge(TSC.h, 20 * (ti - 1))
            for c in range(DCH):
                dma(sync, XNT[par], xn_fm[par][c][:, :],
                    xn_sp[t0:t0 + TILE, c * 128:(c + 1) * 128], transpose=True)

        for m in range(DCH):
            @block.tensor
            def _(t, ti=ti, par=par, m=m):
                if m == 0:
                    t.wait_ge(XNT[par].h, XNT[par].n)
                    if ti == 0:
                        t.wait_ge(WL[0].h, 256)
                if m >= 2:
                    t.wait_ge(SSC.h, 13 * ti + (m - 2) + 4)
                    t.wait_ge(VSC.h, 15 * ti + (m - 2) + 4)
                elif ti >= 1:
                    t.wait_ge(SSC.h, 13 * (ti - 1) + (m + 6) + 4)
                    t.wait_ge(VSC.h, 15 * (ti - 1) + (m + 6) + 4)
                for k in range(DCH):
                    i1 = t.matmul(pg[m % 2][:, :],
                                  wg_sb[k][:, m * 128:(m + 1) * 128],
                                  xn_fm[par][k][:, :],
                                  start=(k == 0), stop=(k == DCH - 1))
                i1.then_inc(TSC.h, 1)
                TSC.n += 1
                for k in range(DCH):
                    i2 = t.matmul(pi_[m % 2][:, :],
                                  wi_sb[k][:, m * 128:(m + 1) * 128],
                                  xn_fm[par][k][:, :],
                                  start=(k == 0), stop=(k == DCH - 1))
                i2.then_inc(TSC.h, 1)
                TSC.n += 1

            @block.scalar
            def _(s, ti=ti, m=m):
                s.wait_ge(TSC.h, 20 * ti + 2 * m + 1)
                if m == 0 and ti >= 1:
                    s.wait_ge(VSC.h, 15 * (ti - 1) + 12)
                s.activation(gate_t[m][:, :], pg[m % 2][:, :], AF.Silu,
                             bias=cols["bg"][:, m:m + 1],
                             scale=1.0).then_inc(SSC.h, 1)
                SSC.n += 1

            @block.vector
            def _(v, ti=ti, m=m):
                v.wait_ge(TSC.h, 20 * ti + 2 * m + 2)
                v.tensor_scalar(ui_t[m][:, :], pi_[m % 2][:, :],
                                cols["bi"][:, m:m + 1], None,
                                ALU.add).then_inc(VSC.h, 1)
                VSC.n += 1

        emit_p0(P0_PER_TILE // 3)

        @block.vector
        def _(v, ti=ti, batch_start=batch_start):
            v.drain()
            v.wait_ge(SSC.h, 13 * ti + 11)
            for m in range(DCH):
                v.tensor_tensor(gate_t[m][:, :], gate_t[m][:, :], ui_t[m][:, :],
                                ALU.mult)
            v.drain()
            for m in range(DCH):
                v.tensor_scalar(ui_t[m][:, :], gate_t[m][:, :],
                                cpbp[:, m:m + 1], None, ALU.mult)
            v.drain()
            if batch_start:
                v.memset(carry[:, :], 0.0)
                v.drain()
            if ti == 0:
                v.wait_ge(GSC.h, 1) if False else None
            for m in range(DCH):
                v.tensor_tensor_scan(yc_t[m][:, :], a_bc[m], ui_t[m][:, :],
                                     carry[:, m:m + 1], ALU.mult, ALU.add)
            v.drain()
            for m in range(DCH):
                v.tensor_copy(carry[:, m:m + 1], yc_t[m][:, TILE - 1:TILE])
            v.drain()
            for m in range(DCH):
                v.scalar_tensor_tensor(y_t[m][:, :], gate_t[m][:, :],
                                       cols["Dp"][:, m:m + 1], yc_t[m][:, :],
                                       ALU.mult, ALU.add)
            fence(v, VSC)

        emit_p0(P0_PER_TILE // 3)

        @block.scalar
        def _(s, ti=ti):
            s.wait_ge(VSC.h, 15 * ti + 12)
            if ti >= 1:
                s.wait_ge(TSC.h, 20 * (ti - 1) + 18)
            for m in range(DCH):
                s.activation(ysq_t[m][:, :], y_t[m][:, :], AF.Square,
                             bias=0.0, scale=1.0)
            fence(s, SSC)

        @block.tensor
        def _(t, ti=ti):
            t.wait_ge(VSC.h, 15 * ti + 12)
            if ti >= 1:
                t.wait_ge(VSC.h, 15 * (ti - 1) + 13)
            for m in range(DCH):
                i1 = t.matmul(ps1r[:, :], onescol_f[:, :], y_t[m][:, :],
                              start=(m == 0), stop=(m == DCH - 1))
            i1.then_inc(TSC.h, 1)
            TSC.n += 1

        @block.tensor
        def _(t, ti=ti):
            t.wait_ge(SSC.h, 13 * ti + 12)
            for m in range(DCH):
                i2 = t.matmul(ps2r[:, :], onescol_b[:, :], ysq_t[m][:, :],
                              start=(m == 0), stop=(m == DCH - 1))
            i2.then_inc(TSC.h, 1)
            TSC.n += 1

        @block.vector
        def _(v, ti=ti):
            v.wait_ge(TSC.h, 20 * ti + 18)
            v.tensor_scalar(r2_sc[:, 0:TILE], ps1r[:, :], 1.0 / D, None, ALU.mult)
            v.tensor_scalar(r2_sc[:, TILE:2 * TILE], ps2r[:, :], 1.0 / D, None,
                            ALU.mult)
            v.drain()
            v.tensor_tensor(r2_sc[:, 2 * TILE:3 * TILE], r2_sc[:, 0:TILE],
                            r2_sc[:, 0:TILE], ALU.mult)
            v.drain()
            v.tensor_tensor(r2_sc[:, 3 * TILE:4 * TILE], r2_sc[:, TILE:2 * TILE],
                            r2_sc[:, 2 * TILE:3 * TILE], ALU.subtract)
            fence(v, VSC)

        @block.scalar
        def _(s, ti=ti):
            s.wait_ge(VSC.h, 15 * ti + 13)
            s.activation(r2_sc[:, 2 * TILE:3 * TILE], r2_sc[:, 3 * TILE:4 * TILE],
                         AF.Sqrt, bias=c_eps3[0:1, :], scale=1.0)
            fence(s, SSC)

        @block.vector
        def _(v, ti=ti):
            v.wait_ge(SSC.h, 13 * ti + 13)
            v.reciprocal(r2_sc[:, TILE:2 * TILE], r2_sc[:, 2 * TILE:3 * TILE])
            v.drain()
            v.scalar_tensor_tensor(r2_sc[:, 2 * TILE:3 * TILE], r2_sc[:, 0:TILE],
                                   -1.0, r2_sc[:, TILE:2 * TILE],
                                   ALU.mult, ALU.mult)
            fence(v, VSC)

        @block.tensor
        def _(t, ti=ti):
            t.wait_ge(VSC.h, 15 * ti + 14)
            if ti >= 1:
                t.wait_ge(VSC.h, 15 * (ti - 1) + 15)
            i1 = t.matmul(pR[:, :], ones_k1[:, :], r2_sc[:, TILE:2 * TILE],
                          start=True, stop=True)
            i1.then_inc(TSC.h, 1)
            TSC.n += 1
            i2 = t.matmul(pNM[:, :], ones_k1[:, :], r2_sc[:, 2 * TILE:3 * TILE],
                          start=True, stop=True)
            i2.then_inc(TSC.h, 1)
            TSC.n += 1

        @block.vector
        def _(v, ti=ti):
            v.wait_ge(TSC.h, 20 * ti + 20)
            v.wait_ge(YNS.h, 128 * ti)
            for m in range(DCH):
                v.scalar_tensor_tensor(yc_t[m][:, :], y_t[m][:, :], 1.0,
                                       pR[:, :], ALU.mult, ALU.mult)
            v.drain()
            for m in range(DCH):
                v.tensor_tensor(yn_t[m][:, :], yc_t[m][:, :], pNM[:, :], ALU.add)
            fence(v, VSC)

        @block.sync
        def _(sync, ti=ti, t0=t0):
            sync.wait_ge(VSC.h, 15 * ti + 15)
            for m in range(DCH):
                dma(sync, YNS, yn_sp[m * 128:(m + 1) * 128, t0:t0 + TILE],
                    yn_t[m][:, :])

        if ti + 2 < NTILE:
            emit_x_loads(ti + 2)
        emit_p0(P0_PER_TILE - 2 * (P0_PER_TILE // 3))

    @block.sync
    def _(sync):
        wait_all(sync, VS, SS, GS, TS_, DS, SD, GD)
        wait_all(sync, WL[0], WL[1], XL[0], XL[1], XNT[0], XNT[1], XNS, YNS,
                 TSC, VSC, SSC, GSC)
        wait_all(sync, P0L[0], P0L[1], P0L[2], P0V, P0G, P0SP[0], P0SP[1], P0D, P0M)
        sync.nop().then_inc(PB.h, 1)
    PB.n += 1
    p1.close()

    p2 = ExitStack()
    wfg_sb = [p2.enter_context(nc.sbuf_tensor(f"wfg{k}", [128, D], BF16)) for k in range(DCH)]
    wfu_sb = [p2.enter_context(nc.sbuf_tensor(f"wfu{k}", [128, D], BF16)) for k in range(DCH)]
    wfd_sb = [p2.enter_context(nc.sbuf_tensor(f"wfd{k}", [128, D], BF16)) for k in range(DCH)]
    wo_sb = [p2.enter_context(nc.sbuf_tensor(f"wo{k}", [128, D], BF16)) for k in range(DCH)]
    wf32b = [p2.enter_context(nc.sbuf_tensor(f"wf32b{p}", [128, D], F32))
             for p in range(2)]
    WLB = [SemN("WLB0"), SemN("WLB1")]
    WPV = SemN("WPV")
    WPT = SemN("WPT")
    yn_in = [[p2.enter_context(nc.sbuf_tensor(f"ynin{p}_{c}", [128, TILE], BF16))
              for c in range(DCH)] for p in range(2)]
    big_a = p2.enter_context(nc.sbuf_tensor("big_a", [128, SUB * D], F32))
    big_b = p2.enter_context(nc.sbuf_tensor("big_b", [128, SUB * D], F32))
    hg_t = [big_a[:, c * TILE:(c + 1) * TILE] for c in range(DCH)]
    hu_t = [big_b[:, c * TILE:(c + 1) * TILE] for c in range(DCH)]
    h_t = [p2.enter_context(nc.sbuf_tensor(f"h{c}", [128, TILE], BF16)) for c in range(DCH)]
    fd_t = [p2.enter_context(nc.sbuf_tensor(f"fd{c}", [128, TILE], BF16)) for c in range(DCH)]
    xre_t = p2.enter_context(nc.sbuf_tensor("xre", [128, SUB * D], F32))
    x2_t = [p2.enter_context(nc.sbuf_tensor(f"x2t{p}", [128, SUB * D], F32))
            for p in range(2)]
    xn3_t = big_a
    tq_t = big_b
    mq_t = big_a
    xq_t = p2.enter_context(nc.sbuf_tensor("xq", [128, SUB * D], BF16))
    sq_b = big_b[:, 0:D]
    s1_b = p2.enter_context(nc.sbuf_tensor("s1_b", [128, SUB], F32))
    s2_b = p2.enter_context(nc.sbuf_tensor("s2_b", [128, SUB], F32))
    mu_b = p2.enter_context(nc.sbuf_tensor("mu_b", [128, SUB], F32))
    t1_b = p2.enter_context(nc.sbuf_tensor("t1_b", [128, SUB], F32))
    var_b = p2.enter_context(nc.sbuf_tensor("var_b", [128, SUB], F32))
    r_b = p2.enter_context(nc.sbuf_tensor("r_b", [128, SUB], F32))
    nmr_b = p2.enter_context(nc.sbuf_tensor("nmr_b", [128, SUB], F32))
    mx_b = p2.enter_context(nc.sbuf_tensor("mx_b", [128, SUB], F32))
    pfg = [p2.enter_context(nc.psum_tensor(f"pfg{p}", [128, TILE], F32)) for p in range(3)]
    pfu = [p2.enter_context(nc.psum_tensor(f"pfu{p}", [128, TILE], F32)) for p in range(3)]
    pot = [p2.enter_context(nc.psum_tensor(f"pot{p}", [128, 512], F32)) for p in range(2)]
    pbias = pot[0][:, 0:2 * DCH]
    YNL = [SemN("YNL0"), SemN("YNL1")]
    X2P = [SemN("X2P0"), SemN("X2P1")]
    XQS = SemN("XQS")
    XRL = SemN("XRL")
    TSB = SemN("TSB")
    VSB = SemN("VSB")
    SSB = SemN("SSB")
    GSB = SemN("GSB")

    WCB = SemN("WCB")
    WCO = SemN("WCO")
    @block.gpsimd
    def _(g):
        g.wait_ge(PB.h, PB.n)
        for k in range(DCH):
            dma(g, WCB, wfd_sb[k][:, :], w_d["Wfd"][k * 128:(k + 1) * 128, :])
        for k in range(DCH):
            dma(g, WCO, wo_sb[k][:, :], w_d["Wo"][k * 128:(k + 1) * 128, :])

    wlist_b = ([("Wfg", wfg_sb, True, k) for k in range(DCH)]
               + [("Wfu", wfu_sb, True, k) for k in range(DCH)])
    wlb_after = {}
    for j, (wname, sb, scale_gamma, k) in enumerate(wlist_b):
        @block.sync
        def _(sync, j=j, wname=wname, k=k):
            if j < 2:
                sync.wait_ge(PB.h, PB.n)
            else:
                sync.wait_ge(WPV.h, j - 1)
                sync.wait_ge(WPT.h, j - 1)
            dma(sync, WLB[j % 2], wf32b[j % 2][:, :],
                w_d[wname][k * 128:(k + 1) * 128, :])
            wlb_after[j] = WLB[j % 2].n

        if scale_gamma:
            @block.tensor
            def _(t, j=j, k=k):
                t.wait_ge(WLB[j % 2].h, wlb_after[j])
                base = (j // DCH) * DCH
                for m in range(DCH):
                    t.matmul(pbias[:, base + m:base + m + 1],
                             wf32b[j % 2][:, m * 128:(m + 1) * 128],
                             cols["beta"][:, k:k + 1],
                             start=(j == 0 and m == 0),
                             stop=(j == 15 and m == DCH - 1),
                             skip_group_check=True)
                fence(t, WPT)

        @block.vector
        def _(v, j=j, sb=sb, k=k, scale_gamma=scale_gamma):
            v.wait_ge(WLB[j % 2].h, wlb_after[j])
            if scale_gamma:
                v.tensor_scalar(sb[k][:, :], wf32b[j % 2][:, :],
                                cols["gamma"][:, k:k + 1], None, ALU.mult)
            else:
                v.tensor_copy(sb[k][:, :], wf32b[j % 2][:, :])
            v.engine_nop().then_inc(WPV.h, 1)
            WPV.n += 1

    @block.vector
    def _(v):
        v.wait_ge(WPT.h, 16)
        v.tensor_tensor(bfgp[:, :], pbias[:, 0:DCH], cols["bfg"][:, :], ALU.add)
        v.tensor_tensor(bfup[:, :], pbias[:, DCH:2 * DCH], cols["bfu"][:, :],
                        ALU.add)
        fence(v, VS)
    bias_ready = VS.n

    yn_after = {}
    yn_after = {}

    def emit_yn_loads(ti):
        par = ti % 2
        t0 = ti * TILE

        @block.sync
        def _(sync, ti=ti, par=par, t0=t0):
            if ti >= 2:
                sync.wait_ge(TSB.h, 32 * (ti - 2) + 16)
            for c in range(DCH):
                dma(sync, YNL[par], yn_in[par][c][:, :],
                    yn_sp[c * 128:(c + 1) * 128, t0:t0 + TILE])
            yn_after[ti] = YNL[par].n

    emit_yn_loads(0)
    emit_yn_loads(1)

    @block.sync
    def _(sync):
        for ss in range(SUB):
            dma(sync, XRL, xre_t[:, ss * D:(ss + 1) * D],
                xr[ss * 128:(ss + 1) * 128, :])

    for ti in range(NTILE):
        t0 = ti * TILE
        par = ti % 2

        for m in range(DCH):
            @block.tensor
            def _(t, ti=ti, par=par, m=m):
                if m == 0:
                    t.wait_ge(YNL[par].h, yn_after[ti])
                    if ti == 0:
                        t.wait_ge(WPV.h, len(wlist_b))
                if m >= 3:
                    t.wait_ge(SSB.h, 20 * ti + (m - 3) + 1)
                    t.wait_ge(VSB.h, 22 * ti + (m - 3) + 1)
                elif ti >= 1:
                    t.wait_ge(SSB.h, 20 * (ti - 1) + 8 + {0: 6, 1: 7, 2: 5}[m] + 1)
                for k in range(DCH):
                    i1 = t.matmul(pfg[m % 3][:, :],
                                  wfg_sb[k][:, m * 128:(m + 1) * 128],
                                  yn_in[par][k][:, :],
                                  start=(k == 0), stop=(k == DCH - 1))
                i1.then_inc(TSB.h, 1)
                TSB.n += 1
                for k in range(DCH):
                    i2 = t.matmul(pfu[m % 3][:, :],
                                  wfu_sb[k][:, m * 128:(m + 1) * 128],
                                  yn_in[par][k][:, :],
                                  start=(k == 0), stop=(k == DCH - 1))
                i2.then_inc(TSB.h, 1)
                TSB.n += 1

            @block.scalar
            def _(s, ti=ti, m=m):
                s.wait_ge(TSB.h, 32 * ti + 2 * m + 1)
                if ti == 0 and m == 0:
                    s.wait_ge(VS.h, bias_ready)
                s.activation(hg_t[m], pfg[m % 3][:, :], AF.Silu,
                             bias=bfgp[:, m:m + 1], scale=1.0).then_inc(SSB.h, 1)
                SSB.n += 1

            @block.vector
            def _(v, ti=ti, m=m):
                v.wait_ge(TSB.h, 32 * ti + 2 * m + 2)
                if ti == 0 and m == 0:
                    v.wait_ge(VS.h, bias_ready)
                v.tensor_scalar(hu_t[m], pfu[m % 3][:, :], bfup[:, m:m + 1],
                                None, ALU.add).then_inc(VSB.h, 1)
                VSB.n += 1

        @block.vector
        def _(v, ti=ti):
            v.drain()
            v.wait_ge(SSB.h, 20 * ti + 8)
            v.wait_ge(TSB.h, 32 * ti)
            for m in range(DCH):
                v.tensor_tensor(h_t[m][:, :], hg_t[m], hu_t[m], ALU.mult)
            fence(v, VSB)

        for m in range(DCH):
            @block.tensor
            def _(t, ti=ti, m=m):
                if m == 0:
                    t.wait_ge(VSB.h, 22 * ti + 9)
                    if ti == 0:
                        t.wait_ge(WCB.h, 128)
                if m >= 3:
                    t.wait_ge(SSB.h, 20 * ti + 8 + (m - 3) + 1)
                else:
                    t.wait_ge(SSB.h, 20 * ti + {0: 6, 1: 7, 2: 5}[m] + 1)
                for k in range(DCH):
                    i1 = t.matmul(pfg[m % 3][:, :],
                                  wfd_sb[k][:, m * 128:(m + 1) * 128],
                                  h_t[k][:, :],
                                  start=(k == 0), stop=(k == DCH - 1))
                i1.then_inc(TSB.h, 1)
                TSB.n += 1

            @block.scalar
            def _(s, ti=ti, m=m):
                s.wait_ge(TSB.h, 32 * ti + 16 + m + 1)
                s.activation(fd_t[m][:, :], pfg[m % 3][:, :], AF.Identity,
                             bias=cols["bfd"][:, m:m + 1],
                             scale=1.0).then_inc(SSB.h, 1)
                SSB.n += 1

        for g in range(8):
            ss_, nh = divmod(g, 2)

            @block.tensor
            def _(t, ti=ti, g=g, ss_=ss_, nh=nh):
                if g == 0:
                    t.wait_ge(SSB.h, 20 * ti + 16)
                    if ti == 0:
                        t.wait_ge(WCO.h, 128)
                if g >= 2:
                    t.wait_ge(VSB.h, 22 * ti + 10 + (g - 2))
                elif ti >= 1:
                    t.wait_ge(VSB.h, 22 * (ti - 1) + 10 + (g + 6))
                for k in range(DCH):
                    i1 = t.matmul(pot[g % 2][:, :],
                                  fd_t[k][:, ss_ * 128:(ss_ + 1) * 128],
                                  wo_sb[k][:, nh * 512:(nh + 1) * 512],
                                  start=(k == 0), stop=(k == DCH - 1))
                i1.then_inc(TSB.h, 1)
                TSB.n += 1

            @block.vector
            def _(v, ti=ti, par=par, g=g, ss_=ss_, nh=nh):
                v.wait_ge(TSB.h, 32 * ti + 24 + g + 1)
                if g == 0:
                    if ti >= 2:
                        v.wait_ge(X2P[par].h, 64 * (ti // 2))
                        v.wait_ge(SSB.h, 20 * (ti - 1) + 19)
                v.scalar_tensor_tensor(
                    x2_t[par][:, ss_ * D + nh * 512:ss_ * D + (nh + 1) * 512],
                    pot[g % 2][:, :], 1.0, bo_bc[:, nh * 512:(nh + 1) * 512],
                    ALU.mult, ALU.add).then_inc(VSB.h, 1)
                VSB.n += 1

        @block.gpsimd
        def _(g_, ti=ti, par=par):
            g_.wait_ge(VSB.h, 22 * ti + 17)
            g_.wait_ge(XRL.h, 64 * (ti + 1))
            for ss in range(SUB):
                g_.tensor_tensor(x2_t[par][:, ss * D:(ss + 1) * D],
                                 x2_t[par][:, ss * D:(ss + 1) * D],
                                 xre_t[:, ss * D:(ss + 1) * D], ALU.add)
            fence(g_, GSB)

        @block.sync
        def _(sync, ti=ti, par=par, t0=t0):
            if ti + 1 < NTILE:
                sync.wait_ge(GSB.h, ti + 1)
                for ss in range(SUB):
                    dma(sync, XRL, xre_t[:, ss * D:(ss + 1) * D],
                        xr[t0 + TILE + ss * 128:t0 + TILE + (ss + 1) * 128, :])
            sync.wait_ge(GSB.h, ti + 1)
            for ss in range(SUB):
                dma(sync, X2P[par], x2_sp[t0 + ss * 128:t0 + (ss + 1) * 128, :],
                    x2_t[par][:, ss * D:(ss + 1) * D])

        @block.vector
        def _(v, ti=ti, par=par):
            v.wait_ge(GSB.h, ti + 1)
            for ss in range(SUB):
                v.reduce_sum(s1_b[:, ss:ss + 1], x2_t[par][:, ss * D:(ss + 1) * D],
                             AX.X)
            fence(v, VSB)

        @block.scalar
        def _(s, ti=ti, par=par):
            s.wait_ge(GSB.h, ti + 1)
            for ss in range(SUB):
                if ss:
                    s.drain()
                s.activation(sq_b, x2_t[par][:, ss * D:(ss + 1) * D],
                             AF.Square, bias=0.0, scale=1.0,
                             accum_out=s2_b[:, ss:ss + 1])
            fence(s, SSB)

        @block.vector
        def _(v, ti=ti):
            v.wait_ge(SSB.h, 20 * ti + 17)
            v.tensor_scalar(mu_b[:, :], s1_b[:, :], 1.0 / D, None, ALU.mult)
            v.drain()
            v.scalar_tensor_tensor(t1_b[:, :], mu_b[:, :], -1.0, mu_b[:, :],
                                   ALU.mult, ALU.mult)
            v.drain()
            v.scalar_tensor_tensor(var_b[:, :], s2_b[:, :], 1.0 / D, t1_b[:, :],
                                   ALU.mult, ALU.add)
            fence(v, VSB)

        @block.scalar
        def _(s, ti=ti):
            s.wait_ge(VSB.h, 22 * ti + 19)
            s.activation(t1_b[:, :], var_b[:, :], AF.Sqrt, bias=c_eps5[:, :],
                         scale=1.0)
            fence(s, SSB)

        @block.vector
        def _(v, ti=ti):
            v.wait_ge(SSB.h, 20 * ti + 18)
            v.reciprocal(r_b[:, :], t1_b[:, :])
            v.drain()
            v.scalar_tensor_tensor(nmr_b[:, :], mu_b[:, :], -1.0, r_b[:, :],
                                   ALU.mult, ALU.mult)
            fence(v, VSB)

        @block.scalar
        def _(s, ti=ti, par=par):
            s.wait_ge(VSB.h, 22 * ti + 20)
            for ss in range(SUB):
                s.activation(xn3_t[:, ss * D:(ss + 1) * D],
                             x2_t[par][:, ss * D:(ss + 1) * D], AF.Identity,
                             bias=nmr_b[:, ss:ss + 1], scale=r_b[:, ss:ss + 1])
            fence(s, SSB)

        @block.vector
        def _(v, ti=ti):
            v.wait_ge(SSB.h, 20 * ti + 19)
            for ss in range(SUB):
                v.reduce_max(mx_b[:, ss:ss + 1], xn3_t[:, ss * D:(ss + 1) * D],
                             AX.X, apply_absolute_value=True)
            v.drain()
            v.tensor_scalar(mx_b[:, :], mx_b[:, :], 1e-5, None, ALU.max)
            v.drain()
            v.reciprocal(t1_b[:, :], mx_b[:, :])
            v.drain()
            v.tensor_scalar(s3_cols[:, ti * SUB:(ti + 1) * SUB], t1_b[:, :],
                            127.0, None, ALU.mult)
            v.drain()
            v.tensor_scalar(t1_b[:, :], mx_b[:, :], 1.0 / 127.0, None, ALU.mult)
            v.drain()
            v.tensor_scalar(e_gs[:, ti * SUB:(ti + 1) * SUB], t1_b[:, :],
                            m_cols[:, 0:1], None, ALU.mult)
            v.tensor_scalar(e_us[:, ti * SUB:(ti + 1) * SUB], t1_b[:, :],
                            m_cols[:, 1:2], None, ALU.mult)
            fence(v, VSB)

        @block.scalar
        def _(s, ti=ti):
            s.wait_ge(VSB.h, 22 * ti + 21)
            for ss in range(SUB):
                s.activation(tq_t[:, ss * D:(ss + 1) * D],
                             xn3_t[:, ss * D:(ss + 1) * D], AF.Identity,
                             bias=(c_1275[:, :] if ROUND_MODE == "ladder"
                                   else c_magic[:, :]),
                             scale=s3_cols[:, ti * SUB + ss:ti * SUB + ss + 1])
            fence(s, SSB)

        @block.vector
        def _(v, ti=ti):
            v.wait_ge(SSB.h, 20 * ti + 20)
            v.wait_ge(XQS.h, 64 * ti)
            if ROUND_MODE == "ladder":
                emit_round_ladder(v, tq_t[:, :], mq_t[:, :], xq_t[:, :], 127.0)
            else:
                v.tensor_scalar(xq_t[:, :], tq_t[:, :], MAGIC, None,
                                ALU.subtract)
            fence(v, VSB)

        @block.sync
        def _(sync, ti=ti, t0=t0):
            sync.wait_ge(VSB.h, 22 * ti + 22)
            for ss in range(SUB):
                dma(sync, XQS, xq_sp[t0 + ss * 128:t0 + (ss + 1) * 128, :],
                    xq_t[:, ss * D:(ss + 1) * D])

        if ti + 2 < NTILE:
            emit_yn_loads(ti + 2)

    @block.sync
    def _(sync):
        wait_all(sync, VS, SS, GS, TS_, DS, SD, GD)
        wait_all(sync, YNL[0], YNL[1], X2P[0], X2P[1], XQS, XRL,
                 TSB, VSB, SSB, GSB, WCB, WCO)
        sync.nop().then_inc(PB.h, 1)
    PB.n += 1
    p2.close()

    p3 = ExitStack()
    wqg_sb = [p3.enter_context(nc.sbuf_tensor(f"wqg{k}", [128, H], BF16)) for k in range(DCH)]
    wqu_sb = [p3.enter_context(nc.sbuf_tensor(f"wqu{k}", [128, H], BF16)) for k in range(DCH)]
    xqf = [[p3.enter_context(nc.sbuf_tensor(f"xqf{p}_{k}", [128, TILE], BF16))
            for k in range(DCH)] for p in range(2)]
    sg_m = p3.enter_context(nc.sbuf_tensor("sg_m", [128, H], F32))
    h_m = p3.enter_context(nc.sbuf_tensor("h_m", [128, H], F32))
    hq_m = [p3.enter_context(nc.sbuf_tensor(f"hq_m{p}", [128, H], BF16))
            for p in range(2)]
    mx2_b = p3.enter_context(nc.sbuf_tensor("mx2_b", [128, 2], F32))
    s4_b = p3.enter_context(nc.sbuf_tensor("s4_b", [128, 2], F32))
    t2_b = p3.enter_context(nc.sbuf_tensor("t2_b", [128, 2], F32))
    pga = [p3.enter_context(nc.psum_tensor(f"pga{p}", [128, 512], F32)) for p in range(4)]
    pua = [p3.enter_context(nc.psum_tensor(f"pua{p}", [128, 512], F32)) for p in range(4)]
    XQ = [SemN("XQ0"), SemN("XQ1")]
    HQS = [SemN("HQS0"), SemN("HQS1")]
    TSA = SemN("TSA")
    VSA = SemN("VSA")
    SSA = SemN("SSA")

    @block.sync
    def _(sync):
        for k in range(DCH):
            dma(sync, DS, wqg_sb[k][:, :], wq_sp["g"][k * 128:(k + 1) * 128, :])
            dma(sync, DS, wqu_sb[k][:, :], wq_sp["u"][k * 128:(k + 1) * 128, :])
        sync.wait_ge(DS.h, DS.n)

    xq_after = {}

    def emit_xq_loads(gi):
        par = gi % 2
        t0 = gi * TILE

        @block.sync
        def _(sync, gi=gi, par=par, t0=t0):
            if gi >= 2:
                sync.wait_ge(TSA.h, 2 * 8 * SUB * (gi - 1))
            for k in range(DCH):
                dma(sync, XQ[par], xqf[par][k][:, :],
                    xq_sp[t0:t0 + TILE, k * 128:(k + 1) * 128], transpose=True)
            xq_after[gi] = XQ[par].n

    emit_xq_loads(0)
    emit_xq_loads(1)

    SPC = 9 if ROUND_MODE == "ladder" else 8
    for ci in range(NCHUNK):
        gi = ci // SUB
        par = gi % 2
        so = (ci % SUB) * 128
        cpar = ci % 2
        t0 = ci * 128

        for ob in range(8):
            G = ci * 16 + ob * 2

            @block.tensor
            def _(t, ci=ci, gi=gi, par=par, so=so, ob=ob):
                if ob == 0 and ci % SUB == 0:
                    t.wait_ge(XQ[par].h, xq_after[gi])
                slot = ob % 4
                if ob >= 4:
                    t.wait_ge(SSA.h, SPC * ci + (ob - 4) + 1)
                    t.wait_ge(VSA.h, 10 * ci + (ob - 4) + 1)
                elif ci >= 1:
                    t.wait_ge(SSA.h, SPC * (ci - 1) + (ob + 4) + 1)
                    t.wait_ge(VSA.h, 10 * (ci - 1) + (ob + 4) + 1)
                for k in range(DCH):
                    i1 = t.matmul(pga[slot][:, :], xqf[par][k][:, so:so + 128],
                                  wqg_sb[k][:, ob * 512:(ob + 1) * 512],
                                  start=(k == 0), stop=(k == DCH - 1))
                    i2 = t.matmul(pua[slot][:, :], xqf[par][k][:, so:so + 128],
                                  wqu_sb[k][:, ob * 512:(ob + 1) * 512],
                                  start=(k == 0), stop=(k == DCH - 1))
                i1.then_inc(TSA.h, 1)
                TSA.n += 1
                i2.then_inc(TSA.h, 1)
                TSA.n += 1

            @block.scalar
            def _(s, ci=ci, ob=ob):
                slot = ob % 4
                s.wait_ge(TSA.h, 2 * (ci * 8 + ob) + 2)
                if ob == 0 and ci >= 1:
                    s.wait_ge(VSA.h, 10 * ci)
                s.activation(sg_m[:, ob * 512:(ob + 1) * 512], pga[slot][:, :],
                             AF.Silu, bias=0.0,
                             scale=e_gs[:, ci:ci + 1]).then_inc(SSA.h, 1)
                SSA.n += 1

            @block.vector
            def _(v, ci=ci, ob=ob):
                slot = ob % 4
                v.wait_ge(SSA.h, SPC * ci + ob + 1)
                v.wait_ge(TSA.h, 2 * (ci * 8 + ob) + 2)
                v.scalar_tensor_tensor(h_m[:, ob * 512:(ob + 1) * 512],
                                       pua[slot][:, :], e_us[:, ci:ci + 1],
                                       sg_m[:, ob * 512:(ob + 1) * 512],
                                       ALU.mult, ALU.mult).then_inc(VSA.h, 1)
                VSA.n += 1

        @block.vector
        def _(v, ci=ci, cpar=cpar):
            v.wait_ge(VSA.h, 10 * ci + 8)
            v.drain()
            v.reduce_max(mx2_b[:, cpar:cpar + 1], h_m[:, :], AX.X,
                         apply_absolute_value=True)
            v.drain()
            v.tensor_scalar(t2_b[:, cpar:cpar + 1], mx2_b[:, cpar:cpar + 1],
                            1e-5, 1.0 / 127.0, ALU.max, ALU.mult)
            v.drain()
            v.reciprocal(s4_b[:, cpar:cpar + 1], t2_b[:, cpar:cpar + 1])
            v.tensor_scalar(e_ds[:, ci:ci + 1], t2_b[:, cpar:cpar + 1],
                            m_cols[:, 2:3], None, ALU.mult)
            v.engine_nop().then_inc(VSA.h, 1)
            VSA.n += 1

        if ROUND_MODE == "ladder":
            @block.scalar
            def _(s, ci=ci, cpar=cpar):
                s.wait_ge(VSA.h, 10 * ci + 9)
                s.activation(sg_m[:, :], h_m[:, :], AF.Identity,
                             bias=c_1275[:, :],
                             scale=s4_b[:, cpar:cpar + 1]).then_inc(SSA.h, 1)
                SSA.n += 1

            @block.vector
            def _(v, ci=ci, cpar=cpar):
                v.wait_ge(SSA.h, SPC * ci + 9)
                if ci >= 2:
                    v.wait_ge(HQS[cpar].h, 16 * (ci // 2))
                emit_round_ladder(v, sg_m[:, :], h_m[:, :], hq_m[cpar][:, :], 127.0)
                v.engine_nop().then_inc(VSA.h, 1)
                VSA.n += 1
        else:
            @block.vector
            def _(v, ci=ci, cpar=cpar):
                v.drain()
                if ci >= 2:
                    v.wait_ge(HQS[cpar].h, 16 * (ci // 2))
                v.tensor_scalar(sg_m[:, :], h_m[:, :],
                                s4_b[:, cpar:cpar + 1], MAGIC,
                                ALU.mult, ALU.add)
                v.drain()
                v.tensor_scalar(hq_m[cpar][:, :], sg_m[:, :], MAGIC, None,
                                ALU.subtract).then_inc(VSA.h, 1)
                VSA.n += 1

        @block.sync
        def _(sync, ci=ci, cpar=cpar, t0=t0):
            sync.wait_ge(VSA.h, 10 * ci + 10)
            dma(sync, HQS[cpar], hq_sp[t0:t0 + 128, :], hq_m[cpar][:, :])

        if ci % SUB == 0 and ci // SUB + 2 < NTILE:
            emit_xq_loads(ci // SUB + 2)

    @block.sync
    def _(sync):
        sync.wait_ge(HQS[0].h, HQS[0].n)
        sync.wait_ge(HQS[1].h, HQS[1].n)
        wait_all(sync, VS, SS, GS, TS_, DS, SD, GD)
        wait_all(sync, XQ[0], XQ[1], TSA, VSA, SSA)
        sync.nop().then_inc(PB.h, 1)
    PB.n += 1
    p3.close()

    p4 = ExitStack()
    wqd_sb = [p4.enter_context(nc.sbuf_tensor(f"wqd{k}", [128, D], BF16)) for k in range(HCH)]
    hqf = [[p4.enter_context(nc.sbuf_tensor(f"hqf{p}_{k}", [128, TILE], BF16))
            for k in range(HCH)] for p in range(2)]
    x2re = [p4.enter_context(nc.sbuf_tensor(f"x2re{p}", [128, SUB * D], F32))
            for p in range(2)]
    o_t = [p4.enter_context(nc.sbuf_tensor(f"o_t{p}", [128, SUB * D], F32))
           for p in range(2)]
    pdn = [p4.enter_context(nc.psum_tensor(f"pdn{p}", [128, 512], F32))
           for p in range(4)]
    HQ = [SemN("HQ0"), SemN("HQ1")]
    HS = [SemN("HS0"), SemN("HS1")]
    X2 = [SemN("X20"), SemN("X21")]
    OG = SemN("OG")

    vs0 = VS.n
    ts0 = TS_.n
    gs0 = GS.n

    @block.sync
    def _(sync):
        for k in range(HCH):
            dma(sync, DS, wqd_sb[k][:, :], wq_sp["d"][k * 128:(k + 1) * 128, :])
        sync.wait_ge(DS.h, DS.n)

    hq_after = {}
    hs_after = {}
    x2_after = {}

    def emit_loads(ti):
        par = ti % 2
        t0 = ti * TILE

        @block.sync
        def _(sync, ti=ti, par=par, t0=t0):
            if ti < 2:
                sync.wait_ge(PB.h, PB.n)
            else:
                sync.wait_ge(TS_.h, ts0 + 8 * (ti - 1))
                sync.wait_ge(VS.h, vs0 + 8 * (ti - 1))
            for k in range(0, HCH, 1):
                dma(sync, HQ[par], hqf[par][k][:, :],
                    hq_sp[t0:t0 + TILE, k * 128:(k + 1) * 128], transpose=True)
            hq_after[ti] = HQ[par].n

        @block.scalar
        def _(s, ti=ti, par=par, t0=t0):
            if ti < 2:
                s.wait_ge(PB.h, PB.n)
            else:
                s.wait_ge(VS.h, vs0 + 8 * (ti - 1))
            for ss in range(SUB):
                dma(s, X2[par], x2re[par][:, ss * D:(ss + 1) * D],
                    x2_sp[t0 + ss * 128:t0 + (ss + 1) * 128, :])
            x2_after[ti] = X2[par].n
            hs_after[ti] = HS[par].n

    emit_loads(0)
    emit_loads(1)

    for ti in range(NTILE):
        par = ti % 2
        t0 = ti * TILE

        for ss in range(SUB):
            G = ti * 8 + 2 * ss

            @block.tensor
            def _(t, ti=ti, par=par, ss=ss, G=G):
                if ss == 0:
                    t.wait_ge(HQ[par].h, hq_after[ti])
                    t.wait_ge(HS[par].h, hs_after[ti])
                if G >= 3:
                    t.wait_ge(VS.h, vs0 + G - 2)
                for k in range(HCH):
                    i1 = t.matmul(pdn[G % 4][:, :],
                                  hqf[par][k][:, ss * 128:(ss + 1) * 128],
                                  wqd_sb[k][:, 0:512],
                                  start=(k == 0), stop=(k == HCH - 1))
                    i2 = t.matmul(pdn[(G + 1) % 4][:, :],
                                  hqf[par][k][:, ss * 128:(ss + 1) * 128],
                                  wqd_sb[k][:, 512:1024],
                                  start=(k == 0), stop=(k == HCH - 1))
                i1.then_inc(TS_.h, 1)
                TS_.n += 1
                i2.then_inc(TS_.h, 1)
                TS_.n += 1

            for nh in range(2):
                ci = ti * SUB + ss

                @block.vector
                def _(v, ti=ti, par=par, ss=ss, nh=nh, G=G, ci=ci):
                    v.wait_ge(TS_.h, ts0 + G + nh + 1)
                    if ss == 0 and nh == 0:
                        v.wait_ge(X2[par].h, x2_after[ti])
                        if ti >= 2:
                            v.wait_ge(GS.h, gs0 + (ti - 1))
                    v.scalar_tensor_tensor(
                        o_t[par][:, ss * D + nh * 512:ss * D + (nh + 1) * 512],
                        pdn[(G + nh) % 4][:, :], e_ds[:, ci:ci + 1],
                        x2re[par][:, ss * D + nh * 512:ss * D + (nh + 1) * 512],
                        ALU.mult, ALU.add).then_inc(VS.h, 1)
                    VS.n += 1

        @block.gpsimd
        def _(g_, ti=ti, par=par, t0=t0):
            g_.wait_ge(VS.h, vs0 + 8 * (ti + 1))
            for ss in range(SUB):
                dma(g_, OG, outr[t0 + ss * 128:t0 + (ss + 1) * 128, :],
                    o_t[par][:, ss * D:(ss + 1) * D])
            g_.wait_ge(OG.h, OG.n)
            g_.engine_nop().then_inc(GS.h, 1)
            GS.n += 1

        if ti + 2 < NTILE:
            emit_loads(ti + 2)

    @block.sync
    def _(sync):
        sync.wait_ge(GS.h, gs0 + NTILE)

    @block.sync
    def _(sync):
        wait_all(sync, DS, SD, GD, VS, SS, GS, TS_)

    p4.close()
    es.close()
    return nc


_NC_CACHE = None


def _get_nc():
    global _NC_CACHE
    if _NC_CACHE is None:
        _NC_CACHE = build_program()
    return _NC_CACHE


def kernel(**inputs):
    n_cores = 8
    x_full = np.ascontiguousarray(np.asarray(inputs["L0_attn_x"]))
    wgT = np.ascontiguousarray(np.asarray(inputs["L0_ffn_w_gate"]).T)
    wuT = np.ascontiguousarray(np.asarray(inputs["L0_ffn_w_up"]).T)
    wdT = np.ascontiguousarray(np.asarray(inputs["L0_ffn_w_down"]).T)
    common = {
        "Wg": np.ascontiguousarray(inputs["L0_attn_Wg"]),
        "Wi": np.ascontiguousarray(inputs["L0_attn_Wi"]),
        "Wfg": np.ascontiguousarray(inputs["L0_attn_Wfg"]),
        "Wfu": np.ascontiguousarray(inputs["L0_attn_Wfu"]),
        "Wfd": np.ascontiguousarray(inputs["L0_attn_Wfd"]),
        "Wo": np.ascontiguousarray(inputs["L0_attn_Wo"]),
        "bg": np.ascontiguousarray(inputs["L0_attn_bg"]),
        "bi": np.ascontiguousarray(inputs["L0_attn_bi"]),
        "A": np.ascontiguousarray(inputs["L0_attn_A"]),
        "Bp": np.ascontiguousarray(inputs["L0_attn_Bp"]),
        "Cp": np.ascontiguousarray(inputs["L0_attn_Cp"]),
        "Dp": np.ascontiguousarray(inputs["L0_attn_Dp"]),
        "gamma": np.ascontiguousarray(inputs["L0_attn_gamma"]),
        "beta": np.ascontiguousarray(inputs["L0_attn_beta"]),
        "bfg": np.ascontiguousarray(inputs["L0_attn_bfg"]),
        "bfu": np.ascontiguousarray(inputs["L0_attn_bfu"]),
        "bfd": np.ascontiguousarray(inputs["L0_attn_bfd"]),
        "bo": np.ascontiguousarray(inputs["L0_attn_bo"]),
        "wgT": wgT, "wuT": wuT, "wdT": wdT,
    }
    in_maps = []
    for i in range(n_cores):
        m = dict(common)
        m["x"] = np.ascontiguousarray(x_full[2 * i:2 * i + 2])
        in_maps.append(m)
    nc = _get_nc()
    res = run_bass_kernel_spmd(nc, in_maps, core_ids=list(range(n_cores)))
    return np.concatenate([np.asarray(r["out"]) for r in res.results], axis=0)
